# version 01
# baseline (speedup 1.0000x reference)
import numpy as np
import ml_dtypes
from contextlib import ExitStack
import concourse.bass as bass
import concourse.mybir as mybir
from concourse.bass_utils import run_bass_kernel_spmd

F32 = mybir.dt.float32
BF16 = mybir.dt.bfloat16
AF = mybir.ActivationFunctionType
ALU = mybir.AluOpType
AX = mybir.AxisListType
BF = ml_dtypes.bfloat16

NCORE = 8
import os as _os
DETECT_RACES = _os.environ.get("DETECT_RACES", "0") == "1"
AUTOFENCE = _os.environ.get("AUTOFENCE", "1") == "1"
B, T, D = 4, 4096, 1024
TK = 2048
ENGINES = ("sync", "vector", "scalar", "tensor", "gpsimd")


def cdiv(a, b):
    return (a + b - 1) // b


def toT(x):
    return np.ascontiguousarray(np.swapaxes(np.asarray(x), -1, -2))


def c_(x):
    return np.ascontiguousarray(x)



class Ctr:
    __slots__ = ("sem", "v", "name")

    def __init__(self, sem, name):
        self.sem = sem
        self.v = 0
        self.name = name


class Kern:

    def __init__(self, name):
        self.nc = bass.Bass("TRN2", target_bir_lowering=False,
                            detect_race_conditions=DETECT_RACES)
        self.stack = ExitStack()
        self.ctrs = []
        self.name = name

    def inp(self, name, shape, dt=F32):
        return self.nc.dram_tensor(name, shape, dt, kind="ExternalInput").ap()

    def out(self, name, shape, dt=F32):
        return self.nc.dram_tensor(name, shape, dt, kind="ExternalOutput").ap()

    def sb(self, name, shape, dt=F32):
        return self.stack.enter_context(self.nc.sbuf_tensor(name, shape, dt))

    def ps(self, name, shape, dt=F32):
        return self.stack.enter_context(self.nc.psum_tensor(name, shape, dt))

    def ctr(self, name):
        c = Ctr(self.stack.enter_context(self.nc.semaphore(name)), name)
        self.ctrs.append(c)
        return c

    def finish(self, script):
        block = self.stack.enter_context(self.nc.Block())
        ctx = Ctx(self)
        for eng in ENGINES:
            ctx.active = eng
            ctx.engine = getattr(self.nc, eng)
            ctx.last = {}
            for c in self.ctrs:
                c.v = 0
            deco = getattr(block, eng)

            def body(E, ctx=ctx, script=script):
                script(ctx)
            deco(body)
        self.stack.close()
        return self.nc


class Ctx:
    def __init__(self, k):
        self.k = k
        self.nc = k.nc
        self.active = None
        self.engine = None
        self.auto = {e: k.ctr("auto_" + e)
                     for e in ("vector", "scalar", "tensor", "gpsimd")} \
            if AUTOFENCE else None
        self.last = {}

    def op(self, eng, thunk, ctr=None, n=1):
        fence = AUTOFENCE and eng in ("vector", "scalar", "tensor", "gpsimd")
        if fence and eng in self.last:
            ls, lv = self.last[eng]
            if self.active == eng:
                self.engine.wait_ge(ls, lv)
        if ctr is None and fence:
            ctr, n = self.auto[eng], 1
        if self.active == eng:
            instr = thunk()
            if ctr is not None:
                instr.then_inc(ctr.sem, n)
        if ctr is not None:
            ctr.v += n
            if fence:
                self.last[eng] = (ctr.sem, ctr.v)
            return ctr.v
        return None

    def dma(self, eng, out, in_, ctr):
        E = getattr(self.nc, eng)
        return self.op(eng, lambda: E.dma_start(out=out, in_=in_), ctr=ctr,
                       n=16)

    def wait(self, eng, ctr, v=None):
        v = ctr.v if v is None else v
        if self.active == eng and v > 0:
            self.engine.wait_ge(ctr.sem, v)

    def mm(self, psum, lhsT, rhs, start=True, stop=True, ctr=None):
        return self.op("tensor", lambda: self.nc.tensor.matmul(
            psum, lhsT, rhs, start=start, stop=stop), ctr=ctr)

    def trans(self, psum, in_, ident, ctr=None):
        return self.op("tensor", lambda: self.nc.tensor.transpose(
            psum, in_, ident), ctr=ctr)

    def act(self, out, in_, func, bias=None, scale=None, accum_out=None,
            ctr=None):
        kw = {}
        if bias is not None:
            kw["bias"] = bias
        if scale is not None:
            kw["scale"] = scale
        if accum_out is not None:
            kw["accum_out"] = accum_out
        return self.op("scalar", lambda: self.nc.scalar.activation(
            out=out, in_=in_, func=func, **kw), ctr=ctr)

    def tt(self, out, in0, in1, op, ctr=None, eng="vector"):
        E = getattr(self.nc, eng)
        return self.op(eng, lambda: E.tensor_tensor(
            out=out, in0=in0, in1=in1, op=op), ctr=ctr)

    def ts(self, out, in0, s1, op0, s2=None, op1=None, ctr=None, eng="vector"):
        E = getattr(self.nc, eng)
        if op1 is None:
            return self.op(eng, lambda: E.tensor_scalar(
                out=out, in0=in0, scalar1=s1, scalar2=None, op0=op0), ctr=ctr)
        return self.op(eng, lambda: E.tensor_scalar(
            out=out, in0=in0, scalar1=s1, scalar2=s2, op0=op0, op1=op1),
            ctr=ctr)

    def stt(self, out, in0, scalar, in1, op0, op1, ctr=None, eng="vector"):
        E = getattr(self.nc, eng)
        return self.op(eng, lambda: E.scalar_tensor_tensor(
            out=out, in0=in0, scalar=scalar, in1=in1, op0=op0, op1=op1),
            ctr=ctr)

    def copy(self, out, in_, ctr=None, eng="vector"):
        E = getattr(self.nc, eng)
        return self.op(eng, lambda: E.tensor_copy(out, in_), ctr=ctr)

    def memset(self, ap, v, ctr=None, eng="vector"):
        E = getattr(self.nc, eng)
        return self.op(eng, lambda: E.memset(ap, v), ctr=ctr)

    def red(self, out, in_, op, ctr=None, negate=None):
        kw = {}
        if negate is not None:
            kw["negate"] = negate
        return self.op("vector", lambda: self.nc.vector.tensor_reduce(
            out, in_, axis=AX.X, op=op, **kw), ctr=ctr)

    def recip(self, out, in_, ctr=None):
        return self.op("vector", lambda: self.nc.vector.reciprocal(out, in_),
                       ctr=ctr)

    def scan(self, out, data0, data1, initial, op0, op1, ctr=None):
        return self.op("vector", lambda: self.nc.vector.tensor_tensor_scan(
            out=out, data0=data0, data1=data1, initial=initial, op0=op0,
            op1=op1), ctr=ctr)


_BUILT = {}


def built(name, builder):
    if name not in _BUILT:
        _BUILT[name] = builder()
    return _BUILT[name]


def run8(nc, in_maps):
    return run_bass_kernel_spmd(nc, in_maps, core_ids=list(range(NCORE)))


RUNNER = None


def runk(name, build, in_maps):
    if RUNNER is not None:
        return RUNNER(name, build, in_maps)
    return run8(built(name, build), in_maps).results



def w0_shapes():
    return {
        "upT": (8192, 2048), "gateT": (8192, 2048), "downT": (16384, 1024),
        "w_inT": (1024, 8208), "w_outT": (2048, 1024),
        "w_writeT": (1024, 1024), "w_readT": (1024, 1024),
        "g_WqT": (1024, 1024), "g_WkT": (1024, 1024), "g_WvT": (1024, 1024),
        "g_W1T": (1024, 1024), "g_W2T": (1024, 1024),
        "a_Wc_down": (1024, 512), "a_Wq_up": (512, 1024),
        "a_Wkv_up": (512, 128), "a_Wc_proj": (1024, 1024),
        "b_U": (1024, 32),
        "f_Wq": (1024, 1024), "f_Wk": (1024, 1024), "f_Wv": (1024, 1024),
        "f_Wo": (1024, 1024), "f_Ws1": (1024, 256),
        "wupsT": (1024, 2048), "wgatesT": (1024, 2048),
        "wdownsT": (2048, 1024),
    }


def w0_sharded(rows):
    return rows % 1024 == 0


def build_w0():
    sh = w0_shapes()
    k = Kern("w0")
    ios = {}
    for name, (R, C) in sh.items():
        rows = R // 8 if w0_sharded(R) else R
        ios[name] = (k.inp("i_" + name, [rows, C], F32),
                     k.out("o_" + name, [rows, C], BF16), rows, C)
    tin = [k.sb(f"tin{j}", [128, 8208], F32) for j in range(2)]
    tout = [k.sb(f"tout{j}", [128, 8208], BF16) for j in range(2)]
    ld = k.ctr("ld")
    cv = k.ctr("cv")
    st = k.ctr("st")

    def script(ctx):
        hist = []
        n = 0
        for name, (i, o, rows, C) in ios.items():
            for r0 in range(0, rows, 128):
                nr = min(128, rows - r0)
                j = n % 2
                if n >= 2:
                    ctx.wait("sync", st, hist[n - 2][1])
                lv = ctx.dma("sync", tin[j][0:nr, 0:C], i[r0:r0 + nr, :], ld)
                ctx.wait("vector", ld, lv)
                cvv = ctx.copy(tout[j][0:nr, 0:C], tin[j][0:nr, 0:C], ctr=cv)
                ctx.wait("gpsimd", cv, cvv)
                sv = ctx.dma("gpsimd", o[r0:r0 + nr, :], tout[j][0:nr, 0:C], st)
                hist.append((lv, sv))
                n += 1
        ctx.wait("gpsimd", st, st.v)
    return k.finish(script)


def run_w0(inputs):
    arrs = {
        "upT": toT(inputs["L1_ffn_up"]).reshape(8192, 2048),
        "gateT": toT(inputs["L1_ffn_gate"]).reshape(8192, 2048),
        "downT": toT(inputs["L1_ffn_down"]).reshape(16384, 1024),
        "w_inT": inputs["L1_attn_a_w_in"].T,
        "w_outT": inputs["L1_attn_a_w_out"].T,
        "w_writeT": inputs["L1_attn_a_w_write"].T,
        "w_readT": inputs["L1_attn_a_w_read"].T,
        "g_WqT": inputs["L1_attn_b_Wq"].T,
        "g_WkT": inputs["L1_attn_b_Wk"].T,
        "g_WvT": inputs["L1_attn_b_Wv"].T,
        "g_W1T": inputs["L1_attn_b_W1"].T,
        "g_W2T": inputs["L1_attn_b_W2"].T,
        "a_Wc_down": inputs["L0_attn_a_Wc_down"],
        "a_Wq_up": inputs["L0_attn_a_Wq_up"],
        "a_Wkv_up": inputs["L0_attn_a_Wkv_up"],
        "a_Wc_proj": inputs["L0_attn_a_Wc_proj"],
        "b_U": inputs["L0_attn_b_U"],
        "f_Wq": inputs["L0_ffn_Wq"], "f_Wk": inputs["L0_ffn_Wk"],
        "f_Wv": inputs["L0_ffn_Wv"], "f_Wo": inputs["L0_ffn_Wo"],
        "f_Ws1": inputs["L0_ffn_Ws1"],
        "wupsT": inputs["L1_ffn_w_up_s"].T,
        "wgatesT": inputs["L1_ffn_w_gate_s"].T,
        "wdownsT": inputs["L1_ffn_w_down_s"].T,
    }
    arrs = {n: c_(np.asarray(a, np.float32)) for n, a in arrs.items()}
    nc = built("w0", build_w0)
    in_maps = []
    for c in range(NCORE):
        m = {}
        for name, a in arrs.items():
            R = a.shape[0]
            if w0_sharded(R):
                rows = R // 8
                m["i_" + name] = c_(a[c * rows:(c + 1) * rows])
            else:
                m["i_" + name] = a
        in_maps.append(m)
    res = run8(nc, in_maps)
    W = {}
    for name, a in arrs.items():
        R = a.shape[0]
        if w0_sharded(R):
            W[name] = np.concatenate(
                [res.results[c]["o_" + name] for c in range(NCORE)], axis=0)
        else:
            W[name] = res.results[0]["o_" + name]
    return W



class LNBlock:

    def __init__(self, k, t_total, tag="", ps3=None, keep_s=False):
        self.t_total = t_total
        self.ones128f = k.sb("ln_ones" + tag, [128, 128], F32)
        self.sq = k.sb("ln_sq" + tag, [128, 512], F32)
        self.row = k.sb("ln_row" + tag, [1, (6 if keep_s else 5) * 512], F32)
        if ps3 is None:
            self.ps_stat = k.ps("ln_ps" + tag, [128, 512], F32)
            self.ps_b1 = k.ps("ln_pb1" + tag, [128, 512], F32)
            self.ps_b2 = k.ps("ln_pb2" + tag, [128, 512], F32)
        else:
            self.ps_stat, self.ps_b1, self.ps_b2 = ps3
        self.sb_s = k.sb("ln_sbs" + tag, [128, 512], F32)
        self.tmp = k.sb("ln_tmp" + tag, [128, 512], F32)
        self.ones_col = k.sb("ln_onec" + tag, [128, 1], F32)
        self.eps1 = k.sb("ln_eps1" + tag, [2, 1], F32)
        self.eps2 = k.sb("ln_eps2" + tag, [2, 1], F32)
        self.pe = k.ctr("ln_pe" + tag)
        self.ve = k.ctr("ln_ve" + tag)
        self.se = k.ctr("ln_se" + tag)

    def init(self, ctx, ini, rms_eps=None):
        ctx.memset(self.ones128f[:], 1.0)
        ctx.memset(self.eps1[:], 1e-5)
        if rms_eps is not None:
            ctx.memset(self.eps2[:], rms_eps)
        return ctx.memset(self.ones_col[:], 1.0, ctr=ini)


def emit_ln(ctx, ln, x_at, xn_at, wcol=None, rms_eps=None,
            pre_block=None, post_block=None, keep_s=False):
    t_total = ln.t_total
    r = ln.row
    TT = ln.t_total
    NB = cdiv(t_total, 512)
    prev_rowmath = 0
    for tb in range(NB):
        t0 = tb * 512
        tw = min(512, t_total - t0)
        if pre_block is not None:
            pre_block(ctx, tb, t0, tw)
        for ci in range(8):
            ctx.wait("scalar", ln.pe, ln.pe.v)
            sev = ctx.act(ln.sq[:, 0:tw], x_at(ci, t0, tw), AF.Square,
                          ctr=ln.se)
            ctx.wait("tensor", ln.se, sev)
            if ci == 0:
                ctx.wait("tensor", ln.se, prev_rowmath)
            ctx.mm(ln.ps_stat[0:1, 0:tw], ln.ones128f[:, 0:1],
                   x_at(ci, t0, tw), start=(ci == 0), stop=(ci == 7))
            ctx.mm(ln.ps_stat[32:33, 0:tw], ln.ones128f[:, 0:1],
                   ln.sq[:, 0:tw], start=(ci == 0), stop=(ci == 7), ctr=ln.pe)
        ctx.wait("scalar", ln.pe, ln.pe.v)
        ctx.act(r[0:1, 0 * 512:0 * 512 + tw], ln.ps_stat[0:1, 0:tw], AF.Copy,
                scale=1.0 / 1024)
        sev = ctx.act(r[0:1, 1 * 512:1 * 512 + tw], ln.ps_stat[32:33, 0:tw], AF.Copy,
                      scale=1.0 / 1024, ctr=ln.se)
        prev_rowmath = sev
        ctx.wait("vector", ln.se, sev)
        ctx.tt(r[0:1, 4 * 512:4 * 512 + tw], r[0:1, 0 * 512:0 * 512 + tw], r[0:1, 0 * 512:0 * 512 + tw],
               ALU.mult)
        vev = ctx.tt(r[0:1, 1 * 512:1 * 512 + tw], r[0:1, 1 * 512:1 * 512 + tw],
                     r[0:1, 4 * 512:4 * 512 + tw], ALU.subtract, ctr=ln.ve)
        ctx.wait("scalar", ln.ve, vev)
        sev = ctx.act(r[0:1, 2 * 512:2 * 512 + tw], r[0:1, 1 * 512:1 * 512 + tw], AF.Sqrt,
                      bias=ln.eps1[0:1, :], ctr=ln.se)
        ctx.wait("vector", ln.se, sev)
        ctx.recip(r[0:1, 2 * 512:2 * 512 + tw], r[0:1, 2 * 512:2 * 512 + tw])
        if keep_s:
            ctx.copy(r[0:1, 5 * 512:5 * 512 + tw], r[0:1, 2 * 512:2 * 512 + tw])
        if rms_eps is not None:
            ctx.tt(r[0:1, 4 * 512:4 * 512 + tw], r[0:1, 2 * 512:2 * 512 + tw], r[0:1, 2 * 512:2 * 512 + tw],
                   ALU.mult)
            vev = ctx.tt(r[0:1, 4 * 512:4 * 512 + tw], r[0:1, 4 * 512:4 * 512 + tw],
                         r[0:1, 1 * 512:1 * 512 + tw], ALU.mult, ctr=ln.ve)
            ctx.wait("scalar", ln.ve, vev)
            sev = ctx.act(r[0:1, 4 * 512:4 * 512 + tw], r[0:1, 4 * 512:4 * 512 + tw], AF.Sqrt,
                          bias=ln.eps2[0:1, :], ctr=ln.se)
            ctx.wait("vector", ln.se, sev)
            ctx.recip(r[0:1, 4 * 512:4 * 512 + tw], r[0:1, 4 * 512:4 * 512 + tw])
            ctx.tt(r[0:1, 2 * 512:2 * 512 + tw], r[0:1, 2 * 512:2 * 512 + tw],
                   r[0:1, 4 * 512:4 * 512 + tw], ALU.mult)
        vev = ctx.ts(r[0:1, 3 * 512:3 * 512 + tw], r[0:1, 0 * 512:0 * 512 + tw],
                     1.0, ALU.mult, ctr=ln.ve)
        ctx.wait("tensor", ln.ve, vev)
        ctx.mm(ln.ps_b1[:, 0:tw], ln.ones128f[0:1, :], r[0:1, 3 * 512:3 * 512 + tw])
        pev = ctx.mm(ln.ps_b2[:, 0:tw], ln.ones128f[0:1, :],
                     r[0:1, 2 * 512:2 * 512 + tw], ctr=ln.pe)
        ctx.wait("vector", ln.pe, pev)
        ctx.copy(ln.sb_s[:, 0:tw], ln.ps_b2[:, 0:tw])
        for ci in range(8):
            wc = wcol[ci] if wcol is not None else ln.ones_col
            ctx.tt(ln.tmp[:, 0:tw], x_at(ci, t0, tw),
                   ln.ps_b1[:, 0:tw], ALU.subtract)
            ctx.stt(xn_at(ci, t0, tw), ln.tmp[:, 0:tw], wc[:],
                    ln.sb_s[:, 0:tw], ALU.mult, ALU.mult, ctr=ln.ve)
        if post_block is not None:
            post_block(ctx, tb, t0, tw)
    return ln.ve



import os
MOE_NE = int(os.environ.get("MOE_NE", "9"))
MOE_PHASE = os.environ.get("MOE_PHASE", "full")
MOE_PB = os.environ.get("MOE_PB", "all")

def build_moe():
    k = Kern("moe")
    xT = k.inp("xT", [1024, TK], F32)
    upS = k.inp("upS", [16384, 1024], BF16)
    gateS = k.inp("gateS", [16384, 1024], BF16)
    downS = k.inp("downS", [8192, 2048], BF16)
    routerTf = k.inp("routerTf", [1024, 8], F32)
    wupsS = k.inp("wupsS", [2048, 1024], BF16)
    wgatesS = k.inp("wgatesS", [2048, 1024], BF16)
    wdownsS = k.inp("wdownsS", [1024, 2048], BF16)
    identf = k.inp("identf", [128, 128], F32)
    oxT = k.out("oxT", [1024, TK], F32)

    pb = [k.ps(f"pb{i}", [128, 512], F32) for i in range(7)]
    psm = k.ps("psm", [128, 128], BF16)
    pbA, pbB, pbC, pbD, pbE, pbF, pbG = pb

    x_blk = [k.sb(f"xb{i}", [128, 512], F32) for i in range(8)]
    xn_ch = [k.sb(f"xn{i}", [128, TK], BF16) for i in range(8)]
    ln = LNBlock(k, TK, ps3=(pbA, pbB, pbC))
    ones128b = k.sb("ones128b", [128, 128], BF16)
    rtf = [k.sb(f"rtf{i}", [128, 8], F32) for i in range(8)]
    idf = k.sb("idf", [128, 128], F32)
    idb = k.sb("idb", [128, 128], BF16)
    su = [k.sb(f"su{j}", [128, 1024], BF16) for j in range(2)]
    sg = [k.sb(f"sg{j}", [128, 1024], BF16) for j in range(2)]
    sd = [k.sb(f"sd{j}", [128, 2048], BF16) for j in range(2)]
    h_sb = [k.sb(f"h{j}", [128, TK], BF16) for j in range(16)]
    u_tmp = k.sb("u_tmp", [128, 512], BF16)
    sg_tmp = k.sb("sg_tmp", [128, 512], BF16)
    y_sb = [k.sb(f"y{j}", [128, TK], BF16) for j in range(8)]
    logits_f = k.sb("logits_f", [128, 128], F32)
    Rrow = k.sb("Rrow", [1, 8], F32)
    Rb = k.sb("Rb", [128, 8], F32)
    lt1 = k.sb("lt1", [128, 8], F32)
    mcols = k.sb("mcols", [128, 16], F32)
    scols = k.sb("scols", [128, 16], F32)
    mask_tok = k.sb("mask_tok", [128, 128], BF16)
    maxc = k.sb("maxc", [128, 16], F32)
    maskT = k.sb("maskT", [8, TK], BF16)
    maskb = k.sb("maskb", [128, TK], BF16)
    mrow = k.sb("mrow", [1, TK], BF16)

    ld = k.ctr("ld")
    mld = k.ctr("mld")
    xld = k.ctr("xld")
    wlds = [k.ctr("wld0"), k.ctr("wld1")]
    pe = k.ctr("pe")
    ve = k.ctr("ve")
    se = k.ctr("se")
    st = k.ctr("st")

    def wslab(e, kind, i):
        if kind in ("u", "g"):
            if e < 8:
                src = upS if kind == "u" else gateS
                r0 = (e * 16 + i) * 128
            else:
                src = wupsS if kind == "u" else wgatesS
                r0 = i * 128
            return src[r0:r0 + 128, :]
        if e < 8:
            return downS[(e * 8 + i) * 128:(e * 8 + i + 1) * 128, :]
        return wdownsS[i * 128:(i + 1) * 128, :]

    items = []
    for e in range(MOE_NE):
        for ic in range(16):
            items.append(("ug", e, ic))
        for dc in range(8):
            items.append(("d", e, dc))

    def script(ctx):
        for i in range(8):
            ctx.dma("sync", rtf[i][:], routerTf[i * 128:(i + 1) * 128, :], ld)
        ldv = ctx.dma("sync", idf[:], identf, ld)
        ctx.memset(ones128b[:], 1.0)
        for j in range(8):
            ctx.memset(y_sb[j][:], 0.0)
        for eng in ("vector", "scalar", "tensor"):
            ctx.wait(eng, ld, ldv)
        ctx.copy(idb[:], idf[:], ctr=ve)
        iniv = ln.init(ctx, ve)
        ctx.wait("tensor", ve, iniv)
        if MOE_PHASE == "none":
            finish_residual(ctx)
            return

        for ci in range(8):
            pev = ctx.mm(pbD[0:1, 0:8], ln.ones128f[:, 0:1], rtf[ci][:],
                         start=(ci == 0), stop=(ci == 7), ctr=pe)
        ctx.wait("vector", pe, pev)
        vev = ctx.copy(Rrow[0:1, 0:8], pbD[0:1, 0:8], ctr=ve)
        ctx.wait("tensor", ve, vev)
        pev = ctx.mm(pbD[:, 0:8], ln.ones128f[0:1, :], Rrow[0:1, 0:8], ctr=pe)
        ctx.wait("vector", pe, pev)
        vev = ctx.copy(Rb[:], pbD[:, 0:8], ctr=ve)
        ctx.wait("tensor", ve, vev)

        state = {"norm_v": 0, "logit_pe": 0}

        def pre_block(ctx, tb, t0, tw):
            if tb > 0:
                ctx.wait("sync", ln.ve, state["norm_v"])
                ctx.wait("sync", pe, state["logit_pe"])
            for ci in range(8):
                xv = ctx.dma("sync", x_blk[ci][:, 0:tw],
                             xT[ci * 128:(ci + 1) * 128, t0:t0 + tw], xld)
            for eng in ("vector", "scalar", "tensor"):
                ctx.wait(eng, xld, xv)

        def post_block(ctx, tb, t0, tw):
            state["norm_v"] = ln.ve.v
            for q in range(4):
                tt_ = tb * 4 + q
                c0 = q * 128
                ctx.wait("tensor", ln.ve, ln.ve.v)
                pev = ctx.trans(pbD[0:128, 0:1],
                                ln.row[0:1, c0:c0 + 128], idf[0:1, 0:1],
                                ctr=pe)
                ctx.wait("vector", pe, pev)
                vev = ctx.copy(mcols[:, tt_:tt_ + 1], pbD[0:128, 0:1], ctr=ve)
                ctx.wait("tensor", ve, vev)
                pev = ctx.trans(pbD[0:128, 0:1],
                                ln.row[0:1, 2 * 512 + c0:2 * 512 + c0 + 128],
                                idf[0:1, 0:1],
                                ctr=pe)
                ctx.wait("vector", pe, pev)
                vev = ctx.copy(scols[:, tt_:tt_ + 1], pbD[0:128, 0:1], ctr=ve)
                ctx.wait("tensor", ve, vev)
                if MOE_PB == "trans":
                    continue
                for ci in range(8):
                    pev = ctx.mm(pbD[:, 0:8],
                                 x_blk[ci][:, c0:c0 + 128], rtf[ci][:],
                                 start=(ci == 0), stop=(ci == 7), ctr=pe)
                ctx.wait("vector", pe, pev)
                ctx.ts(lt1[:], Rb[:], mcols[:, tt_:tt_ + 1], ALU.mult)
                ctx.tt(lt1[:], pbD[:, 0:8], lt1[:], ALU.subtract)
                vev = ctx.ts(logits_f[:, tt_ * 8:(tt_ + 1) * 8], lt1[:],
                             scols[:, tt_:tt_ + 1], ALU.mult, ctr=ve)
                if MOE_PB == "logits":
                    ctx.wait("tensor", ve, vev)
                    continue
                ctx.red(maxc[:, tt_:tt_ + 1],
                        logits_f[:, tt_ * 8:(tt_ + 1) * 8], ALU.max)
                vev = ctx.ts(mask_tok[:, tt_ * 8:(tt_ + 1) * 8],
                             logits_f[:, tt_ * 8:(tt_ + 1) * 8],
                             maxc[:, tt_:tt_ + 1], ALU.is_equal, ctr=ve)
                ctx.wait("tensor", ve, vev)
                if MOE_PB == "mask":
                    continue
                pev = ctx.trans(psm[0:8, 0:128],
                                mask_tok[:, tt_ * 8:(tt_ + 1) * 8], idb[:],
                                ctr=pe)
                ctx.wait("vector", pe, pev)
                vev = ctx.copy(maskT[:, tt_ * 128:(tt_ + 1) * 128],
                               psm[0:8, 0:128], ctr=ve)
                ctx.wait("tensor", ve, vev)
            state["logit_pe"] = pe.v

        lnc = emit_ln(
            ctx, ln,
            x_at=lambda ci, t0, tw: x_blk[ci][:, 0:tw],
            xn_at=lambda ci, t0, tw: xn_ch[ci][:, t0:t0 + tw],
            pre_block=pre_block,
            post_block=(None if MOE_PHASE == "ln" else post_block))
        if MOE_PHASE in ("ln", "router"):
            finish_residual(ctx)
            return

        nug = [0]
        nd = [0]
        nit = [0]
        wv_item = {}

        def load_item(it):
            kind, e, i = it
            wl = wlds[nit[0] % 2]
            nit[0] += 1
            if kind == "ug":
                j = nug[0] % 2
                nug[0] += 1
                ctx.dma("sync", su[j][:], wslab(e, "u", i), wl)
                wv_item[it] = (wl, ctx.dma("sync", sg[j][:], wslab(e, "g", i),
                                           wl))
                return j
            else:
                j = nd[0] % 2
                nd[0] += 1
                wv_item[it] = (wl, ctx.dma("sync", sd[j][:], wslab(e, "d", i),
                                           wl))
                return j

        bufidx = {}
        bufidx[items[0]] = load_item(items[0])

        it_pos = 0
        for e in range(MOE_NE):
            ctx.wait("tensor", ve, ve.v)
            if e < 8:
                ctx.wait("gpsimd", ve, ve.v)
                mv = ctx.dma("gpsimd", mrow[0:1, :], maskT[e:e + 1, :], mld)
                ctx.wait("tensor", mld, mv)
                bank4 = [pbC, pbD, pbE, pbF]
                for tb in range(4):
                    pev = ctx.mm(bank4[tb][:, 0:512], ones128b[0:1, :],
                                 mrow[0:1, tb * 512:(tb + 1) * 512],
                                 ctr=pe)
                ctx.wait("vector", pe, pev)
                for tb in range(4):
                    vev = ctx.copy(maskb[:, tb * 512:(tb + 1) * 512],
                                   bank4[tb][:, 0:512], ctr=ve)
                ctx.wait("tensor", ve, vev)

            for ic in range(16):
                it = ("ug", e, ic)
                if it_pos + 1 < len(items):
                    ctx.wait("sync", pe, pe.v)
                    bufidx[items[it_pos + 1]] = load_item(items[it_pos + 1])
                it_pos += 1
                j = bufidx[it]
                ctx.wait("tensor", wv_item[it][0], wv_item[it][1])
                for tbq in range(4):
                    pu = pbA if tbq % 2 == 0 else pbB
                    pg2 = pbC if tbq % 2 == 0 else pbD
                    tsl = slice(tbq * 512, (tbq + 1) * 512)
                    ctx.wait("tensor", ve, ve.v)
                    for ci in range(8):
                        ctx.mm(pu[:, 0:512],
                               su[j][:, ci * 128:(ci + 1) * 128],
                               xn_ch[ci][:, tsl],
                               start=(ci == 0), stop=(ci == 7))
                    for ci in range(8):
                        pev = ctx.mm(pg2[:, 0:512],
                                     sg[j][:, ci * 128:(ci + 1) * 128],
                                     xn_ch[ci][:, tsl],
                                     start=(ci == 0), stop=(ci == 7), ctr=pe)
                    ctx.wait("scalar", pe, pev)
                    ctx.wait("scalar", ve, ve.v)
                    sev = ctx.act(sg_tmp[:], pg2[:, 0:512], AF.Silu, ctr=se)
                    ctx.wait("vector", pe, pev)
                    if e < 8:
                        ctx.tt(u_tmp[:], pu[:, 0:512], maskb[:, tsl], ALU.mult)
                    else:
                        ctx.copy(u_tmp[:], pu[:, 0:512])
                    ctx.wait("vector", se, sev)
                    vev = ctx.tt(h_sb[ic][:, tsl], u_tmp[:], sg_tmp[:],
                                 ALU.mult, ctr=ve)
            ctx.wait("tensor", ve, ve.v)
            for dc in range(8):
                it = ("d", e, dc)
                if it_pos + 1 < len(items):
                    ctx.wait("sync", pe, pe.v)
                    bufidx[items[it_pos + 1]] = load_item(items[it_pos + 1])
                it_pos += 1
                j = bufidx[it]
                ctx.wait("tensor", wv_item[it][0], wv_item[it][1])
                for tbq in range(4):
                    pd = pbE if tbq % 2 == 0 else pbF
                    tsl = slice(tbq * 512, (tbq + 1) * 512)
                    ctx.wait("tensor", ve, ve.v)
                    for ic in range(16):
                        pev = ctx.mm(pd[:, 0:512],
                                     sd[j][:, ic * 128:(ic + 1) * 128],
                                     h_sb[ic][:, tsl],
                                     start=(ic == 0), stop=(ic == 15),
                                     ctr=(pe if ic == 15 else None))
                    ctx.wait("vector", pe, pev)
                    vev = ctx.tt(y_sb[dc][:, tsl], y_sb[dc][:, tsl],
                                 pd[:, 0:512], ALU.add, ctr=ve)
            ctx.wait("tensor", ve, ve.v)

        finish_residual(ctx)

    def finish_residual(ctx):
        ctx.wait("sync", pe, pe.v)
        ctx.wait("sync", ve, ve.v)
        hrl = [h_sb[j][:].bitcast(F32) for j in range(16)]
        for j in range(8):
            for h2 in range(2):
                lv = ctx.dma("sync", hrl[2 * j + h2],
                             xT[j * 128:(j + 1) * 128,
                                h2 * 1024:(h2 + 1) * 1024], ld)
        ctx.wait("vector", ld, lv)
        for j in range(8):
            for h2 in range(2):
                buf = hrl[2 * j + h2]
                vev = ctx.tt(buf, buf,
                             y_sb[j][:, h2 * 1024:(h2 + 1) * 1024], ALU.add,
                             ctr=ve)
                ctx.wait("gpsimd", ve, vev)
                ctx.dma("gpsimd",
                        oxT[j * 128:(j + 1) * 128, h2 * 1024:(h2 + 1) * 1024],
                        buf, st)
        ctx.wait("gpsimd", st, st.v)

    return k.finish(script)


def moe_slabs(W):
    upS = c_(W["upT"].reshape(8, 8, 128, 16, 128)
             .transpose(0, 3, 2, 1, 4).reshape(16384, 1024))
    gateS = c_(W["gateT"].reshape(8, 8, 128, 16, 128)
               .transpose(0, 3, 2, 1, 4).reshape(16384, 1024))
    downS = c_(W["downT"].reshape(8, 16, 128, 8, 128)
               .transpose(0, 3, 2, 1, 4).reshape(8192, 2048))
    wupsS = c_(W["wupsT"].reshape(8, 128, 16, 128)
               .transpose(2, 1, 0, 3).reshape(2048, 1024))
    wgatesS = c_(W["wgatesT"].reshape(8, 128, 16, 128)
                 .transpose(2, 1, 0, 3).reshape(2048, 1024))
    wdownsS = c_(W["wdownsT"].reshape(16, 128, 8, 128)
                 .transpose(2, 1, 0, 3).reshape(1024, 2048))
    return dict(upS=upS, gateS=gateS, downS=downS, wupsS=wupsS,
                wgatesS=wgatesS, wdownsS=wdownsS)


def run_moe(xT_full, W, raw):
    nc = built(f"moe{MOE_NE}{MOE_PHASE}{MOE_PB}", build_moe)
    identf = np.eye(128, dtype=np.float32)
    routerTf = c_(raw["L1_ffn_router"].T.astype(np.float32))
    slabs = moe_slabs(W)
    in_maps = []
    for c in range(NCORE):
        b, h = c // 2, c % 2
        in_maps.append(dict(
            xT=c_(xT_full[b][:, h * TK:(h + 1) * TK]),
            routerTf=routerTf, identf=identf, **slabs))
    res = run8(nc, in_maps)
    out = np.empty_like(xT_full)
    for c in range(NCORE):
        b, h = c // 2, c % 2
        out[b][:, h * TK:(h + 1) * TK] = res.results[c]["oxT"]
    return out



def build_attn_a():
    k = Kern("attn_a")
    xTb = k.inp("xTb", [1024, T], F32)
    xq = k.inp("xq", [1024, TK], F32)
    Wc_down = k.inp("Wc_down", [1024, 512], BF16)
    Wq_up = k.inp("Wq_up", [512, 1024], BF16)
    Wkv_up = k.inp("Wkv_up", [512, 128], BF16)
    Wc_proj = k.inp("Wc_proj", [1024, 1024], BF16)
    identf = k.inp("identf", [128, 128], F32)
    cmask = k.inp("cmask", [4 * 128, 512], BF16)
    pfull = k.inp("pfull", [128, 65], BF16)
    oxT = k.out("oxT", [1024, TK], F32)

    pb = [k.ps(f"pb{i}", [128, 512], F32) for i in range(7)]
    psb = k.ps("psb", [128, 128], BF16)
    pbA, pbB, pbC, pbD, pbE, pbF, pbG = pb

    x_blk = [k.sb(f"xb{i}", [128, 512], F32) for i in range(8)]
    xn_blk = [k.sb(f"xn{i}", [128, 1024], BF16) for i in range(8)]
    ln = LNBlock(k, T, ps3=(pbA, pbB, pbC))
    latT = [k.sb(f"lat{i}", [128, T], BF16) for i in range(4)]
    kvT = k.sb("kvT", [128, T], BF16)
    qh = [k.sb(f"qh{i}", [64, TK], BF16) for i in range(16)]
    vtok = [k.sb(f"vt{i}", [128, 65], BF16) for i in range(32)]
    ex = [k.sb(f"exb{i}", [128, 512], BF16) for i in range(2)]
    cm = [k.sb(f"cm{i}", [128, 512], BF16) for i in range(4)]
    pf = k.sb("pf", [128, 65], BF16)
    def yts(ci, tb):
        idx = ci * 4 + tb
        if idx < 16:
            return x_blk[idx // 2][:].bitcast(BF16)[
                :, (idx % 2) * 512:(idx % 2) * 512 + 512]
        j = idx - 16
        return xn_blk[j // 2][:, (j % 2) * 512:(j % 2) * 512 + 512]
    rrow = k.sb("rrow", [1, 512], F32)
    rb = k.sb("rb", [64, 512], F32)
    xres = [k.sb(f"xr{i}", [128, 512], F32) for i in range(2)]
    oxs = [k.sb(f"oxsb{i}", [128, 512], F32) for i in range(2)]
    idf = k.sb("idf", [128, 128], F32)
    idb2 = k.sb("idb2", [128, 128], BF16)
    onesb = k.sb("onesb", [1, 128], BF16)
    Wc_down_sb = [k.sb(f"wcd{i}", [128, 512], BF16) for i in range(8)]
    wkv = [k.sb(f"wkv{i}", [128, 128], BF16) for i in range(4)]
    wqu = [k.sb(f"wqu{i}", [128, 1024], BF16) for i in range(4)]

    ld = k.ctr("ld")
    xld = k.ctr("xld")
    rld = [k.ctr("rld0"), k.ctr("rld1")]
    pe = k.ctr("pe")
    ve = k.ctr("ve")
    se = k.ctr("se")
    st = k.ctr("st")

    def script(ctx):
        ctx.dma("sync", idf[:], identf, ld)
        for i in range(4):
            ctx.dma("sync", cm[i][:], cmask[i * 128:(i + 1) * 128, :], ld)
        ctx.dma("sync", pf[:], pfull, ld)
        for i in range(8):
            ctx.dma("sync", Wc_down_sb[i][:],
                    Wc_down[i * 128:(i + 1) * 128, :], ld)
        for i in range(4):
            ctx.dma("sync", wkv[i][:], Wkv_up[i * 128:(i + 1) * 128, :], ld)
        for i in range(4):
            ldv = ctx.dma("sync", wqu[i][:], Wq_up[i * 128:(i + 1) * 128, :],
                          ld)
        ctx.wait("vector", ld, ldv)
        ctx.memset(onesb[:], 1.0)
        ctx.copy(idb2[:], idf[:])
        iniv = ln.init(ctx, ve)
        for eng in ("tensor", "scalar"):
            ctx.wait(eng, ve, iniv)
            ctx.wait(eng, ld, ldv)

        state = {"norm_v": 0}

        def pre_block(ctx, tb, t0, tw):
            if tb > 0:
                ctx.wait("sync", ln.ve, state["norm_v"])
            for ci in range(8):
                xv = ctx.dma("sync", x_blk[ci][:, 0:tw],
                             xTb[ci * 128:(ci + 1) * 128, t0:t0 + tw], xld)
            for eng in ("vector", "scalar", "tensor"):
                ctx.wait(eng, xld, xv)

        def post_block(ctx, tb, t0, tw):
            state["norm_v"] = ln.ve.v
            ctx.wait("tensor", ln.ve, ln.ve.v)
            lp = [pbD, pbE, pbF, pbG]
            par = (t0 // 512) % 2
            for lc in range(4):
                for ci in range(8):
                    pev = ctx.mm(lp[lc][:, 0:tw],
                                 Wc_down_sb[ci][:, lc * 128:(lc + 1) * 128],
                                 xn_blk[ci][:, par * 512:par * 512 + tw],
                                 start=(ci == 0), stop=(ci == 7), ctr=pe)
            ctx.wait("vector", pe, pev)
            for lc in range(4):
                vev = ctx.copy(latT[lc][:, t0:t0 + tw], lp[lc][:, 0:tw],
                               ctr=ve)
            ctx.wait("tensor", ve, vev)

        emit_ln(ctx, ln,
                x_at=lambda ci, t0, tw: x_blk[ci][:, 0:tw],
                xn_at=lambda ci, t0, tw: xn_blk[ci][
                    :, ((t0 // 512) % 2) * 512:((t0 // 512) % 2) * 512 + tw],
                pre_block=pre_block, post_block=post_block)

        for tb in range(8):
            ctx.wait("tensor", ve, ve.v)
            for ci in range(4):
                pev = ctx.mm(pbA[:, 0:512], wkv[ci][:],
                             latT[ci][:, tb * 512:(tb + 1) * 512],
                             start=(ci == 0), stop=(ci == 3), ctr=pe)
            ctx.wait("vector", pe, pev)
            vev = ctx.copy(kvT[:, tb * 512:(tb + 1) * 512], pbA[:, 0:512],
                           ctr=ve)
        ctx.wait("tensor", ve, ve.v)
        for si in range(32):
            pev = ctx.trans(psb[0:128, 0:64],
                            kvT[64:128, si * 128:(si + 1) * 128],
                            idb2[64:128, 64:128], ctr=pe)
            ctx.wait("vector", pe, pev)
            ctx.copy(vtok[si][:, 0:64], psb[0:128, 0:64])
            vev = ctx.memset(vtok[si][:, 64:65], 1.0, ctr=ve)
            if si < 16:
                vev = ctx.tt(vtok[si][:], vtok[si][:], pf[:], ALU.mult,
                             ctr=ve)
            ctx.wait("tensor", ve, vev)
        for h in range(16):
            for qb in range(4):
                ctx.wait("tensor", ve, ve.v)
                for ci in range(4):
                    pev = ctx.mm(pbB[0:64, 0:512],
                                 wqu[ci][:, h * 64:(h + 1) * 64],
                                 latT[ci][:, 2048 + qb * 512:
                                           2048 + (qb + 1) * 512],
                                 start=(ci == 0), stop=(ci == 3), ctr=pe)
                ctx.wait("vector", pe, pev)
                vev = ctx.copy(qh[h][:, qb * 512:(qb + 1) * 512],
                               pbB[0:64, 0:512], ctr=ve)
        ctx.wait("tensor", ve, ve.v)

        qdone_pe = pe.v
        for h in range(16):
            for qb in range(4):
                nown = (qb + 1) * 4
                nsi = 16 + nown
                for sj in range(nsi):
                    si = sj
                    own_j = sj - 16
                    bsel = sj % 2
                    pssc = pbC if bsel == 0 else pbD
                    ctx.wait("tensor", ve, ve.v)
                    ctx.wait("tensor", se, se.v)
                    pev = ctx.mm(pssc[:, 0:512],
                                 kvT[0:64, si * 128:(si + 1) * 128],
                                 qh[h][:, qb * 512:(qb + 1) * 512],
                                 start=True, stop=True, ctr=pe)
                    ctx.wait("scalar", pe, pev)
                    sev = ctx.act(ex[bsel][:, 0:512], pssc[:, 0:512], AF.Exp,
                                  scale=0.125, ctr=se)
                    diag = own_j >= 0 and own_j >= qb * 4
                    lastv = sev
                    if diag:
                        kk = own_j - qb * 4
                        ctx.wait("vector", se, sev)
                        lastv = ctx.tt(ex[bsel][:, 0:512], ex[bsel][:, 0:512],
                                       cm[kk][:], ALU.mult, ctr=ve)
                        ctx.wait("tensor", ve, lastv)
                    else:
                        ctx.wait("tensor", se, sev)
                    pev = ctx.mm(pbE[0:65, 0:512], vtok[si][:, 0:65],
                                 ex[bsel][:, 0:512],
                                 start=(sj == 0), stop=(sj == nsi - 1),
                                 ctr=pe)
                ctx.wait("vector", pe, pev)
                vev = ctx.recip(rrow[0:1, 0:512], pbE[64:65, 0:512], ctr=ve)
                ctx.wait("tensor", ve, vev)
                pev = ctx.mm(pbF[0:64, 0:512], ln.ones128f[0:1, 0:64],
                             rrow[0:1, 0:512], ctr=pe)
                ctx.wait("vector", pe, pev)
                ctx.copy(rb[:, 0:512], pbF[0:64, 0:512])
                ysl = yts(h // 2, qb)[(h % 2) * 64:(h % 2) * 64 + 64, :]
                vev = ctx.tt(ysl, pbE[0:64, 0:512], rb[:, 0:512], ALU.mult,
                             ctr=ve)
                ctx.wait("tensor", ve, vev)

        wcp = [latT[i // 4][:, (i % 4) * 1024:(i % 4 + 1) * 1024]
               for i in range(8)]
        ctx.wait("sync", pe, qdone_pe)
        for i in range(8):
            wv = ctx.dma("sync", wcp[i], Wc_proj[i * 128:(i + 1) * 128, :],
                         ld)
        ctx.wait("tensor", ld, wv)
        st_hist = []
        for n, (ec, tb) in enumerate([(a, b) for a in range(8)
                                      for b in range(4)]):
            par = n % 2
            if n >= 2:
                ctx.wait("sync", st, st_hist[n - 2])
            rv = ctx.dma("sync", xres[par][:],
                         xq[ec * 128:(ec + 1) * 128,
                            tb * 512:(tb + 1) * 512], rld[par])
            ctx.wait("tensor", ve, ve.v)
            for ci in range(8):
                pev = ctx.mm(pbG[:, 0:512],
                             wcp[ci][:, ec * 128:(ec + 1) * 128],
                             yts(ci, tb),
                             start=(ci == 0), stop=(ci == 7), ctr=pe)
            ctx.wait("vector", pe, pev)
            ctx.wait("vector", rld[par], rv)
            if n >= 2:
                ctx.wait("vector", st, st_hist[n - 2])
            vev = ctx.tt(oxs[par][:], pbG[:, 0:512], xres[par][:], ALU.add,
                         ctr=ve)
            ctx.wait("gpsimd", ve, vev)
            sv = ctx.dma("gpsimd",
                         oxT[ec * 128:(ec + 1) * 128,
                             tb * 512:(tb + 1) * 512], oxs[par][:], st)
            st_hist.append(sv)
        ctx.wait("gpsimd", st, st.v)

    return k.finish(script)


def attn_a_cmask():
    m = np.zeros((4, 128, 512), dtype=BF)
    s = np.arange(128)[:, None]
    q = np.arange(512)[None, :]
    for kk in range(4):
        m[kk] = (s + kk * 128 <= q)
    return m.reshape(512, 512)


def run_attn_a(xT_full, W):
    nc = built("attn_a", build_attn_a)
    identf = np.eye(128, dtype=np.float32)
    cmask = attn_a_cmask()
    in_maps = []
    for c in range(NCORE):
        b, h = c // 2, c % 2
        own = xT_full[b][:, h * TK:(h + 1) * TK]
        other = xT_full[b][:, (1 - h) * TK:(2 - h) * TK]
        rot = np.concatenate([other, own], axis=1)
        pf = np.full((128, 65), 1.0 if h == 1 else 0.0, dtype=BF)
        in_maps.append(dict(
            xTb=c_(rot), xq=c_(own),
            Wc_down=W["a_Wc_down"], Wq_up=W["a_Wq_up"],
            Wkv_up=W["a_Wkv_up"], Wc_proj=W["a_Wc_proj"],
            identf=identf, cmask=cmask, pfull=pf))
    res = run8(nc, in_maps)
    out = np.empty_like(xT_full)
    for c in range(NCORE):
        b, h = c // 2, c % 2
        out[b][:, h * TK:(h + 1) * TK] = res.results[c]["oxT"]
    return out



def build_attn_b0():
    k = Kern("attn_b0")
    xTb = k.inp("xTb", [1024, T], F32)
    xq = k.inp("xq", [1024, TK], F32)
    bU = k.inp("bU", [1024, 32], BF16)
    identf = k.inp("identf", [128, 128], F32)
    minus1 = k.inp("minus1", [1, T], BF16)
    oxT = k.out("oxT", [1024, TK], F32)

    pb = [k.ps(f"pb{i}", [128, 512], F32) for i in range(7)]
    psb = k.ps("psb", [128, 128], BF16)
    pbA, pbB, pbC, pbD, pbE, pbF, pbG = pb

    x_blk = [k.sb(f"xb{i}", [128, 512], F32) for i in range(8)]
    xn_blk = [k.sb(f"xn{i}", [128, 1024], BF16) for i in range(8)]
    ln = LNBlock(k, T, ps3=(pbA, pbB, pbC))
    xtok = [k.sb(f"xt{i}", [128, 1032], BF16) for i in range(32)]
    Lt = k.sb("Lt", [34, T], BF16)
    Rt = k.sb("Rt", [34, T], BF16)
    sqy = k.sb("sqy", [32, 512], F32)
    nrow = k.sb("nrow", [1, 512], F32)
    nrow_b = k.sb("nrow_b", [1, T], BF16)
    ex = [k.sb(f"exb{i}", [128, 512], BF16) for i in range(32)]
    rrow = k.sb("rrow", [1, 512], F32)
    rb = k.sb("rb", [128, 512], F32)
    xres = [k.sb(f"xr{i}", [128, 512], F32) for i in range(2)]
    oxs = [k.sb(f"oxsb{i}", [128, 512], F32) for i in range(2)]
    idf = k.sb("idf", [128, 128], F32)
    idb2 = k.sb("idb2", [128, 128], BF16)
    bU_sb = [k.sb(f"bu{i}", [128, 32], BF16) for i in range(8)]

    ld = k.ctr("ld")
    xld = k.ctr("xld")
    rld = [k.ctr("rld0"), k.ctr("rld1")]
    dmv = k.ctr("dmv")
    pe = k.ctr("pe")
    ve = k.ctr("ve")
    se = k.ctr("se")
    st = k.ctr("st")

    def script(ctx):
        ctx.dma("sync", idf[:], identf, ld)
        for i in range(8):
            ldv = ctx.dma("sync", bU_sb[i][:], bU[i * 128:(i + 1) * 128, :],
                          ld)
        ctx.wait("vector", ld, ldv)
        ctx.copy(idb2[:], idf[:])
        iniv = ln.init(ctx, ve)
        for eng in ("tensor", "scalar"):
            ctx.wait(eng, ve, iniv)
            ctx.wait(eng, ld, ldv)
        ctx.dma("gpsimd", Lt[33:34, :], minus1, dmv)
        dv = ctx.dma("gpsimd", Rt[32:33, :], minus1, dmv)
        ctx.wait("tensor", dmv, dv)

        state = {"norm_v": 0}

        def pre_block(ctx, tb, t0, tw):
            if tb > 0:
                ctx.wait("sync", ln.ve, state["norm_v"])
            for ci in range(8):
                xv = ctx.dma("sync", x_blk[ci][:, 0:tw],
                             xTb[ci * 128:(ci + 1) * 128, t0:t0 + tw], xld)
            for eng in ("vector", "scalar", "tensor"):
                ctx.wait(eng, xld, xv)

        def post_block(ctx, tb, t0, tw):
            state["norm_v"] = ln.ve.v
            ctx.wait("tensor", ln.ve, ln.ve.v)
            par = (t0 // 512) % 2
            xs = slice(par * 512, par * 512 + tw)
            for ci in range(8):
                pev = ctx.mm(pbD[0:32, 0:tw], bU_sb[ci][:],
                             xn_blk[ci][:, xs],
                             start=(ci == 0), stop=(ci == 7), ctr=pe)
            ctx.wait("scalar", pe, pev)
            ctx.act(Lt[0:32, t0:t0 + tw], pbD[0:32, 0:tw], AF.Copy,
                    scale=float(np.sqrt(2.0)))
            ctx.act(Rt[0:32, t0:t0 + tw], pbD[0:32, 0:tw], AF.Copy,
                    scale=float(np.sqrt(2.0)))
            sev = ctx.act(sqy[:, 0:tw], pbD[0:32, 0:tw], AF.Square, ctr=se)
            ctx.wait("tensor", se, sev)
            pev = ctx.mm(pbE[0:1, 0:tw], ln.ones128f[0:32, 0:1],
                         sqy[0:32, 0:tw], ctr=pe)
            ctx.wait("vector", pe, pev)
            vev = ctx.copy(nrow_b[0:1, t0:t0 + tw], pbE[0:1, 0:tw], ctr=ve)
            ctx.wait("gpsimd", ve, vev)
            dv = ctx.dma("gpsimd", Lt[32:33, t0:t0 + tw],
                         nrow_b[0:1, t0:t0 + tw], dmv)
            dv = ctx.dma("gpsimd", Rt[33:34, t0:t0 + tw],
                         nrow_b[0:1, t0:t0 + tw], dmv)
            for sl in range(tw // 128):
                si = (t0 + sl * 128) // 128
                for dc in range(8):
                    pev = ctx.trans(psb[:, 0:128],
                                    xn_blk[dc][:, par * 512 + sl * 128:
                                               par * 512 + sl * 128 + 128],
                                    idb2[:], ctr=pe)
                    ctx.wait("vector", pe, pev)
                    vev = ctx.copy(xtok[si][:, dc * 128:(dc + 1) * 128],
                                   psb[:, 0:128], ctr=ve)
                    ctx.wait("tensor", ve, vev)
                vev = ctx.memset(xtok[si][:, 1024:1025], 1.0, ctr=ve)
            ctx.wait("tensor", ve, ve.v)

        emit_ln(ctx, ln,
                x_at=lambda ci, t0, tw: x_blk[ci][:, 0:tw],
                xn_at=lambda ci, t0, tw: xn_blk[ci][
                    :, ((t0 // 512) % 2) * 512:((t0 // 512) % 2) * 512 + tw],
                pre_block=pre_block, post_block=post_block)
        ctx.wait("tensor", dmv, dmv.v)

        st_hist = []
        n_out = 0
        for qb in range(4):
            q0 = 2048 + qb * 512
            for si in range(32):
                psc = pbC if si % 2 == 0 else pbD
                ctx.wait("tensor", se, se.v)
                pev = ctx.mm(psc[:, 0:512], Lt[:, si * 128:(si + 1) * 128],
                             Rt[:, q0:q0 + 512], start=True, stop=True,
                             ctr=pe)
                ctx.wait("scalar", pe, pev)
                sev = ctx.act(ex[si][:, 0:512], psc[:, 0:512], AF.Exp,
                              ctr=se)
            ctx.wait("tensor", se, se.v)
            pev = None
            for si in range(32):
                pev = ctx.mm(pbF[0:1, 0:512], xtok[si][:, 1024:1025],
                             ex[si][:, 0:512], start=(si == 0),
                             stop=(si == 31), ctr=pe)
            ctx.wait("vector", pe, pev)
            vev = ctx.recip(rrow[0:1, 0:512], pbF[0:1, 0:512], ctr=ve)
            ctx.wait("tensor", ve, vev)
            pev = ctx.mm(pbG[:, 0:512], ln.ones128f[0:1, :],
                         rrow[0:1, 0:512], ctr=pe)
            ctx.wait("vector", pe, pev)
            vev = ctx.copy(rb[:, 0:512], pbG[:, 0:512], ctr=ve)
            ctx.wait("tensor", ve, vev)
            for dc in range(8):
                pd = pbE if dc % 2 == 0 else pbF
                for si in range(32):
                    pev = ctx.mm(pd[:, 0:512],
                                 xtok[si][:, dc * 128:(dc + 1) * 128],
                                 ex[si][:, 0:512], start=(si == 0),
                                 stop=(si == 31), ctr=(pe if si == 31 else None))
                par = n_out % 2
                if n_out >= 2:
                    ctx.wait("sync", st, st_hist[n_out - 2])
                rv = ctx.dma("sync", xres[par][:],
                             xq[dc * 128:(dc + 1) * 128,
                                qb * 512:(qb + 1) * 512], rld[par])
                ctx.wait("vector", pe, pev)
                ctx.wait("vector", rld[par], rv)
                if n_out >= 2:
                    ctx.wait("vector", st, st_hist[n_out - 2])
                ctx.tt(oxs[par][:], pd[:, 0:512], rb[:, 0:512], ALU.mult)
                vev = ctx.tt(oxs[par][:], oxs[par][:], xres[par][:], ALU.add,
                             ctr=ve)
                ctx.wait("gpsimd", ve, vev)
                sv = ctx.dma("gpsimd",
                             oxT[dc * 128:(dc + 1) * 128,
                                 qb * 512:(qb + 1) * 512], oxs[par][:], st)
                st_hist.append(sv)
                n_out += 1
                ctx.wait("tensor", ve, ve.v)
        ctx.wait("gpsimd", st, st.v)

    return k.finish(script)


def run_attn_b0(xT_full, W):
    nc = built("attn_b0", build_attn_b0)
    identf = np.eye(128, dtype=np.float32)
    minus1 = np.full((1, T), -1.0, dtype=BF)
    in_maps = []
    for c in range(NCORE):
        b, h = c // 2, c % 2
        own = xT_full[b][:, h * TK:(h + 1) * TK]
        other = xT_full[b][:, (1 - h) * TK:(2 - h) * TK]
        rot = np.concatenate([other, own], axis=1)
        in_maps.append(dict(xTb=c_(rot), xq=c_(own), bU=W["b_U"],
                            identf=identf, minus1=minus1))
    res = run8(nc, in_maps)
    out = np.empty_like(xT_full)
    for c in range(NCORE):
        b, h = c // 2, c % 2
        out[b][:, h * TK:(h + 1) * TK] = res.results[c]["oxT"]
    return out



def build_gffn():
    k = Kern("gffn")
    xTb = k.inp("xTb", [1024, T], F32)
    xq = k.inp("xq", [1024, TK], F32)
    fWq = k.inp("fWq", [1024, 1024], BF16)
    fWk = k.inp("fWk", [1024, 1024], BF16)
    fWv = k.inp("fWv", [1024, 1024], BF16)
    fWo = k.inp("fWo", [1024, 1024], BF16)
    Ws1 = k.inp("Ws1", [1024, 256], F32)
    bs1 = k.inp("bs1", [2 * 128, 1], F32)
    Ws2 = k.inp("Ws2", [256, 1], F32)
    bs2 = k.inp("bs2", [1, 1], F32)
    identf = k.inp("identf", [128, 128], F32)
    REP = k.inp("REP", [256, TK], BF16)
    oxT = k.out("oxT", [1024, TK], F32)

    pb = [k.ps(f"pb{i}", [128, 512], F32) for i in range(7)]
    psb = k.ps("psb", [128, 128], BF16)
    pbA, pbB, pbC, pbD, pbE, pbF, pbG = pb

    x_blk = [k.sb(f"xb{i}", [128, 512], F32) for i in range(8)]
    xn_blk = [k.sb(f"xn{i}", [128, 1024], BF16) for i in range(8)]
    ln = LNBlock(k, T, ps3=(pbA, pbB, pbC))
    meansT = [k.sb(f"mn{i}", [128, 256], F32) for i in range(8)]
    meansB = [k.sb(f"mb{i}", [128, 256], BF16) for i in range(8)]
    QT = [k.sb(f"qt{i}", [128, 256], BF16) for i in range(8)]
    KT = [k.sb(f"kt{i}", [128, 256], BF16) for i in range(8)]
    Vg = [k.sb(f"vg{i}", [128, 1025], BF16) for i in range(2)]
    ctxT = [k.sb(f"cx{i}", [128, 256], BF16) for i in range(8)]
    og = [k.sb(f"og{i}", [128, 1024], BF16) for i in range(2)]
    hT = [k.sb(f"ht{i}", [128, 256], F32) for i in range(2)]
    srow = k.sb("srow", [1, 256], F32)
    scol = [k.sb(f"sc{i}", [128, 1], F32) for i in range(2)]
    sbc = k.sb("sbc", [128, 256], F32)
    cmp2 = [k.sb(f"cmp{i}", [128, 256], F32) for i in range(2)]
    selrow = k.sb("selrow", [1, 256], F32)
    selcol = [k.sb(f"sel{i}", [128, 1], F32) for i in range(2)]
    exg = [k.sb(f"exg{i}", [128, 256], BF16) for i in range(2)]
    rrow = k.sb("rrow", [1, 256], F32)
    rbg = k.sb("rbg", [64, 256], F32)
    xres = [k.sb(f"xr{i}", [128, 512], F32) for i in range(2)]
    oxs = [k.sb(f"oxsb{i}", [128, 512], F32) for i in range(2)]
    idf = k.sb("idf", [128, 128], F32)
    rep_sb = [k.sb(f"rp{i}", [128, TK], BF16) for i in range(2)]
    ws1_sb = [k.sb(f"w1s{i}", [128, 256], F32) for i in range(8)]
    ws2_sb = [k.sb(f"w2s{i}", [128, 1], F32) for i in range(2)]
    bs1_sb = [k.sb(f"b1s{i}", [128, 1], F32) for i in range(2)]
    bs2_sb = k.sb("b2s", [1, 1], F32)
    wq_sb = [k.sb(f"wqs{i}", [128, 1024], BF16) for i in range(8)]
    wk_sb = [k.sb(f"wks{i}", [128, 1024], BF16) for i in range(8)]
    wv_sb = [k.sb(f"wvs{i}", [128, 1024], BF16) for i in range(8)]
    wo_sb = [k.sb(f"wos{i}", [128, 1024], BF16) for i in range(8)]

    ld = k.ctr("ld")
    xld = k.ctr("xld")
    rld = [k.ctr("rld0"), k.ctr("rld1")]
    pe = k.ctr("pe")
    ve = k.ctr("ve")
    se = k.ctr("se")
    st = k.ctr("st")

    def script(ctx):
        ctx.dma("sync", idf[:], identf, ld)
        for i in range(8):
            ctx.dma("sync", ws1_sb[i][:], Ws1[i * 128:(i + 1) * 128, :], ld)
        for i in range(2):
            ctx.dma("sync", ws2_sb[i][:], Ws2[i * 128:(i + 1) * 128, :], ld)
            ctx.dma("sync", bs1_sb[i][:], bs1[i * 128:(i + 1) * 128, :], ld)
            ctx.dma("sync", rep_sb[i][:], REP[i * 128:(i + 1) * 128, :], ld)
        ctx.dma("sync", bs2_sb[:], bs2, ld)
        for i in range(8):
            ctx.dma("sync", wq_sb[i][:], fWq[i * 128:(i + 1) * 128, :], ld)
            ctx.dma("sync", wk_sb[i][:], fWk[i * 128:(i + 1) * 128, :], ld)
            ctx.dma("sync", wv_sb[i][:], fWv[i * 128:(i + 1) * 128, :], ld)
            ldv = ctx.dma("sync", wo_sb[i][:], fWo[i * 128:(i + 1) * 128, :],
                          ld)
        ctx.wait("vector", ld, ldv)
        iniv = ln.init(ctx, ve)
        for eng in ("tensor", "scalar"):
            ctx.wait(eng, ve, iniv)
            ctx.wait(eng, ld, ldv)

        state = {"norm_v": 0}

        def pre_block(ctx, tb, t0, tw):
            if tb > 0:
                ctx.wait("sync", ln.ve, state["norm_v"])
            for ci in range(8):
                xv = ctx.dma("sync", x_blk[ci][:, 0:tw],
                             xTb[ci * 128:(ci + 1) * 128, t0:t0 + tw], xld)
            for eng in ("vector", "scalar", "tensor"):
                ctx.wait(eng, xld, xv)

        def post_block(ctx, tb, t0, tw):
            state["norm_v"] = ln.ve.v
            par = (t0 // 512) % 2
            g0 = t0 // 16
            for ci in range(8):
                ctx.red(meansT[ci][:, g0:g0 + 32].rearrange("p g -> p g 1")
                        if False else meansT[ci][:, g0:g0 + 32],
                        xn_blk[ci][:, par * 512:par * 512 + tw]
                        .rearrange("p (g w) -> p g w", w=16), ALU.add)
                vev = ctx.ts(meansT[ci][:, g0:g0 + 32],
                             meansT[ci][:, g0:g0 + 32], 1.0 / 16, ALU.mult,
                             ctr=ve)
                ctx.copy(meansB[ci][:, g0:g0 + 32], meansT[ci][:, g0:g0 + 32],
                         ctr=ve)

        emit_ln(ctx, ln,
                x_at=lambda ci, t0, tw: x_blk[ci][:, 0:tw],
                xn_at=lambda ci, t0, tw: xn_blk[ci][
                    :, ((t0 // 512) % 2) * 512:((t0 // 512) % 2) * 512 + tw],
                pre_block=pre_block, post_block=post_block)

        ctx.wait("tensor", ve, ve.v)
        for ft in range(2):
            for ci in range(8):
                pev = ctx.mm(pbD[:, 0:256], ws1_sb[ci][:, ft * 128:(ft + 1) * 128],
                             meansT[ci][:], start=(ci == 0), stop=(ci == 7),
                             ctr=pe)
            ctx.wait("scalar", pe, pev)
            sev = ctx.act(hT[ft][:], pbD[:, 0:256], AF.Relu,
                          bias=bs1_sb[ft][:], ctr=se)
            ctx.wait("tensor", se, sev)
        for ft in range(2):
            pev = ctx.mm(pbE[0:1, 0:256], ws2_sb[ft][:], hT[ft][:],
                         start=(ft == 0), stop=(ft == 1), ctr=pe)
        ctx.wait("vector", pe, pev)
        vev = ctx.ts(srow[0:1, :], pbE[0:1, 0:256], bs2_sb[0:1, :], ALU.add,
                     ctr=ve)
        ctx.wait("tensor", ve, vev)
        for gt in range(2):
            pev = ctx.trans(pbF[0:128, 0:1], srow[0:1, gt * 128:(gt + 1) * 128],
                            idf[0:1, 0:1], ctr=pe)
            ctx.wait("vector", pe, pev)
            vev = ctx.copy(scol[gt][:], pbF[0:128, 0:1], ctr=ve)
            ctx.wait("tensor", ve, vev)
        pev = ctx.mm(pbG[:, 0:256], ln.ones128f[0:1, :], srow[0:1, :], ctr=pe)
        ctx.wait("vector", pe, pev)
        vev = ctx.copy(sbc[:], pbG[:, 0:256], ctr=ve)
        for gt in range(2):
            vev = ctx.ts(cmp2[gt][:], sbc[:], scol[gt][:], ALU.is_lt, ctr=ve)
            ctx.wait("tensor", ve, vev)
            pev = ctx.mm(pbE[0:1, 0:256], ln.ones128f[:, 0:1], cmp2[gt][:],
                         start=(gt == 0), stop=(gt == 1), ctr=pe)
        ctx.wait("vector", pe, pev)
        vev = ctx.ts(selrow[0:1, :], pbE[0:1, 0:256], 64.0, ALU.is_lt,
                     ctr=ve)
        ctx.wait("tensor", ve, vev)
        for gt in range(2):
            pev = ctx.trans(pbF[0:128, 0:1],
                            selrow[0:1, gt * 128:(gt + 1) * 128],
                            idf[0:1, 0:1], ctr=pe)
            ctx.wait("vector", pe, pev)
            vev = ctx.copy(selcol[gt][:], pbF[0:128, 0:1], ctr=ve)
            ctx.wait("tensor", ve, vev)

        for ci_out in range(8):
            for ci in range(8):
                ctx.mm(pbC[:, 0:256], wq_sb[ci][:, ci_out * 128:(ci_out + 1) * 128],
                       meansB[ci][:], start=(ci == 0), stop=(ci == 7))
            for ci in range(8):
                pev = ctx.mm(pbD[:, 0:256],
                             wk_sb[ci][:, ci_out * 128:(ci_out + 1) * 128],
                             meansB[ci][:], start=(ci == 0), stop=(ci == 7),
                             ctr=pe)
            ctx.wait("vector", pe, pev)
            ctx.copy(QT[ci_out][:], pbC[:, 0:256])
            vev = ctx.copy(KT[ci_out][:], pbD[:, 0:256], ctr=ve)
            ctx.wait("tensor", ve, vev)
        for gt in range(2):
            for eb in range(2):
                ctx.wait("tensor", ve, ve.v)
                for ci in range(8):
                    pev = ctx.mm(pbG[:, 0:512],
                                 meansB[ci][:, gt * 128:(gt + 1) * 128],
                                 wv_sb[ci][:, eb * 512:(eb + 1) * 512],
                                 start=(ci == 0), stop=(ci == 7), ctr=pe)
                ctx.wait("vector", pe, pev)
                vev = ctx.copy(Vg[gt][:, eb * 512:(eb + 1) * 512],
                               pbG[:, 0:512], ctr=ve)
            vev = ctx.memset(Vg[gt][:, 1024:1025], 1.0, ctr=ve)
            ctx.wait("tensor", ve, vev)

        for h in range(16):
            for gt in range(2):
                psc = pbC if gt == 0 else pbD
                hc, hr = h // 2, (h % 2) * 64
                ctx.wait("tensor", se, se.v)
                ctx.wait("tensor", ve, ve.v)
                pev = ctx.mm(psc[:, 0:256],
                             KT[hc][hr:hr + 64, gt * 128:(gt + 1) * 128],
                             QT[hc][hr:hr + 64, :], start=True, stop=True,
                             ctr=pe)
                ctx.wait("scalar", pe, pev)
                ctx.wait("scalar", ve, ve.v)
                sev = ctx.act(exg[gt][:], psc[:, 0:256], AF.Exp, scale=0.125,
                              ctr=se)
                ctx.wait("vector", se, sev)
                vev = ctx.ts(exg[gt][:], exg[gt][:], selcol[gt][:], ALU.mult,
                             ctr=ve)
                ctx.wait("tensor", ve, vev)
            for gt in range(2):
                pev = ctx.mm(pbE[0:65, 0:256],
                             Vg[gt][:, h * 64:h * 64 + 64]
                             if False else Vg[gt][:, list(range(h * 64, h * 64 + 64)) + [1024]]
                             if False else Vg[gt][:, h * 64:h * 64 + 64],
                             exg[gt][:], start=(gt == 0), stop=False,
                             ctr=None) if False else None
            ctx.wait("tensor", ve, ve.v)
            for gt in range(2):
                ctx.mm(pbE[0:64, 0:256], Vg[gt][:, h * 64:h * 64 + 64],
                       exg[gt][:], start=(gt == 0), stop=(gt == 1))
                pev = ctx.mm(pbF[0:1, 0:256], Vg[gt][:, 1024:1025],
                             exg[gt][:], start=(gt == 0), stop=(gt == 1),
                             ctr=pe)
            ctx.wait("vector", pe, pev)
            vev = ctx.recip(rrow[0:1, :], pbF[0:1, 0:256], ctr=ve)
            ctx.wait("tensor", ve, vev)
            pev = ctx.mm(pbG[0:64, 0:256], ln.ones128f[0:1, 0:64],
                         rrow[0:1, :], ctr=pe)
            ctx.wait("vector", pe, pev)
            ctx.copy(rbg[:, :], pbG[0:64, 0:256])
            vev = ctx.tt(ctxT[h // 2][(h % 2) * 64:(h % 2) * 64 + 64, :],
                         pbE[0:64, 0:256], rbg[:, :], ALU.mult, ctr=ve)
            ctx.wait("tensor", ve, vev)

        for gt in range(2):
            for db in range(2):
                for ci in range(8):
                    pev = ctx.mm(pbC[:, 0:512],
                                 ctxT[ci][:, gt * 128:(gt + 1) * 128],
                                 wo_sb[ci][:, db * 512:(db + 1) * 512],
                                 start=(ci == 0), stop=(ci == 7), ctr=pe)
                ctx.wait("vector", pe, pev)
                vev = ctx.ts(og[gt][:, db * 512:(db + 1) * 512],
                             pbC[:, 0:512], selcol[gt][:], ALU.mult, ctr=ve)
                ctx.wait("tensor", ve, vev)

        st_hist = []
        n_out = 0
        for dc in range(8):
            for tb in range(4):
                par = n_out % 2
                if n_out >= 2:
                    ctx.wait("sync", st, st_hist[n_out - 2])
                rv = ctx.dma("sync", xres[par][:],
                             xq[dc * 128:(dc + 1) * 128,
                                tb * 512:(tb + 1) * 512], rld[par])
                for gt in range(2):
                    pev = ctx.mm(pbE[:, 0:512],
                                 og[gt][:, dc * 128:(dc + 1) * 128],
                                 rep_sb[gt][:, tb * 512:(tb + 1) * 512],
                                 start=(gt == 0), stop=(gt == 1), ctr=pe)
                ctx.wait("vector", pe, pev)
                ctx.wait("vector", rld[par], rv)
                if n_out >= 2:
                    ctx.wait("vector", st, st_hist[n_out - 2])
                vev = ctx.tt(oxs[par][:], pbE[:, 0:512], xres[par][:],
                             ALU.add, ctr=ve)
                ctx.wait("gpsimd", ve, vev)
                sv = ctx.dma("gpsimd",
                             oxT[dc * 128:(dc + 1) * 128,
                                 tb * 512:(tb + 1) * 512], oxs[par][:], st)
                st_hist.append(sv)
                n_out += 1
                ctx.wait("tensor", ve, ve.v)
        ctx.wait("gpsimd", st, st.v)

    return k.finish(script)


def run_gffn(xT_full, W, raw):
    nc = built("gffn", build_gffn)
    identf = np.eye(128, dtype=np.float32)
    g = 128 + np.arange(TK) // 16
    REP = (np.arange(256)[:, None] == g[None, :]).astype(BF)
    bs1 = c_(raw["L0_ffn_bs1"].astype(np.float32).reshape(256, 1))
    bs2 = c_(raw["L0_ffn_bs2"].astype(np.float32).reshape(1, 1))
    Ws1 = c_(raw["L0_ffn_Ws1"].astype(np.float32))
    Ws2 = c_(raw["L0_ffn_Ws2"].astype(np.float32))
    in_maps = []
    for c in range(NCORE):
        b, h = c // 2, c % 2
        own = xT_full[b][:, h * TK:(h + 1) * TK]
        other = xT_full[b][:, (1 - h) * TK:(2 - h) * TK]
        rot = np.concatenate([other, own], axis=1)
        in_maps.append(dict(xTb=c_(rot), xq=c_(own),
                            fWq=W["f_Wq"], fWk=W["f_Wk"], fWv=W["f_Wv"],
                            fWo=W["f_Wo"], Ws1=Ws1, bs1=bs1, Ws2=Ws2,
                            bs2=bs2, identf=identf, REP=REP))
    res = run8(nc, in_maps)
    out = np.empty_like(xT_full)
    for c in range(NCORE):
        b, h = c // 2, c % 2
        out[b][:, h * TK:(h + 1) * TK] = res.results[c]["oxT"]
    return out



G_EPS = float(np.finfo(np.float32).eps)
G_LINIT = 0.8 - 0.6 * float(np.exp(-0.3 * 0))
G_SCALE = 1.0 / float(np.sqrt(512))


def build_dattn():
    k = Kern("dattn")
    xq = k.inp("xq", [1024, TK], F32)
    WqT = k.inp("WqT", [1024, 1024], BF16)
    WkT = k.inp("WkT", [1024, 1024], BF16)
    WvT = k.inp("WvT", [1024, 1024], BF16)
    W1T = k.inp("W1T", [1024, 1024], BF16)
    W2T = k.inp("W2T", [1024, 1024], BF16)
    lam = k.inp("lam", [1, 4], F32)
    g1c = k.inp("g1c", [1024, 1], F32)
    g2r = k.inp("g2r", [1, 64], F32)
    g3full = k.inp("g3full", [128, 64], F32)
    identf = k.inp("identf", [128, 128], F32)
    maskq = k.inp("maskq", [128, 512], BF16)
    blkS = k.inp("blkS", [128, 8], F32)
    blkST = k.inp("blkST", [8, 128], F32)
    oxT = k.out("oxT", [1024, TK], F32)

    pb = [k.ps(f"pb{i}", [128, 512], F32) for i in range(7)]
    psb = k.ps("psb", [128, 256], BF16)
    pbA, pbB, pbC, pbD, pbE, pbF, pbG = pb

    x_blk = [k.sb(f"xb{i}", [128, 512], F32) for i in range(8)]
    xn_blk = [k.sb(f"xn{i}", [128, 1024], BF16) for i in range(8)]
    ln = LNBlock(k, TK, ps3=(pbA, pbB, pbC))
    wA = [k.sb(f"wA{i}", [128, 1024], BF16) for i in range(8)]
    wB = [k.sb(f"wB{i}", [128, 1024], BF16) for i in range(8)]
    wC = [k.sb(f"wC{i}", [128, 1024], BF16) for i in range(8)]
    Qb = k.sb("Qb", [128, 4096], BF16)
    Kb = k.sb("Kb", [128, 4096], BF16)
    Vb = k.sb("Vb", [128, 4096], BF16)
    QV = k.sb("QV", [128, 8192], BF16)
    KTb = k.sb("KTb", [128, 8192], BF16)
    Qs = QV[0:64, :]
    Vs = QV[64:128, :]
    Ks = KTb[0:64, :]
    Tbig = KTb[64:128, :]
    A3 = [k.sb(f"A3{i}", [128, 1024], BF16) for i in range(8)]
    A3b = [k.sb(f"A3b{i}", [128, 1024], BF16) for i in range(8)]
    sm1 = k.sb("sm1", [128, 512], BF16)
    sm2 = k.sb("sm2", [128, 512], BF16)
    vhf = k.sb("vhf", [128, 256], BF16)
    att_f = k.sb("att_f", [128, 256], F32)
    att2 = k.sb("att2", [128, 256], F32)
    att2b = k.sb("att2b", [128, 256], BF16)
    sq2 = k.sb("sq2", [128, 256], F32)
    rms4 = k.sb("rms4", [128, 4], F32)
    rstdx = k.sb("rstdx", [128, 4], F32)
    ssq8 = k.sb("ssq8", [8, 4], F32)
    lcol = k.sb("lcol", [128, 1], F32)
    g2b = k.sb("g2b", [128, 64], F32)
    g3t = k.sb("g3t", [128, 64], F32)
    epsc = k.sb("epsc", [128, 1], F32)
    ltmp = k.sb("ltmp", [1, 4], F32)
    xres = [k.sb(f"xr{i}", [128, 512], F32) for i in range(2)]
    oxs = [k.sb(f"oxsb{i}", [128, 512], F32) for i in range(2)]
    idf = k.sb("idf", [128, 128], F32)
    idb2 = k.sb("idb2", [128, 128], BF16)
    mq = k.sb("mq", [128, 512], BF16)
    blkS_sb = k.sb("blkS_sb", [128, 8], F32)
    blkST_sb = k.sb("blkST_sb", [8, 128], F32)
    g1sb = [k.sb(f"g1s{i}", [128, 1], F32) for i in range(8)]
    g2sb = k.sb("g2sb", [1, 64], F32)

    def hts(ci, tb):
        sl = slice(tb * 512, (tb + 1) * 512)
        if ci < 2:
            return Qb[:, ci * 2048:(ci + 1) * 2048][:, sl]
        if ci < 4:
            return Kb[:, (ci - 2) * 2048:(ci - 1) * 2048][:, sl]
        if ci < 6:
            return Vb[:, (ci - 4) * 2048:(ci - 3) * 2048][:, sl]
        return wC[(ci - 6) * 4 + tb][:, 0:512]

    def a3(ci):
        return (A3b[ci][:], A3[ci][:])

    ld = k.ctr("ld")
    xld = k.ctr("xld")
    sld = k.ctr("sld")
    rld = [k.ctr("rld0"), k.ctr("rld1")]
    pe = k.ctr("pe")
    ve = k.ctr("ve")
    se = k.ctr("se")
    st = k.ctr("st")

    def script(ctx):
        ctx.dma("sync", idf[:], identf, ld)
        ctx.dma("sync", mq[:], maskq, ld)
        ctx.dma("sync", blkS_sb[:], blkS, ld)
        ctx.dma("sync", blkST_sb[:], blkST, ld)
        ctx.dma("sync", ltmp[:], lam, ld)
        ctx.dma("sync", g2sb[:], g2r, ld)
        for i in range(8):
            ctx.dma("sync", g1sb[i][:], g1c[i * 128:(i + 1) * 128, :], ld)
        for i in range(8):
            ctx.dma("sync", wA[i][:], WqT[i * 128:(i + 1) * 128, :], ld)
            ctx.dma("sync", wB[i][:], WkT[i * 128:(i + 1) * 128, :], ld)
            ldv = ctx.dma("sync", wC[i][:], WvT[i * 128:(i + 1) * 128, :],
                          ld)
        ldv = ctx.dma("sync", g3t[:], g3full, ld)
        ctx.wait("vector", ld, ldv)
        ctx.copy(idb2[:], idf[:])
        ctx.memset(epsc[:], G_EPS)
        iniv = ln.init(ctx, ve, rms_eps=G_EPS)
        for eng in ("tensor", "scalar"):
            ctx.wait(eng, ve, iniv)
            ctx.wait(eng, ld, ldv)
        ctx.tt(ltmp[0:1, 0:1], ltmp[0:1, 0:1], ltmp[0:1, 1:2], ALU.mult)
        vev = ctx.tt(ltmp[0:1, 2:3], ltmp[0:1, 2:3], ltmp[0:1, 3:4], ALU.mult,
                     ctr=ve)
        ctx.wait("scalar", ve, vev)
        ctx.act(ltmp[0:1, 0:1], ltmp[0:1, 0:1], AF.Exp)
        sev = ctx.act(ltmp[0:1, 2:3], ltmp[0:1, 2:3], AF.Exp, ctr=se)
        ctx.wait("vector", se, sev)
        ctx.tt(ltmp[0:1, 0:1], ltmp[0:1, 0:1], ltmp[0:1, 2:3], ALU.subtract)
        vev = ctx.ts(ltmp[0:1, 0:1], ltmp[0:1, 0:1], G_LINIT, ALU.add, ctr=ve)
        ctx.wait("tensor", ve, vev)
        pev = ctx.mm(pbD[0:128, 0:1], ln.ones128f[0:1, :], ltmp[0:1, 0:1],
                     ctr=pe)
        ctx.wait("vector", pe, pev)
        vev = ctx.copy(lcol[:], pbD[0:128, 0:1], ctr=ve)
        ctx.wait("tensor", ve, vev)
        pev = ctx.mm(pbD[:, 0:64], ln.ones128f[0:1, :], g2sb[0:1, :], ctr=pe)
        ctx.wait("vector", pe, pev)
        vev = ctx.ts(g2b[:], pbD[:, 0:64], 1.0 - G_LINIT, ALU.mult, ctr=ve)
        ctx.wait("tensor", ve, vev)

        state = {"norm_v": 0, "blk_pe": 0}

        def pre_block(ctx, tb, t0, tw):
            if tb > 0:
                ctx.wait("sync", ln.ve, state["norm_v"])
                ctx.wait("sync", pe, state["blk_pe"])
            for ci in range(8):
                xv = ctx.dma("sync", x_blk[ci][:, 0:tw],
                             xq[ci * 128:(ci + 1) * 128, t0:t0 + tw], xld)
            for eng in ("vector", "scalar", "tensor"):
                ctx.wait(eng, xld, xv)

        def post_block(ctx, tb, t0, tw):
            state["norm_v"] = ln.ve.v
            ctx.wait("tensor", ln.ve, ln.ve.v)
            par = (t0 // 512) % 2
            xs = slice(par * 512, par * 512 + tw)
            for dst, wsb in ((Qb, wA), (Kb, wB), (Vb, wC)):
                for co in range(8):
                    ctx.wait("tensor", ve, ve.v)
                    for ci in range(8):
                        pev = ctx.mm(pbD[:, 0:512],
                                     wsb[ci][:, co * 128:(co + 1) * 128],
                                     xn_blk[ci][:, xs],
                                     start=(ci == 0), stop=(ci == 7), ctr=pe)
                    ctx.wait("vector", pe, pev)
                    vev = ctx.copy(dst[:, co * 512:(co + 1) * 512],
                                   pbD[:, 0:512], ctr=ve)
            ctx.wait("scalar", ve, ve.v)
            ctx.wait("vector", se, se.v)
            for dst in (Qb, Kb):
                for co in range(8):
                    sl = slice(co * 512, (co + 1) * 512)
                    vev = ctx.ts(att_f[:, 0:256].bitcast(BF16)[:, 0:512]
                                 if False else sq2[:, 0:256].bitcast(BF16)[:, 0:512],
                                 dst[:, sl], 0.0, ALU.min, ctr=ve)
                    ctx.wait("scalar", ve, vev)
                    sev = ctx.act(att2[:, 0:256].bitcast(BF16)[:, 0:512],
                                  sq2[:, 0:256].bitcast(BF16)[:, 0:512],
                                  AF.Exp, ctr=se)
                    ctx.wait("vector", se, sev)
                    ctx.stt(dst[:, sl], dst[:, sl], 0.0,
                            att2[:, 0:256].bitcast(BF16)[:, 0:512],
                            ALU.max, ALU.add)
                    vev = ctx.ts(dst[:, sl], dst[:, sl], 1.0, ALU.subtract,
                                 ctr=ve)
            ctx.wait("gpsimd", ve, ve.v)
            for h in range(16):
                par2, hh = h % 2, h // 2
                for dstT, src, r0 in (
                        (QV[0:32, :], Qb, 0), (QV[32:64, :], Qb, 32),
                        (KTb[0:32, :], Kb, 0), (KTb[32:64, :], Kb, 32)):
                    sv = ctx.dma(
                        "gpsimd",
                        dstT.rearrange("p (g c) -> p g c", c=128)[
                            :, :, h * 8:(h + 1) * 8],
                        src[par2 * 64 + r0:par2 * 64 + r0 + 32,
                            hh * 512:(hh + 1) * 512].rearrange(
                            "p (g s) -> p g s", s=8),
                        sld)
                sv = ctx.dma(
                    "gpsimd",
                    Vs.rearrange("p (g c) -> p g c", c=128)[
                        :, :, h * 8:(h + 1) * 8],
                    Vb[par2 * 64:par2 * 64 + 64,
                       hh * 512:(hh + 1) * 512].rearrange(
                        "p (g s) -> p g s", s=8),
                    sld)
            ctx.wait("tensor", sld, sv)

            for q4 in range(16):
                ctx.wait("tensor", ve, ve.v)
                ctx.wait("tensor", se, se.v)
                for j in range(4):
                    g = q4 * 4 + j
                    gap = lambda src, r0: src[r0:r0 + 32,
                                              g * 128:(g + 1) * 128]
                    ctx.mm(pbD[:, j * 128:(j + 1) * 128], gap(KTb, 0),
                           gap(QV, 0), start=True, stop=True)
                    ctx.mm(pbE[:, j * 128:(j + 1) * 128], gap(KTb, 32),
                           gap(QV, 32), start=True, stop=True)
                    vap = Vs[:, g * 128:(g + 1) * 128]
                    pev = ctx.trans(psb[:, j * 64:(j + 1) * 64], vap,
                                    idb2[64:128, 64:128], ctr=pe)
                ctx.wait("vector", pe, pev)
                ctx.tt(sm1[:], pbD[:, 0:512], mq[:], ALU.mult)
                ctx.tt(sm2[:], pbE[:, 0:512], mq[:], ALU.mult)
                vev = ctx.copy(vhf[:], psb[:, 0:256], ctr=ve)
                ctx.wait("tensor", ve, vev)
                for j in range(4):
                    ctx.mm(pbF[:, j * 64:(j + 1) * 64],
                           sm1[:, j * 128:(j + 1) * 128],
                           vhf[:, j * 64:(j + 1) * 64], start=True, stop=True)
                    pev = ctx.mm(pbG[:, j * 64:(j + 1) * 64],
                                 sm2[:, j * 128:(j + 1) * 128],
                                 vhf[:, j * 64:(j + 1) * 64], start=True,
                                 stop=True, ctr=pe)
                ctx.wait("vector", pe, pev)
                ctx.ts(att_f[:], pbG[:, 0:256], lcol[:], ALU.mult)
                ctx.tt(att_f[:], pbF[:, 0:256], att_f[:], ALU.subtract)
                vev = ctx.ts(att_f[:], att_f[:], G_SCALE, ALU.mult, ctr=ve)
                ctx.tt(sq2[:], att_f[:], att_f[:], ALU.mult)
                ctx.red(rms4[:], sq2[:].rearrange("p (j d) -> p j d", d=64),
                        ALU.add)
                vev = ctx.ts(rms4[:], rms4[:], 1.0 / 64, ALU.mult, ctr=ve)
                ctx.wait("scalar", ve, vev)
                sev = ctx.act(rms4[:], rms4[:], AF.Sqrt, bias=epsc[:], ctr=se)
                ctx.wait("vector", se, sev)
                vev = ctx.recip(rms4[:], rms4[:], ctr=ve)
                for j in range(4):
                    ctx.ts(att2[:, j * 64:(j + 1) * 64],
                           att_f[:, j * 64:(j + 1) * 64],
                           rms4[:, j:j + 1], ALU.mult)
                    vev = ctx.tt(att2[:, j * 64:(j + 1) * 64],
                                 att2[:, j * 64:(j + 1) * 64], g2b[:],
                                 ALU.mult, ctr=ve)
                vev = ctx.tt(sq2[:], att2[:], att2[:], ALU.mult, ctr=ve)
                ctx.wait("tensor", ve, vev)
                pev = ctx.mm(pbD[0:8, 0:256], blkS_sb[:], sq2[:], ctr=pe)
                ctx.wait("vector", pe, pev)
                ctx.red(ssq8[:], pbD[0:8, 0:256].rearrange(
                    "p (j d) -> p j d", d=64), ALU.add)
                vev = ctx.ts(ssq8[:], ssq8[:], 1.0 / 1024, ALU.mult, ctr=ve)
                ctx.wait("scalar", ve, vev)
                sev = ctx.act(ssq8[:], ssq8[:], AF.Sqrt, bias=epsc[0:8, :],
                              ctr=se)
                ctx.wait("vector", se, sev)
                vev = ctx.recip(ssq8[:], ssq8[:], ctr=ve)
                ctx.wait("tensor", ve, vev)
                pev = ctx.mm(pbD[0:128, 0:4], blkST_sb[:], ssq8[:], ctr=pe)
                ctx.wait("vector", pe, pev)
                vev = ctx.copy(rstdx[:], pbD[0:128, 0:4], ctr=ve)
                for j in range(4):
                    ctx.ts(sq2[:, j * 64:(j + 1) * 64],
                           att2[:, j * 64:(j + 1) * 64],
                           rstdx[:, j:j + 1], ALU.mult)
                ctx.tt(sq2[:], sq2[:], g3t[:].rearrange(
                    "p d -> p () d").broadcast_to([128, 4, 64])
                    if False else sq2[:], ALU.bypass) if False else None
                for j in range(4):
                    ctx.tt(sq2[:, j * 64:(j + 1) * 64],
                           sq2[:, j * 64:(j + 1) * 64], g3t[:], ALU.mult)
                ctx.tt(att2[:], sq2[:], att2[:], ALU.add)
                vev = ctx.copy(att2b[:], att2[:], ctr=ve)
                ctx.wait("tensor", ve, vev)
                for j in range(4):
                    g = q4 * 4 + j
                    pev = ctx.trans(psb[0:64, 0:128],
                                    att2b[:, j * 64:(j + 1) * 64],
                                    idb2[:], ctr=pe)
                    ctx.wait("vector", pe, pev)
                    vev = ctx.copy(Tbig[:, g * 128:(g + 1) * 128],
                                   psb[0:64, 0:128], ctr=ve)
                    ctx.wait("tensor", ve, vev)
            ctx.wait("gpsimd", ve, ve.v)
            for h in range(16):
                lo, hi = a3(h // 2)
                dst = (lo if t0 < 1024 else hi)
                doff = (t0 % 1024)
                sv = ctx.dma(
                    "gpsimd",
                    dst[(h % 2) * 64:(h % 2) * 64 + 64, doff:doff + 512]
                    .rearrange("p (g s) -> p g s", s=8),
                    Tbig.rearrange("p (g c) -> p g c", c=128)[
                        :, :, h * 8:(h + 1) * 8],
                    sld)
            ctx.wait("tensor", sld, sv)
            state["blk_pe"] = pe.v

        emit_ln(ctx, ln,
                x_at=lambda ci, t0, tw: x_blk[ci][:, 0:tw],
                xn_at=lambda ci, t0, tw: xn_blk[ci][
                    :, ((t0 // 512) % 2) * 512:((t0 // 512) % 2) * 512 + tw],
                wcol=g1sb,
                rms_eps=G_EPS,
                pre_block=pre_block, post_block=post_block)

        ctx.wait("sync", pe, pe.v)
        for i in range(8):
            ctx.dma("sync", wA[i][:], W1T[i * 128:(i + 1) * 128, :], ld)
            ldv = ctx.dma("sync", wB[i][:], W2T[i * 128:(i + 1) * 128, :],
                          ld)
        ctx.wait("tensor", ld, ldv)
        for co in range(8):
            for tb in range(4):
                ctx.wait("tensor", se, se.v)
                for ci in range(8):
                    lo, hi = a3(ci)
                    src = lo if tb < 2 else hi
                    pev = ctx.mm(pbD[:, 0:512],
                                 wA[ci][:, co * 128:(co + 1) * 128],
                                 src[:, (tb % 2) * 512:(tb % 2) * 512 + 512],
                                 start=(ci == 0), stop=(ci == 7), ctr=pe)
                ctx.wait("scalar", pe, pev)
                sev = ctx.act(hts(co, tb), pbD[:, 0:512], AF.Gelu, ctr=se)
        ctx.wait("tensor", se, se.v)
        st_hist = []
        n_out = 0
        for co in range(8):
            for tb in range(4):
                par = n_out % 2
                if n_out >= 2:
                    ctx.wait("sync", st, st_hist[n_out - 2])
                rv = ctx.dma("sync", xres[par][:],
                             xq[co * 128:(co + 1) * 128,
                                tb * 512:(tb + 1) * 512], rld[par])
                ctx.wait("tensor", ve, ve.v)
                for ci in range(8):
                    pev = ctx.mm(pbE[:, 0:512],
                                 wB[ci][:, co * 128:(co + 1) * 128],
                                 hts(ci, tb),
                                 start=(ci == 0), stop=(ci == 7), ctr=pe)
                ctx.wait("vector", pe, pev)
                ctx.wait("vector", rld[par], rv)
                if n_out >= 2:
                    ctx.wait("vector", st, st_hist[n_out - 2])
                lo, hi = a3(co)
                src = lo if tb < 2 else hi
                ctx.tt(oxs[par][:], pbE[:, 0:512],
                       src[:, (tb % 2) * 512:(tb % 2) * 512 + 512], ALU.add)
                vev = ctx.tt(oxs[par][:], oxs[par][:], xres[par][:], ALU.add,
                             ctr=ve)
                ctx.wait("gpsimd", ve, vev)
                sv = ctx.dma("gpsimd",
                             oxT[co * 128:(co + 1) * 128,
                                 tb * 512:(tb + 1) * 512], oxs[par][:], st)
                st_hist.append(sv)
                n_out += 1
        ctx.wait("gpsimd", st, st.v)

    return k.finish(script)


def run_dattn(xT_full, W, raw):
    nc = built("dattn", build_dattn)
    identf = np.eye(128, dtype=np.float32)
    s = np.arange(128) % 8
    maskq = np.tile((s[:, None] == s[None, :]), (1, 4)).astype(BF)
    blkS = (np.arange(128)[:, None] % 8 == np.arange(8)[None, :]) \
        .astype(np.float32)
    blkST = c_(blkS.T)
    lam = c_(raw["L1_attn_b_lambda_params"].astype(np.float32).reshape(1, 4))
    g1c = c_(raw["L1_attn_b_g1"].astype(np.float32).reshape(1024, 1))
    g2r = c_(raw["L1_attn_b_g2"].astype(np.float32).reshape(1, 64))
    g3f = raw["L1_attn_b_g3"].astype(np.float32).reshape(16, 64)
    g3full = c_(np.repeat(g3f, 8, axis=0))
    in_maps = []
    for c in range(NCORE):
        b, h = c // 2, c % 2
        own = xT_full[b][:, h * TK:(h + 1) * TK]
        in_maps.append(dict(xq=c_(own), WqT=W["g_WqT"], WkT=W["g_WkT"],
                            WvT=W["g_WvT"], W1T=W["g_W1T"], W2T=W["g_W2T"],
                            lam=lam, g1c=g1c, g2r=g2r, g3full=g3full,
                            identf=identf, maskq=c_(maskq), blkS=blkS,
                            blkST=blkST))
    res = run8(nc, in_maps)
    out = np.empty_like(xT_full)
    for c in range(NCORE):
        b, h = c // 2, c % 2
        out[b][:, h * TK:(h + 1) * TK] = res.results[c]["oxT"]
    return out



def build_e1():
    k = Kern("e1")
    XH = 2051
    xh = k.inp("xh", [1024, XH], F32)
    winS = k.inp("winS", [65 * 128, 1024], BF16)
    nwc = k.inp("nwc", [1024, 1], F32)
    convw = k.inp("convw", [2048, 4], F32)
    convb = k.inp("convb", [2048, 1], F32)
    alog = k.inp("alog", [16, 1], F32)
    dtb = k.inp("dtb", [16, 1], F32)
    o_va = k.out("o_va", [2048, TK], BF16)
    o_sg = k.out("o_sg", [2048, TK], BF16)
    o_bc = k.out("o_bc", [4096, TK], BF16)
    o_ld = k.out("o_ld", [16, TK], F32)
    o_xa = k.out("o_xa", [1024, TK], F32)

    pb = [k.ps(f"pb{i}", [128, 512], F32) for i in range(7)]
    pbA, pbB, pbC, pbD, pbE, pbF, pbG = pb
    x_blk = [k.sb(f"xb{i}", [128, 512], F32) for i in range(8)]
    xnT = [k.sb(f"xn{i}", [128, XH], BF16) for i in range(8)]
    ln = LNBlock(k, XH, ps3=(pbA, pbB, pbC), keep_s=True)
    xat = [k.sb(f"xat{i}", [128, 512], F32) for i in range(2)]
    slab = [k.sb(f"sl{i}", [128, 1024], BF16) for i in range(2)]
    valT = [k.sb(f"vt{i}", [128, XH], BF16) for i in range(16)]
    nw_sb = [k.sb(f"nw{i}", [128, 1], F32) for i in range(8)]
    cw_sb = [k.sb(f"cw{i}", [128, 4], F32) for i in range(16)]
    cb_sb = [k.sb(f"cb{i}", [128, 1], F32) for i in range(16)]
    al_sb = k.sb("al_sb", [16, 1], F32)
    dtb_sb = k.sb("dtb_sb", [16, 1], F32)
    nal = k.sb("nal", [16, 1], F32)
    dtt = k.sb("dtt", [16, XH], F32)
    ldt = k.sb("ldt", [16, TK], F32)
    acc = k.sb("acc", [128, 512], F32)
    otile = [k.sb(f"ot{i}", [128, 512], BF16) for i in range(2)]
    t16 = k.sb("t16", [16, 512], F32)

    ld = k.ctr("ld")
    xld = k.ctr("xld")
    wld = [k.ctr("wld0"), k.ctr("wld1")]
    pe = k.ctr("pe")
    ve = k.ctr("ve")
    se = k.ctr("se")
    st = k.ctr("st")

    NBLK = cdiv(XH, 512)

    def script(ctx):
        for i in range(8):
            ctx.dma("sync", nw_sb[i][:], nwc[i * 128:(i + 1) * 128, :], ld)
        for i in range(16):
            ctx.dma("sync", cw_sb[i][:], convw[i * 128:(i + 1) * 128, :], ld)
            ctx.dma("sync", cb_sb[i][:], convb[i * 128:(i + 1) * 128, :], ld)
        ctx.dma("sync", al_sb[:], alog, ld)
        ldv = ctx.dma("sync", dtb_sb[:], dtb, ld)
        ctx.wait("vector", ld, ldv)
        iniv = ln.init(ctx, ve, rms_eps=1e-6)
        for eng in ("tensor", "scalar"):
            ctx.wait(eng, ve, iniv)
            ctx.wait(eng, ld, ldv)
        sev = ctx.act(nal[:], al_sb[:], AF.Exp, ctr=se)
        ctx.wait("vector", se, sev)
        vev = ctx.ts(nal[:], nal[:], -1.0, ALU.mult, ctr=ve)

        state = {"norm_v": 0}
        xa_store = {0: None, 1: None}

        def pre_block(ctx, tb, t0, tw):
            if tb > 0:
                ctx.wait("sync", ln.ve, state["norm_v"])
            for ci in range(8):
                xv = ctx.dma("sync", x_blk[ci][:, 0:tw],
                             xh[ci * 128:(ci + 1) * 128, t0:t0 + tw], xld)
            for eng in ("vector", "scalar", "tensor"):
                ctx.wait(eng, xld, xv)

        def post_block(ctx, tb, t0, tw):
            r5 = ln.row[0:1, 5 * 512:5 * 512 + tw]
            ctx.wait("tensor", ln.ve, ln.ve.v)
            pev = ctx.mm(ln.ps_b2[:, 0:tw], ln.ones128f[0:1, :], r5,
                         ctr=ln.pe)
            ctx.wait("vector", ln.pe, pev)
            lo = max(t0, 3)
            hi = min(t0 + tw, XH)
            for ci in range(8):
                par = ci % 2
                if xa_store[par] is not None:
                    ctx.wait("vector", st, xa_store[par])
                ctx.tt(acc[:, 0:tw], x_blk[ci][:, 0:tw], ln.ps_b1[:, 0:tw],
                       ALU.subtract)
                ctx.tt(acc[:, 0:tw], acc[:, 0:tw], ln.ps_b2[:, 0:tw],
                       ALU.mult)
                vev = ctx.tt(xat[par][:, 0:tw], acc[:, 0:tw],
                             x_blk[ci][:, 0:tw], ALU.add, ctr=ln.ve)
                ctx.wait("gpsimd", ln.ve, vev)
                sv = ctx.dma("gpsimd",
                             o_xa[ci * 128:(ci + 1) * 128, lo - 3:hi - 3],
                             xat[par][:, lo - t0:hi - t0], st)
                xa_store[par] = sv
            state["norm_v"] = ln.ve.v

        emit_ln(ctx, ln,
                x_at=lambda ci, t0, tw: x_blk[ci][:, 0:tw],
                xn_at=lambda ci, t0, tw: xnT[ci][:, t0:t0 + tw],
                wcol=nw_sb, rms_eps=1e-6,
                pre_block=pre_block, post_block=post_block, keep_s=True)
        ln_done = ln.ve.v

        sv0 = ctx.dma("sync", slab[0][:], winS[0:128, :], wld[0])
        wv = {0: (wld[0], sv0)}
        for eng in ("tensor",):
            ctx.wait(eng, ln.ve, ln_done)
        ot_store = {0: None, 1: None}
        for pc in range(65):
            if pc + 1 < 65:
                ctx.wait("sync", pe, pe.v)
                wl = wld[(pc + 1) % 2]
                svn = ctx.dma("sync", slab[(pc + 1) % 2][:],
                              winS[(pc + 1) * 128:(pc + 2) * 128, :], wl)
                wv[pc + 1] = (wl, svn)
            ctx.wait("tensor", wv[pc][0], wv[pc][1])
            pw = 128 if pc < 64 else 16
            for tb in range(NBLK):
                t0 = tb * 512
                tw = min(512, XH - t0)
                pp = pbD if tb % 2 == 0 else pbE
                ctx.wait("tensor", ve, ve.v)
                ctx.wait("tensor", se, se.v)
                for ci in range(8):
                    pev = ctx.mm(pp[0:pw, 0:tw],
                                 slab[pc % 2][:, ci * 128:ci * 128 + pw],
                                 xnT[ci][:, t0:t0 + tw],
                                 start=(ci == 0), stop=(ci == 7), ctr=pe)
                if pc < 16:
                    ctx.wait("vector", pe, pev)
                    vev = ctx.copy(valT[pc][:, t0:t0 + tw], pp[:, 0:tw],
                                   ctr=ve)
                elif pc < 32:
                    ctx.wait("scalar", pe, pev)
                    if ot_store[tb % 2] is not None:
                        ctx.wait("scalar", st, ot_store[tb % 2])
                    sev = ctx.act(otile[tb % 2][:, 0:tw], pp[:, 0:tw],
                                  AF.Silu, ctr=se)
                    ctx.wait("gpsimd", se, sev)
                    lo = max(t0, 3)
                    hi = min(t0 + tw, XH)
                    if hi > lo:
                        sv = ctx.dma("gpsimd",
                                     o_sg[(pc - 16) * 128:(pc - 15) * 128,
                                          lo - 3:hi - 3],
                                     otile[tb % 2][:, lo - t0:hi - t0], st)
                        ot_store[tb % 2] = sv
                elif pc < 64:
                    ctx.wait("vector", pe, pev)
                    if ot_store[tb % 2] is not None:
                        ctx.wait("vector", st, ot_store[tb % 2])
                    vev = ctx.copy(otile[tb % 2][:, 0:tw], pp[:, 0:tw],
                                   ctr=ve)
                    ctx.wait("gpsimd", ve, vev)
                    lo = max(t0, 3)
                    hi = min(t0 + tw, XH)
                    if hi > lo:
                        sv = ctx.dma("gpsimd",
                                     o_bc[(pc - 32) * 128:(pc - 31) * 128,
                                          lo - 3:hi - 3],
                                     otile[tb % 2][:, lo - t0:hi - t0], st)
                        ot_store[tb % 2] = sv
                else:
                    ctx.wait("vector", pe, pev)
                    vev = ctx.copy(dtt[:, t0:t0 + tw], pp[0:16, 0:tw],
                                   ctr=ve)

        ctx.wait("vector", pe, pe.v)
        for tb in range(4):
            sl = slice(3 + tb * 512, 3 + (tb + 1) * 512)
            ol = slice(tb * 512, (tb + 1) * 512)
            vev = ctx.ts(dtt[:, sl], dtt[:, sl], dtb_sb[:], ALU.add, ctr=ve)
            ctx.ts(t16[:], dtt[:, sl], -1.0, ALU.mult)
            vev = ctx.tt(t16[:], t16[:], dtt[:, sl], ALU.min, ctr=ve)
            ctx.wait("scalar", ve, vev)
            ctx.act(t16[:], t16[:], AF.Exp)
            sev = ctx.act(t16[:], t16[:], AF.Ln, bias=ln.ones_col[0:16, :],
                          ctr=se)
            ctx.wait("vector", se, sev)
            ctx.ts(ldt[:, ol], dtt[:, sl], 0.0, ALU.max)
            ctx.tt(ldt[:, ol], ldt[:, ol], t16[:], ALU.add)
            vev = ctx.ts(ldt[:, ol], ldt[:, ol], nal[:], ALU.mult, ctr=ve)
            ctx.wait("gpsimd", ve, vev)
            sv = ctx.dma("gpsimd", o_ld[:, ol], ldt[:, ol], st)

        for cc in range(16):
            for tb in range(4):
                sl0 = 3 + tb * 512
                par = (cc * 4 + tb) % 2
                ctx.wait("vector", se, se.v)
                if ot_store[par] is not None:
                    ctx.wait("scalar", st, ot_store[par])
                ctx.ts(acc[:], valT[cc][:, sl0 - 3:sl0 + 509], cw_sb[cc][:, 0:1],
                       ALU.mult)
                for kk in range(1, 4):
                    ctx.stt(acc[:], valT[cc][:, sl0 - 3 + kk:sl0 + 509 + kk],
                            cw_sb[cc][:, kk:kk + 1], acc[:], ALU.mult,
                            ALU.add)
                vev = ctx.op("vector", lambda: ctx.nc.vector.tensor_copy(
                    acc[:], acc[:]), ctr=ve) if False else None
                vev = ctx.ts(acc[:], acc[:], 1.0, ALU.mult, ctr=ve)
                ctx.wait("scalar", ve, vev)
                sev = ctx.act(otile[par][:], acc[:], AF.Silu,
                              bias=cb_sb[cc][:], ctr=se)
                ctx.wait("gpsimd", se, sev)
                sv = ctx.dma("gpsimd",
                             o_va[cc * 128:(cc + 1) * 128,
                                  tb * 512:(tb + 1) * 512], otile[par][:],
                             st)
                ot_store[par] = sv
        ctx.wait("gpsimd", st, st.v)

    return k.finish(script)


def build_e2():
    k = Kern("e2")
    vA = k.inp("vA", [1024, T], BF16)
    BT = k.inp("BT", [1024, T], BF16)
    CT = k.inp("CT", [1024, T], BF16)
    ldr = k.inp("ldr", [8, T], F32)
    dsk = k.inp("dsk", [8, 128], F32)
    identf = k.inp("identf", [128, 128], F32)
    trilT = k.inp("trilT", [256, 256], BF16)
    o_yT = k.out("o_yT", [1024, T], BF16)

    pb = [k.ps(f"pb{i}", [128, 512], F32) for i in range(7)]
    psb = k.ps("psb", [128, 256], BF16)
    pbA, pbB, pbC, pbD, pbE, pbF, pbG = pb

    vsl = [k.sb(f"vsl{i}", [128, T], BF16) for i in range(8)]
    bsl = [k.sb(f"bsl{i}", [128, T], BF16) for i in range(8)]
    cstr = [k.sb(f"cst{i}", [128, 256], BF16) for i in range(2)]
    ldst = [k.sb(f"ldst{i}", [1, 256], F32) for i in range(2)]
    dsk_sb = k.sb("dsk_sb", [8, 128], F32)
    dvc = [k.sb(f"dvc{i}", [128, 1], F32) for i in range(8)]
    tril_sb = [k.sb(f"tr{i}", [128, 256], BF16) for i in range(2)]
    idf = k.sb("idf", [128, 128], F32)
    idb2 = k.sb("idb2", [128, 128], BF16)
    onesf = k.sb("onesf", [1, 128], F32)
    hst = [k.sb(f"hst{i}", [128, 128], F32) for i in range(8)]
    hbf = [k.sb(f"hbf{i}", [128, 128], BF16) for i in range(8)]
    cum = k.sb("cum", [1, 256], F32)
    ecr = k.sb("ecr", [1, 256], F32)
    zrow = k.sb("zrow", [1, 256], F32)
    cumc = [k.sb(f"cumc{i}", [128, 1], F32) for i in range(2)]
    ddt = [k.sb(f"ddt{i}", [128, 256], F32) for i in range(2)]
    lmat = [k.sb(f"lm{i}", [128, 256], BF16) for i in range(2)]
    ecol = [k.sb(f"ec{i}", [128, 1], F32) for i in range(2)]
    elast = k.sb("elast", [128, 1], F32)
    vtok = [k.sb(f"vtk{i}", [128, 128], BF16) for i in range(2)]
    btok = [k.sb(f"btk{i}", [128, 128], BF16) for i in range(2)]
    bw = [k.sb(f"bw{i}", [128, 128], BF16) for i in range(2)]
    ctsc = k.sb("ctsc", [128, 256], BF16)
    ysb = k.sb("ysb", [128, 256], F32)
    ytmp = k.sb("ytmp", [128, 256], F32)
    ybf = [k.sb(f"ybf{i}", [128, 256], BF16) for i in range(2)]
    rtmp = k.sb("rtmp", [1, 128], F32)

    ld = k.ctr("ld")
    dmv = k.ctr("dmv")
    pe = k.ctr("pe")
    ve = k.ctr("ve")
    se = k.ctr("se")
    st = k.ctr("st")
    cld = [k.ctr("cld0"), k.ctr("cld1")]

    def script(ctx):
        for i in range(8):
            ctx.dma("sync", vsl[i][:], vA[i * 128:(i + 1) * 128, :], ld)
            ctx.dma("sync", bsl[i][:], BT[i * 128:(i + 1) * 128, :], ld)
        ctx.dma("sync", dsk_sb[:], dsk, ld)
        ctx.dma("sync", idf[:], identf, ld)
        for i in range(2):
            ldv = ctx.dma("sync", tril_sb[i][:],
                          trilT[i * 128:(i + 1) * 128, :], ld)
        ctx.wait("vector", ld, ldv)
        ctx.memset(zrow[:], 0.0)
        ctx.memset(onesf[:], 1.0)
        vev = ctx.copy(idb2[:], idf[:], ctr=ve)
        for eng in ("tensor", "scalar", "gpsimd"):
            ctx.wait(eng, ve, vev)
            ctx.wait(eng, ld, ldv)
        for u in range(8):
            dv = ctx.dma("gpsimd", rtmp[0:1, :], dsk_sb[u:u + 1, :], dmv)
            ctx.wait("tensor", dmv, dv)
            ctx.wait("tensor", ve, ve.v)
            pev = ctx.trans(pbD[0:128, 0:1], rtmp[0:1, :], idf[0:1, 0:1],
                            ctr=pe)
            ctx.wait("vector", pe, pev)
            vev = ctx.copy(dvc[u][:], pbD[0:128, 0:1], ctr=ve)
            ctx.wait("gpsimd", ve, vev)
        for u in range(8):
            ctx.memset(hst[u][:], 0.0)
            vev = ctx.memset(hbf[u][:], 0.0, ctr=ve)
        ctx.wait("tensor", ve, ve.v)

        items = [(c, u) for c in range(16) for u in range(8)]
        cw = {}

        def loadC(n):
            c_, u_ = items[n]
            wl = cld[n % 2]
            ctx.dma("sync", ldst[n % 2][:],
                    ldr[u_:u_ + 1, c_ * 256:(c_ + 1) * 256], wl)
            cw[n] = (wl, ctx.dma("sync", cstr[n % 2][:],
                                 CT[u_ * 128:(u_ + 1) * 128,
                                    c_ * 256:(c_ + 1) * 256], wl))
        loadC(0)
        yb_store = {0: None, 1: None}
        for c in range(16):
            c0 = c * 256
            for u in range(8):
                nit = c * 8 + u
                if nit + 1 < len(items):
                    ctx.wait("sync", pe, pe.v)
                    loadC(nit + 1)
                csl_u = cstr[nit % 2]
                ctx.wait("tensor", cw[nit][0], cw[nit][1])
                ctx.wait("vector", cw[nit][0], cw[nit][1])
                ctx.wait("vector", pe, pe.v)
                vev = ctx.scan(cum[:], ldst[nit % 2][0:1, :], zrow[:],
                               0.0, ALU.add, ALU.add, ctr=ve)
                ctx.wait("tensor", ve, vev)
                for j in range(2):
                    pev = ctx.trans(pbD[0:128, 0:1],
                                    cum[0:1, j * 128:(j + 1) * 128],
                                    idf[0:1, 0:1], ctr=pe)
                    ctx.wait("vector", pe, pev)
                    vev = ctx.copy(cumc[j][:], pbD[0:128, 0:1], ctr=ve)
                    ctx.wait("tensor", ve, vev)
                pev = ctx.mm(pbE[:, 0:256], onesf[0:1, :], cum[0:1, :],
                             ctr=pe)
                ctx.wait("vector", pe, pev)
                for j in range(2):
                    ctx.ts(ddt[j][:], pbE[:, 0:256], cumc[j][:],
                           ALU.subtract)
                    ctx.ts(ddt[j][:], ddt[j][:], 0.0, ALU.min)
                vev = ctx.op("vector", lambda: ctx.nc.vector.tensor_copy(
                    ecol[0][:], ddt[0][:, 255:256]), ctr=ve)
                ctx.wait("scalar", ve, ve.v)
                for j in range(2):
                    sev = ctx.act(ddt[j][:], ddt[j][:], AF.Exp, ctr=se)
                ctx.wait("vector", se, sev)
                for j in range(2):
                    ctx.copy(ecol[j][:], ddt[j][:, 255:256])
                    vev = ctx.tt(lmat[j][:], ddt[j][:], tril_sb[j][:],
                                 ALU.mult, ctr=ve)
                ctx.wait("tensor", ve, vev)
                for j in range(2):
                    pev = ctx.trans(psb[:, 0:128],
                                    vsl[u][:, c0 + j * 128:c0 + (j + 1) * 128],
                                    idb2[:], ctr=pe)
                    ctx.wait("vector", pe, pev)
                    vev = ctx.copy(vtok[j][:], psb[:, 0:128], ctr=ve)
                    ctx.wait("tensor", ve, vev)
                    pev = ctx.trans(psb[:, 0:128],
                                    bsl[u][:, c0 + j * 128:c0 + (j + 1) * 128],
                                    idb2[:], ctr=pe)
                    ctx.wait("vector", pe, pev)
                    vev = ctx.copy(btok[j][:], psb[:, 0:128], ctr=ve)
                    ctx.wait("tensor", ve, vev)
                for j in range(2):
                    pev = ctx.mm(pbF[:, 0:256],
                                 bsl[u][:, c0 + j * 128:c0 + (j + 1) * 128],
                                 csl_u[:, 0:256], ctr=pe)
                    ctx.wait("vector", pe, pev)
                    vev = ctx.tt(lmat[j][:], lmat[j][:], pbF[:, 0:256],
                                 ALU.mult, ctr=ve)
                    ctx.wait("tensor", ve, vev)
                sev = ctx.act(ecr[:], cum[0:1, :], AF.Exp, ctr=se)
                ctx.wait("tensor", se, sev)
                pev = ctx.mm(pbE[:, 0:256], onesf[0:1, :], ecr[0:1, :],
                             ctr=pe)
                ctx.wait("vector", pe, pev)
                vev = ctx.tt(ctsc[:], csl_u[:, 0:256], pbE[:, 0:256],
                             ALU.mult, ctr=ve)
                ctx.wait("tensor", ve, vev)
                pev = ctx.mm(pbD[0:128, 0:1], onesf[0:1, :],
                             ecr[0:1, 255:256], ctr=pe)
                ctx.wait("vector", pe, pev)
                vev = ctx.copy(elast[:], pbD[0:128, 0:1], ctr=ve)
                ctx.wait("tensor", ve, ve.v)
                for j in range(2):
                    ctx.mm(pbF[:, 0:256], vtok[j][:], lmat[j][:],
                           start=(j == 0), stop=(j == 1))
                ctx.mm(pbG[:, 0:256], hbf[u][:], ctsc[:])
                for j in range(2):
                    vev = ctx.ts(bw[j][:], btok[j][:], ecol[j][:], ALU.mult,
                                 ctr=ve)
                ctx.wait("tensor", ve, vev)
                for j in range(2):
                    pev = ctx.mm(pbE[:, 0:128], bw[j][:], vtok[j][:],
                                 start=(j == 0), stop=(j == 1), ctr=pe)
                ctx.wait("vector", pe, pe.v)
                ctx.copy(ysb[:], pbG[:, 0:256])
                ctx.tt(ysb[:], ysb[:], pbF[:, 0:256], ALU.add)
                ctx.ts(ytmp[:], vsl[u][:, c0:c0 + 256], dvc[u][:], ALU.mult)
                ctx.tt(ysb[:], ysb[:], ytmp[:], ALU.add)
                par = (c * 8 + u) % 2
                if yb_store[par] is not None:
                    ctx.wait("vector", st, yb_store[par])
                vev = ctx.copy(ybf[par][:], ysb[:], ctr=ve)
                ctx.wait("gpsimd", ve, vev)
                sv = ctx.dma("gpsimd", o_yT[u * 128:(u + 1) * 128,
                                            c0:c0 + 256], ybf[par][:], st)
                yb_store[par] = sv
                ctx.ts(hst[u][:], hst[u][:], elast[:], ALU.mult)
                ctx.tt(hst[u][:], hst[u][:], pbE[:, 0:128], ALU.add)
                vev = ctx.copy(hbf[u][:], hst[u][:], ctr=ve)
                ctx.wait("tensor", ve, ve.v)
        ctx.wait("gpsimd", st, st.v)

    return k.finish(script)


def build_e3():
    k = Kern("e3")
    YT = k.inp("YT", [2048, TK], BF16)
    SG = k.inp("SG", [2048, TK], BF16)
    woT = k.inp("woT", [2048, 1024], BF16)
    xq = k.inp("xq", [1024, TK], F32)
    identf = k.inp("identf", [128, 128], F32)
    o_outT = k.out("o_outT", [1024, TK], BF16)
    o_otok = k.out("o_otok", [TK, 1024], BF16)
    o_x45 = k.out("o_x45", [1024, TK], F32)

    pb = [k.ps(f"pb{i}", [128, 512], F32) for i in range(4)]
    psb = [k.ps(f"psb{i}", [128, 128], BF16) for i in range(2)]
    zT = [k.sb(f"z{i}", [128, TK], BF16) for i in range(16)]
    sg_sb = [k.sb(f"sg{i}", [128, TK], BF16) for i in range(16)]
    wo_sb = [k.sb(f"wo{i}", [128, 1024], BF16) for i in range(16)]
    osb = [k.sb(f"os{i}", [128, TK], BF16) for i in range(8)]
    otk = k.sb("otk", [128, 1024], BF16)
    xres = [k.sb(f"xr{i}", [128, 512], F32) for i in range(2)]
    x45s = [k.sb(f"x45_{i}", [128, 512], F32) for i in range(2)]
    idb2 = k.sb("idb2", [128, 128], BF16)
    idf = k.sb("idf", [128, 128], F32)

    ld = k.ctr("ld")
    rld = [k.ctr("rld0"), k.ctr("rld1")]
    pe = k.ctr("pe")
    ve = k.ctr("ve")
    st = k.ctr("st")

    def script(ctx):
        ctx.dma("sync", idf[:], identf, ld)
        for i in range(16):
            ctx.dma("sync", zT[i][:], YT[i * 128:(i + 1) * 128, :], ld)
            ctx.dma("sync", sg_sb[i][:], SG[i * 128:(i + 1) * 128, :], ld)
            ldv = ctx.dma("sync", wo_sb[i][:], woT[i * 128:(i + 1) * 128, :],
                          ld)
        ctx.wait("vector", ld, ldv)
        ctx.copy(idb2[:], idf[:])
        for i in range(16):
            vev = ctx.tt(zT[i][:], zT[i][:], sg_sb[i][:], ALU.mult, ctr=ve)
        ctx.wait("tensor", ve, vev)
        st_hist = []
        n_out = 0
        for co in range(8):
            for tb in range(4):
                pp = pb[n_out % 2]
                par = n_out % 2
                if n_out >= 2:
                    ctx.wait("sync", st, st_hist[n_out - 2][0])
                rv = ctx.dma("sync", xres[par][:],
                             xq[co * 128:(co + 1) * 128,
                                tb * 512:(tb + 1) * 512], rld[par])
                ctx.wait("tensor", ve, ve.v)
                for ic in range(16):
                    pev = ctx.mm(pp[:, 0:512],
                                 wo_sb[ic][:, co * 128:(co + 1) * 128],
                                 zT[ic][:, tb * 512:(tb + 1) * 512],
                                 start=(ic == 0), stop=(ic == 15), ctr=pe)
                ctx.wait("vector", pe, pev)
                ctx.wait("vector", rld[par], rv)
                if n_out >= 2:
                    ctx.wait("vector", st, st_hist[n_out - 2][1])
                ctx.copy(osb[co][:, tb * 512:(tb + 1) * 512], pp[:, 0:512])
                vev = ctx.tt(x45s[par][:], pp[:, 0:512], xres[par][:],
                             ALU.add, ctr=ve)
                ctx.wait("gpsimd", ve, vev)
                sv1 = ctx.dma("gpsimd",
                              o_outT[co * 128:(co + 1) * 128,
                                     tb * 512:(tb + 1) * 512],
                              osb[co][:, tb * 512:(tb + 1) * 512], st)
                sv2 = ctx.dma("gpsimd",
                              o_x45[co * 128:(co + 1) * 128,
                                    tb * 512:(tb + 1) * 512], x45s[par][:],
                              st)
                st_hist.append((sv2, sv2))
                n_out += 1
        otk_store = None
        for tt_ in range(16):
            if otk_store is not None:
                ctx.wait("vector", st, otk_store)
            for co in range(8):
                pq = psb[co % 2]
                ctx.wait("tensor", ve, ve.v)
                pev = ctx.trans(pq[:, 0:128],
                                osb[co][:, tt_ * 128:(tt_ + 1) * 128],
                                idb2[:], ctr=pe)
                ctx.wait("vector", pe, pev)
                vev = ctx.copy(otk[:, co * 128:(co + 1) * 128], pq[:, 0:128],
                               ctr=ve)
            ctx.wait("gpsimd", ve, vev)
            sv = ctx.dma("gpsimd", o_otok[tt_ * 128:(tt_ + 1) * 128, :],
                         otk[:], st)
            otk_store = sv
        ctx.wait("gpsimd", st, st.v)

    return k.finish(script)


def build_e4():
    k = Kern("e4")
    outTz = k.inp("outTz", [1024, T + 1], BF16)
    otkz = k.inp("otkz", [T + 1, 1024], BF16)
    wwTh = k.inp("wwTh", [1024, 512], BF16)
    decay = k.inp("decay", [1, 1], F32)
    DTc = k.inp("DTc", [256, 256], F32)
    causT = k.inp("causT", [256, 256], BF16)
    prow = k.inp("prow", [1, 256], F32)
    qrow = k.inp("qrow", [1, 256], F32)
    identf = k.inp("identf", [128, 128], F32)
    o_reads = k.out("o_reads", [512, T], BF16)

    pb = [k.ps(f"pb{i}", [128, 512], F32) for i in range(7)]
    pbA, pbB, pbC, pbD, pbE, pbF, pbG = pb

    oz = [k.sb(f"oz{i}", [128, T + 1], BF16) for i in range(8)]
    ot = [k.sb(f"ot{i}", [128, 1024], BF16) for i in range(33)]
    ww = [k.sb(f"ww{i}", [128, 512], BF16) for i in range(8)]
    WT = [k.sb(f"WT{i}", [128, 512], F32) for i in range(8)]
    WB = [k.sb(f"WB{i}", [128, 512], BF16) for i in range(8)]
    dts = [k.sb(f"dts{i}", [128, 256], F32) for i in range(2)]
    cts = [k.sb(f"cts{i}", [128, 256], BF16) for i in range(2)]
    mmT = [k.sb(f"mmT{i}", [128, 256], BF16) for i in range(2)]
    smv = [k.sb(f"smv{i}", [128, 256], BF16) for i in range(2)]
    vtok = [k.sb(f"vq{i}", [128, 512], BF16) for i in range(2)]
    vg = [k.sb(f"vgq{i}", [128, 512], BF16) for i in range(2)]
    gpt = k.sb("gpt", [128, 256], F32)
    gwc = [k.sb(f"gwc{i}", [128, 1], F32) for i in range(2)]
    g256 = k.sb("g256", [128, 1], F32)
    lgs = k.sb("lgs", [1, 4], F32)
    growb = k.sb("growb", [1, 256], F32)
    lcol = k.sb("lcol", [128, 1], F32)
    rsb = k.sb("rsb", [128, 256], F32)
    rbf = [k.sb(f"rbf{i}", [128, 256], BF16) for i in range(2)]
    onesf = k.sb("onesf", [1, 128], F32)
    idf = k.sb("idf", [128, 128], F32)

    ld = k.ctr("ld")
    pe = k.ctr("pe")
    ve = k.ctr("ve")
    se = k.ctr("se")
    st = k.ctr("st")

    def script(ctx):
        for i in range(8):
            ctx.dma("sync", oz[i][:], outTz[i * 128:(i + 1) * 128, :], ld)
        for i in range(33):
            r0 = i * 128
            nr = min(128, T + 1 - r0)
            ctx.dma("sync", ot[i][0:nr, :], otkz[r0:r0 + nr, :], ld)
        for i in range(8):
            ctx.dma("sync", ww[i][:], wwTh[i * 128:(i + 1) * 128, :], ld)
        ctx.dma("sync", lgs[0:1, 0:1], decay, ld)
        ctx.dma("sync", dts[0][:], DTc[0:128, :], ld)
        ctx.dma("sync", dts[1][:], DTc[128:256, :], ld)
        ctx.dma("sync", cts[0][:], causT[0:128, :], ld)
        ctx.dma("sync", cts[1][:], causT[128:256, :], ld)
        ctx.dma("sync", growb[:], qrow, ld)
        ldv = ctx.dma("sync", idf[:], identf, ld)
        ctx.wait("scalar", ld, ldv)
        ctx.wait("vector", ld, ldv)
        ctx.memset(onesf[:], 1.0)
        for i in range(8):
            vev = ctx.memset(WT[i][:], 0.0, ctr=ve)
            vev = ctx.memset(WB[i][:], 0.0, ctr=ve)
        sev = ctx.act(lgs[0:1, 0:1], lgs[0:1, 0:1], AF.Sigmoid, ctr=se)
        sev = ctx.act(lgs[0:1, 1:2], lgs[0:1, 0:1], AF.Ln, ctr=se)
        ctx.wait("vector", se, sev)
        vev = ctx.ts(lgs[0:1, 2:3], lgs[0:1, 1:2], 256.0, ALU.mult, ctr=ve)
        ctx.wait("scalar", ve, vev)
        sev = ctx.act(lgs[0:1, 2:3], lgs[0:1, 2:3], AF.Exp, ctr=se)
        ctx.wait("tensor", se, sev)
        ctx.wait("tensor", ve, ve.v)
        pev = ctx.mm(pbD[0:128, 0:1], onesf[0:1, :], lgs[0:1, 1:2], ctr=pe)
        ctx.wait("vector", pe, pev)
        vev = ctx.copy(lcol[:], pbD[0:128, 0:1], ctr=ve)
        ctx.wait("tensor", ve, vev)
        pev = ctx.mm(pbD[0:128, 0:1], onesf[0:1, :], lgs[0:1, 2:3], ctr=pe)
        ctx.wait("vector", pe, pev)
        vev = ctx.copy(g256[:], pbD[0:128, 0:1], ctr=ve)
        vev = ctx.ts(growb[:], growb[:], lgs[0:1, 1:2], ALU.mult, ctr=ve)
        ctx.wait("scalar", ve, vev)
        sev = ctx.act(growb[:], growb[:], AF.Exp, ctr=se)
        ctx.wait("tensor", se, sev)
        for j in range(2):
            pev = ctx.trans(pbE[0:128, 0:1], growb[0:1, j * 128:(j + 1) * 128],
                            idf[0:1, 0:1], ctr=pe)
            ctx.wait("vector", pe, pev)
            vev = ctx.copy(gwc[j][:], pbE[0:128, 0:1], ctr=ve)
            ctx.wait("tensor", ve, vev)
        ctx.wait("sync", pe, pe.v)
        dv = ctx.dma("sync", growb[:], prow, ld)
        ctx.wait("vector", ld, dv)
        vev = ctx.ts(growb[:], growb[:], lgs[0:1, 1:2], ALU.mult, ctr=ve)
        ctx.wait("scalar", ve, vev)
        sev = ctx.act(growb[:], growb[:], AF.Exp, ctr=se)
        ctx.wait("tensor", se, sev)
        pev = ctx.mm(pbD[:, 0:256], onesf[0:1, :], growb[0:1, :], ctr=pe)
        ctx.wait("vector", pe, pev)
        vev = ctx.copy(gpt[:], pbD[:, 0:256], ctr=ve)
        for j in range(2):
            vev = ctx.ts(dts[j][:], dts[j][:], lcol[:], ALU.mult, ctr=ve)
        ctx.wait("scalar", ve, vev)
        for j in range(2):
            sev = ctx.act(dts[j][:], dts[j][:], AF.Exp, ctr=se)
        ctx.wait("vector", se, sev)
        for j in range(2):
            vev = ctx.tt(mmT[j][:], dts[j][:], cts[j][:], ALU.mult, ctr=ve)
        ctx.wait("tensor", ve, ve.v)

        rb_store = {0: None, 1: None}
        for c in range(16):
            c0 = c * 256
            for tt_ in range(2):
                pp = pbA if tt_ == 0 else pbB
                for ci in range(8):
                    pev = ctx.mm(pp[:, 0:512],
                                 oz[ci][:, 1 + c0 + tt_ * 128:
                                        1 + c0 + tt_ * 128 + 128],
                                 ww[ci][:], start=(ci == 0), stop=(ci == 7),
                                 ctr=pe)
            ctx.wait("vector", pe, pev)
            for tt_ in range(2):
                pp = pbA if tt_ == 0 else pbB
                ctx.copy(vtok[tt_][:], pp[:, 0:512])
                vev = ctx.ts(vg[tt_][:], pp[:, 0:512], gwc[tt_][:], ALU.mult,
                             ctr=ve)
            ctx.wait("tensor", ve, vev)
            for j in range(2):
                pp = pbC if j == 0 else pbD
                for ci in range(8):
                    pev = ctx.mm(pp[:, 0:256],
                                 oz[ci][:, c0 + j * 128:c0 + j * 128 + 128],
                                 oz[ci][:, 1 + c0:1 + c0 + 256],
                                 start=(ci == 0), stop=(ci == 7), ctr=pe)
            ctx.wait("vector", pe, pev)
            for j in range(2):
                pp = pbC if j == 0 else pbD
                vev = ctx.tt(smv[j][:], pp[:, 0:256], mmT[j][:], ALU.mult,
                             ctr=ve)
            ctx.wait("tensor", ve, vev)
            for ds in range(4):
                for j in range(2):
                    ctx.mm(pbE[:, 0:256], vtok[j][:, ds * 128:(ds + 1) * 128],
                           smv[j][:], start=(j == 0), stop=(j == 1))
                for ci in range(8):
                    pev = ctx.mm(pbF[:, 0:256],
                                 WB[ci][:, ds * 128:(ds + 1) * 128],
                                 oz[ci][:, 1 + c0:1 + c0 + 256],
                                 start=(ci == 0), stop=(ci == 7), ctr=pe)
                ctx.wait("vector", pe, pev)
                par = (c * 4 + ds) % 2
                ctx.tt(rsb[:], pbF[:, 0:256], gpt[:], ALU.mult)
                vev = ctx.tt(rsb[:], rsb[:], pbE[:, 0:256], ALU.add, ctr=ve)
                if rb_store[par] is not None:
                    ctx.wait("vector", st, rb_store[par])
                vev = ctx.copy(rbf[par][:], rsb[:], ctr=ve)
                ctx.wait("gpsimd", ve, vev)
                sv = ctx.dma("gpsimd",
                             o_reads[ds * 128:(ds + 1) * 128, c0:c0 + 256],
                             rbf[par][:], st)
                rb_store[par] = sv
                ctx.wait("tensor", ve, ve.v)
            for ci in range(8):
                pp = pbE if ci % 2 == 0 else pbF
                ctx.wait("tensor", ve, ve.v)
                for tt_ in range(2):
                    pev = ctx.mm(pp[:, 0:512],
                                 ot[2 * c + tt_][:, ci * 128:(ci + 1) * 128],
                                 vg[tt_][:], start=(tt_ == 0),
                                 stop=(tt_ == 1), ctr=pe)
                ctx.wait("vector", pe, pev)
                ctx.stt(WT[ci][:], WT[ci][:], g256[:], pp[:, 0:512],
                        ALU.mult, ALU.add)
                vev = ctx.copy(WB[ci][:], WT[ci][:], ctr=ve)
        ctx.wait("gpsimd", st, st.v)

    return k.finish(script)


def build_e5():
    k = Kern("e5")
    rT = k.inp("rT", [1024, TK], BF16)
    wrT = k.inp("wrT", [1024, 1024], BF16)
    x45 = k.inp("x45", [1024, TK], F32)
    oxT = k.out("oxT", [1024, TK], F32)

    pb = [k.ps(f"pb{i}", [128, 512], F32) for i in range(2)]
    r_sb = [k.sb(f"r{i}", [128, TK], BF16) for i in range(8)]
    wr_sb = [k.sb(f"wr{i}", [128, 1024], BF16) for i in range(8)]
    xres = [k.sb(f"xr{i}", [128, 512], F32) for i in range(2)]
    oxs = [k.sb(f"oxsb{i}", [128, 512], F32) for i in range(2)]

    ld = k.ctr("ld")
    rld = [k.ctr("rld0"), k.ctr("rld1")]
    pe = k.ctr("pe")
    ve = k.ctr("ve")
    st = k.ctr("st")

    def script(ctx):
        for i in range(8):
            ctx.dma("sync", r_sb[i][:], rT[i * 128:(i + 1) * 128, :], ld)
            ldv = ctx.dma("sync", wr_sb[i][:], wrT[i * 128:(i + 1) * 128, :],
                          ld)
        ctx.wait("tensor", ld, ldv)
        st_hist = []
        n_out = 0
        for co in range(8):
            for tb in range(4):
                pp = pb[n_out % 2]
                par = n_out % 2
                if n_out >= 2:
                    ctx.wait("sync", st, st_hist[n_out - 2])
                rv = ctx.dma("sync", xres[par][:],
                             x45[co * 128:(co + 1) * 128,
                                 tb * 512:(tb + 1) * 512], rld[par])
                ctx.wait("tensor", ve, ve.v)
                for ci in range(8):
                    pev = ctx.mm(pp[:, 0:512],
                                 wr_sb[ci][:, co * 128:(co + 1) * 128],
                                 r_sb[ci][:, tb * 512:(tb + 1) * 512],
                                 start=(ci == 0), stop=(ci == 7), ctr=pe)
                ctx.wait("vector", pe, pev)
                ctx.wait("vector", rld[par], rv)
                if n_out >= 2:
                    ctx.wait("vector", st, st_hist[n_out - 2])
                vev = ctx.stt(oxs[par][:], pp[:, 0:512], 0.5, xres[par][:],
                              ALU.mult, ALU.add, ctr=ve)
                ctx.wait("gpsimd", ve, vev)
                sv = ctx.dma("gpsimd",
                             oxT[co * 128:(co + 1) * 128,
                                 tb * 512:(tb + 1) * 512], oxs[par][:], st)
                st_hist.append(sv)
                n_out += 1
        ctx.wait("gpsimd", st, st.v)

    return k.finish(script)


def run_estage(xT_full, W, raw):
    identf = np.eye(128, dtype=np.float32)
    slabs = []
    for pc in range(65):
        blk = W["w_inT"][:, pc * 128:min((pc + 1) * 128, 8208)]
        pw = blk.shape[1]
        sl = blk.reshape(8, 128, pw).transpose(1, 0, 2).reshape(128, 8 * pw)
        if pw < 128:
            sl = np.concatenate(
                [sl.reshape(128, 8, pw),
                 np.zeros((128, 8, 128 - pw), dtype=sl.dtype)],
                axis=2).reshape(128, 1024)
        slabs.append(sl)
    winS = c_(np.concatenate(slabs, axis=0))
    nwc = c_(raw["L1_attn_a_norm_w"].astype(np.float32).reshape(1024, 1))
    convw = c_(raw["L1_attn_a_conv_w"].astype(np.float32))
    convb = c_(raw["L1_attn_a_conv_b"].astype(np.float32).reshape(2048, 1))
    alog = c_(raw["L1_attn_a_A_log"].astype(np.float32).reshape(16, 1))
    dtb = c_(raw["L1_attn_a_dt_bias"].astype(np.float32).reshape(16, 1))
    in_maps = []
    for c in range(NCORE):
        b, h = c // 2, c % 2
        t0 = h * TK
        xh = np.zeros((1024, 2051), dtype=np.float32)
        xh[:, 3:] = xT_full[b][:, t0:t0 + TK]
        if t0 >= 3:
            xh[:, 0:3] = xT_full[b][:, t0 - 3:t0]
        in_maps.append(dict(xh=c_(xh), winS=winS, nwc=nwc, convw=convw,
                            convb=convb, alog=alog, dtb=dtb))
    res1 = runk("e1", build_e1, in_maps)
    va = np.empty((B, 2048, T), dtype=BF)
    sg = np.empty((B, 2048, T), dtype=BF)
    bc = np.empty((B, 4096, T), dtype=BF)
    ldv = np.empty((B, 16, T), dtype=np.float32)
    xa = np.empty((B, 1024, T), dtype=np.float32)
    for c in range(NCORE):
        b, h = c // 2, c % 2
        sl = slice(h * TK, (h + 1) * TK)
        va[b][:, sl] = res1[c]["o_va"]
        sg[b][:, sl] = res1[c]["o_sg"]
        bc[b][:, sl] = res1[c]["o_bc"]
        ldv[b][:, sl] = res1[c]["o_ld"]
        xa[b][:, sl] = res1[c]["o_xa"]

    tril = (np.arange(256)[None, :] >= np.arange(256)[:, None]).astype(BF)
    in_maps = []
    for c in range(NCORE):
        b, hh = c // 2, c % 2
        hsl = slice(hh * 8 * 128, (hh + 1) * 8 * 128)
        in_maps.append(dict(
            vA=c_(va[b][hsl]), BT=c_(bc[b][hsl]),
            CT=c_(bc[b][2048 + hh * 8 * 128: 2048 + (hh + 1) * 8 * 128]),
            ldr=c_(ldv[b][hh * 8:(hh + 1) * 8]),
            dsk=c_(raw["L1_attn_a_D_skip"].astype(np.float32)
                   .reshape(16, 128)[hh * 8:(hh + 1) * 8]),
            identf=identf, trilT=c_(tril)))
    res2 = runk("e2", build_e2, in_maps)
    yT = np.empty((B, 2048, T), dtype=BF)
    for c in range(NCORE):
        b, hh = c // 2, c % 2
        yT[b][hh * 8 * 128:(hh + 1) * 8 * 128] = res2[c]["o_yT"]

    in_maps = []
    for c in range(NCORE):
        b, h = c // 2, c % 2
        sl = slice(h * TK, (h + 1) * TK)
        in_maps.append(dict(YT=c_(yT[b][:, sl]), SG=c_(sg[b][:, sl]),
                            woT=W["w_outT"], xq=c_(xa[b][:, sl]),
                            identf=identf))
    res3 = runk("e3", build_e3, in_maps)
    outT = np.empty((B, 1024, T), dtype=BF)
    otok = np.empty((B, T, 1024), dtype=BF)
    x45 = np.empty((B, 1024, T), dtype=np.float32)
    for c in range(NCORE):
        b, h = c // 2, c % 2
        sl = slice(h * TK, (h + 1) * TK)
        outT[b][:, sl] = res3[c]["o_outT"]
        otok[b][sl] = res3[c]["o_otok"]
        x45[b][:, sl] = res3[c]["o_x45"]

    i_ = np.arange(256)
    DTc = c_(np.maximum(i_[None, :] - 1 - i_[:, None], 0).astype(np.float32))
    causT = c_((i_[None, :] > i_[:, None]).astype(BF))
    prow = c_(i_.astype(np.float32).reshape(1, 256))
    qrow = c_((255.0 - i_).astype(np.float32).reshape(1, 256))
    decay = c_(np.asarray(raw["L1_attn_a_decay"], np.float32).reshape(1, 1))
    in_maps = []
    for c in range(NCORE):
        b, dh = c // 2, c % 2
        outTz = np.zeros((1024, T + 1), dtype=BF)
        outTz[:, 1:] = outT[b]
        otkz = np.zeros((T + 1, 1024), dtype=BF)
        otkz[1:] = otok[b]
        in_maps.append(dict(
            outTz=c_(outTz), otkz=c_(otkz),
            wwTh=c_(W["w_writeT"][:, dh * 512:(dh + 1) * 512]),
            decay=decay, DTc=DTc, causT=causT, prow=prow, qrow=qrow,
            identf=identf))
    res4 = runk("e4", build_e4, in_maps)
    readsT = np.empty((B, 1024, T), dtype=BF)
    for c in range(NCORE):
        b, dh = c // 2, c % 2
        readsT[b][dh * 512:(dh + 1) * 512] = res4[c]["o_reads"]

    in_maps = []
    for c in range(NCORE):
        b, h = c // 2, c % 2
        sl = slice(h * TK, (h + 1) * TK)
        in_maps.append(dict(rT=c_(readsT[b][:, sl]), wrT=W["w_readT"],
                            x45=c_(x45[b][:, sl])))
    res5 = runk("e5", build_e5, in_maps)
    out = np.empty_like(xT_full)
    for c in range(NCORE):
        b, h = c // 2, c % 2
        out[b][:, h * TK:(h + 1) * TK] = res5[c]["oxT"]
    return out



def kernel(**inputs):
    inputs = {k: np.asarray(v) for k, v in inputs.items()}
    W = run_w0(inputs)
    x = np.asarray(inputs["L0_attn_a_x"], np.float32)
    xT = toT(x)
    xT = run_attn_a(xT, W)
    xT = run_gffn(xT, W, inputs)
    xT = run_attn_b0(xT, W)
    xT = run_gffn(xT, W, inputs)
    xT = run_estage(xT, W, inputs)
    xT = run_moe(xT, W, inputs)
    xT = run_dattn(xT, W, inputs)
    xT = run_moe(xT, W, inputs)
    return np.ascontiguousarray(xT.transpose(0, 2, 1))


# version 02
# speedup vs baseline: 1.6441x; 1.6441x over previous
import numpy as np
import ml_dtypes
from contextlib import ExitStack
import concourse.bass as bass
import concourse.mybir as mybir
from concourse.bass_utils import run_bass_kernel_spmd

F32 = mybir.dt.float32
BF16 = mybir.dt.bfloat16
AF = mybir.ActivationFunctionType
ALU = mybir.AluOpType
AX = mybir.AxisListType
BF = ml_dtypes.bfloat16

NCORE = 8
import os as _os
DETECT_RACES = _os.environ.get("DETECT_RACES", "0") == "1"
AUTOFENCE = _os.environ.get("AUTOFENCE", "1") == "1"
FENCE_ENGINES = tuple(filter(None, _os.environ.get("FENCE", "").split(",")))
_VSG = ("vector", "scalar", "gpsimd")
STAGE_FENCE = {"gffn": _VSG, "dattn": _VSG, "moe": _VSG,
               "e1": _VSG, "e2": _VSG, "e3": _VSG, "e4": _VSG, "e5": _VSG}
B, T, D = 4, 4096, 1024
TK = 2048
ENGINES = ("sync", "vector", "scalar", "tensor", "gpsimd")


def cdiv(a, b):
    return (a + b - 1) // b


def toT(x):
    return np.ascontiguousarray(np.swapaxes(np.asarray(x), -1, -2))


def c_(x):
    return np.ascontiguousarray(x)



class Ctr:
    __slots__ = ("sem", "v", "name")

    def __init__(self, sem, name):
        self.sem = sem
        self.v = 0
        self.name = name


class Kern:

    def __init__(self, name):
        self.nc = bass.Bass("TRN2", target_bir_lowering=False,
                            detect_race_conditions=DETECT_RACES)
        self.stack = ExitStack()
        self.ctrs = []
        self.name = name

    def inp(self, name, shape, dt=F32):
        return self.nc.dram_tensor(name, shape, dt, kind="ExternalInput").ap()

    def out(self, name, shape, dt=F32):
        return self.nc.dram_tensor(name, shape, dt, kind="ExternalOutput").ap()

    def sb(self, name, shape, dt=F32):
        return self.stack.enter_context(self.nc.sbuf_tensor(name, shape, dt))

    def ps(self, name, shape, dt=F32):
        return self.stack.enter_context(self.nc.psum_tensor(name, shape, dt))

    def ctr(self, name):
        c = Ctr(self.stack.enter_context(self.nc.semaphore(name)), name)
        self.ctrs.append(c)
        return c

    def finish(self, script):
        block = self.stack.enter_context(self.nc.Block())
        ctx = Ctx(self)
        for eng in ENGINES:
            ctx.active = eng
            ctx.engine = getattr(self.nc, eng)
            ctx.last = {}
            for c in self.ctrs:
                c.v = 0
            deco = getattr(block, eng)

            def body(E, ctx=ctx, script=script):
                script(ctx)
            deco(body)
        self.stack.close()
        return self.nc


class Ctx:
    def __init__(self, k):
        self.k = k
        self.nc = k.nc
        self.active = None
        self.engine = None
        self.fence_engines = tuple(FENCE_ENGINES)
        self.auto = {e: k.ctr("auto_" + e) for e in self.fence_engines}
        self.last = {}

    def op(self, eng, thunk, ctr=None, n=1):
        fence = eng in self.fence_engines
        if fence and eng in self.last:
            ls, lv = self.last[eng]
            if self.active == eng:
                self.engine.wait_ge(ls, lv)
        if ctr is None and fence:
            ctr, n = self.auto[eng], 1
        if self.active == eng:
            instr = thunk()
            if ctr is not None:
                instr.then_inc(ctr.sem, n)
        if ctr is not None:
            ctr.v += n
            if fence:
                self.last[eng] = (ctr.sem, ctr.v)
            return ctr.v
        return None

    def dma(self, eng, out, in_, ctr):
        E = getattr(self.nc, eng)
        return self.op(eng, lambda: E.dma_start(out=out, in_=in_), ctr=ctr,
                       n=16)

    def wait(self, eng, ctr, v=None):
        v = ctr.v if v is None else v
        if self.active == eng and v > 0:
            self.engine.wait_ge(ctr.sem, v)

    def mm(self, psum, lhsT, rhs, start=True, stop=True, ctr=None):
        return self.op("tensor", lambda: self.nc.tensor.matmul(
            psum, lhsT, rhs, start=start, stop=stop), ctr=ctr)

    def trans(self, psum, in_, ident, ctr=None):
        return self.op("tensor", lambda: self.nc.tensor.transpose(
            psum, in_, ident), ctr=ctr)

    def act(self, out, in_, func, bias=None, scale=None, accum_out=None,
            ctr=None):
        kw = {}
        if bias is not None:
            kw["bias"] = bias
        if scale is not None:
            kw["scale"] = scale
        if accum_out is not None:
            kw["accum_out"] = accum_out
        return self.op("scalar", lambda: self.nc.scalar.activation(
            out=out, in_=in_, func=func, **kw), ctr=ctr)

    def tt(self, out, in0, in1, op, ctr=None, eng="vector"):
        E = getattr(self.nc, eng)
        return self.op(eng, lambda: E.tensor_tensor(
            out=out, in0=in0, in1=in1, op=op), ctr=ctr)

    def ts(self, out, in0, s1, op0, s2=None, op1=None, ctr=None, eng="vector"):
        E = getattr(self.nc, eng)
        if op1 is None:
            return self.op(eng, lambda: E.tensor_scalar(
                out=out, in0=in0, scalar1=s1, scalar2=None, op0=op0), ctr=ctr)
        return self.op(eng, lambda: E.tensor_scalar(
            out=out, in0=in0, scalar1=s1, scalar2=s2, op0=op0, op1=op1),
            ctr=ctr)

    def stt(self, out, in0, scalar, in1, op0, op1, ctr=None, eng="vector"):
        E = getattr(self.nc, eng)
        return self.op(eng, lambda: E.scalar_tensor_tensor(
            out=out, in0=in0, scalar=scalar, in1=in1, op0=op0, op1=op1),
            ctr=ctr)

    def copy(self, out, in_, ctr=None, eng="vector"):
        E = getattr(self.nc, eng)
        return self.op(eng, lambda: E.tensor_copy(out, in_), ctr=ctr)

    def memset(self, ap, v, ctr=None, eng="vector"):
        E = getattr(self.nc, eng)
        return self.op(eng, lambda: E.memset(ap, v), ctr=ctr)

    def red(self, out, in_, op, ctr=None, negate=None):
        kw = {}
        if negate is not None:
            kw["negate"] = negate
        return self.op("vector", lambda: self.nc.vector.tensor_reduce(
            out, in_, axis=AX.X, op=op, **kw), ctr=ctr)

    def recip(self, out, in_, ctr=None):
        return self.op("vector", lambda: self.nc.vector.reciprocal(out, in_),
                       ctr=ctr)

    def scan(self, out, data0, data1, initial, op0, op1, ctr=None):
        return self.op("vector", lambda: self.nc.vector.tensor_tensor_scan(
            out=out, data0=data0, data1=data1, initial=initial, op0=op0,
            op1=op1), ctr=ctr)


_BUILT = {}


def built(name, builder):
    global FENCE_ENGINES
    if name not in _BUILT:
        saved = FENCE_ENGINES
        FENCE_ENGINES = next(
            (v for k2, v in STAGE_FENCE.items() if name.startswith(k2)),
            saved)
        try:
            _BUILT[name] = builder()
        finally:
            FENCE_ENGINES = saved
    return _BUILT[name]


def run8(nc, in_maps):
    return run_bass_kernel_spmd(nc, in_maps, core_ids=list(range(NCORE)))


RUNNER = None


def runk(name, build, in_maps):
    if RUNNER is not None:
        return RUNNER(name, build, in_maps)
    return run8(built(name, build), in_maps).results



def w0_shapes():
    return {
        "upT": (8192, 2048), "gateT": (8192, 2048), "downT": (16384, 1024),
        "w_inT": (1024, 8208), "w_outT": (2048, 1024),
        "w_writeT": (1024, 1024), "w_readT": (1024, 1024),
        "g_WqT": (1024, 1024), "g_WkT": (1024, 1024), "g_WvT": (1024, 1024),
        "g_W1T": (1024, 1024), "g_W2T": (1024, 1024),
        "a_Wc_down": (1024, 512), "a_Wq_up": (512, 1024),
        "a_Wkv_up": (512, 128), "a_Wc_proj": (1024, 1024),
        "b_U": (1024, 32),
        "f_Wq": (1024, 1024), "f_Wk": (1024, 1024), "f_Wv": (1024, 1024),
        "f_Wo": (1024, 1024), "f_Ws1": (1024, 256),
        "wupsT": (1024, 2048), "wgatesT": (1024, 2048),
        "wdownsT": (2048, 1024),
    }


def w0_sharded(rows):
    return rows % 1024 == 0


def build_w0():
    sh = w0_shapes()
    k = Kern("w0")
    ios = {}
    for name, (R, C) in sh.items():
        rows = R // 8 if w0_sharded(R) else R
        ios[name] = (k.inp("i_" + name, [rows, C], F32),
                     k.out("o_" + name, [rows, C], BF16), rows, C)
    tin = [k.sb(f"tin{j}", [128, 8208], F32) for j in range(2)]
    tout = [k.sb(f"tout{j}", [128, 8208], BF16) for j in range(2)]
    ld = k.ctr("ld")
    cv = k.ctr("cv")
    st = k.ctr("st")

    def script(ctx):
        hist = []
        n = 0
        for name, (i, o, rows, C) in ios.items():
            for r0 in range(0, rows, 128):
                nr = min(128, rows - r0)
                j = n % 2
                if n >= 2:
                    ctx.wait("sync", st, hist[n - 2][1])
                lv = ctx.dma("sync", tin[j][0:nr, 0:C], i[r0:r0 + nr, :], ld)
                ctx.wait("vector", ld, lv)
                cvv = ctx.copy(tout[j][0:nr, 0:C], tin[j][0:nr, 0:C], ctr=cv)
                ctx.wait("gpsimd", cv, cvv)
                sv = ctx.dma("gpsimd", o[r0:r0 + nr, :], tout[j][0:nr, 0:C], st)
                hist.append((lv, sv))
                n += 1
        ctx.wait("gpsimd", st, st.v)
    return k.finish(script)


def run_w0(inputs):
    arrs = {
        "upT": toT(inputs["L1_ffn_up"]).reshape(8192, 2048),
        "gateT": toT(inputs["L1_ffn_gate"]).reshape(8192, 2048),
        "downT": toT(inputs["L1_ffn_down"]).reshape(16384, 1024),
        "w_inT": inputs["L1_attn_a_w_in"].T,
        "w_outT": inputs["L1_attn_a_w_out"].T,
        "w_writeT": inputs["L1_attn_a_w_write"].T,
        "w_readT": inputs["L1_attn_a_w_read"].T,
        "g_WqT": inputs["L1_attn_b_Wq"].T,
        "g_WkT": inputs["L1_attn_b_Wk"].T,
        "g_WvT": inputs["L1_attn_b_Wv"].T,
        "g_W1T": inputs["L1_attn_b_W1"].T,
        "g_W2T": inputs["L1_attn_b_W2"].T,
        "a_Wc_down": inputs["L0_attn_a_Wc_down"],
        "a_Wq_up": inputs["L0_attn_a_Wq_up"],
        "a_Wkv_up": inputs["L0_attn_a_Wkv_up"],
        "a_Wc_proj": inputs["L0_attn_a_Wc_proj"],
        "b_U": inputs["L0_attn_b_U"],
        "f_Wq": inputs["L0_ffn_Wq"], "f_Wk": inputs["L0_ffn_Wk"],
        "f_Wv": inputs["L0_ffn_Wv"], "f_Wo": inputs["L0_ffn_Wo"],
        "f_Ws1": inputs["L0_ffn_Ws1"],
        "wupsT": inputs["L1_ffn_w_up_s"].T,
        "wgatesT": inputs["L1_ffn_w_gate_s"].T,
        "wdownsT": inputs["L1_ffn_w_down_s"].T,
    }
    arrs = {n: c_(np.asarray(a, np.float32)) for n, a in arrs.items()}
    nc = built("w0", build_w0)
    in_maps = []
    for c in range(NCORE):
        m = {}
        for name, a in arrs.items():
            R = a.shape[0]
            if w0_sharded(R):
                rows = R // 8
                m["i_" + name] = c_(a[c * rows:(c + 1) * rows])
            else:
                m["i_" + name] = a
        in_maps.append(m)
    res = run8(nc, in_maps)
    W = {}
    for name, a in arrs.items():
        R = a.shape[0]
        if w0_sharded(R):
            W[name] = np.concatenate(
                [res.results[c]["o_" + name] for c in range(NCORE)], axis=0)
        else:
            W[name] = res.results[0]["o_" + name]
    return W



class LNBlock:

    def __init__(self, k, t_total, tag="", ps3=None, keep_s=False):
        self.t_total = t_total
        self.ones128f = k.sb("ln_ones" + tag, [128, 128], F32)
        self.sq = k.sb("ln_sq" + tag, [128, 512], F32)
        self.row = k.sb("ln_row" + tag, [1, (6 if keep_s else 5) * 512], F32)
        if ps3 is None:
            self.ps_stat = k.ps("ln_ps" + tag, [128, 512], F32)
            self.ps_b1 = k.ps("ln_pb1" + tag, [128, 512], F32)
            self.ps_b2 = k.ps("ln_pb2" + tag, [128, 512], F32)
        else:
            self.ps_stat, self.ps_b1, self.ps_b2 = ps3
        self.sb_s = k.sb("ln_sbs" + tag, [128, 512], F32)
        self.tmp = k.sb("ln_tmp" + tag, [128, 512], F32)
        self.ones_col = k.sb("ln_onec" + tag, [128, 1], F32)
        self.eps1 = k.sb("ln_eps1" + tag, [2, 1], F32)
        self.eps2 = k.sb("ln_eps2" + tag, [2, 1], F32)
        self.pe = k.ctr("ln_pe" + tag)
        self.ve = k.ctr("ln_ve" + tag)
        self.se = k.ctr("ln_se" + tag)

    def init(self, ctx, ini, rms_eps=None):
        ctx.memset(self.ones128f[:], 1.0)
        ctx.memset(self.eps1[:], 1e-5)
        if rms_eps is not None:
            ctx.memset(self.eps2[:], rms_eps)
        return ctx.memset(self.ones_col[:], 1.0, ctr=ini)


def emit_ln(ctx, ln, x_at, xn_at, wcol=None, rms_eps=None,
            pre_block=None, post_block=None, keep_s=False):
    t_total = ln.t_total
    r = ln.row
    TT = ln.t_total
    NB = cdiv(t_total, 512)
    prev_rowmath = 0
    for tb in range(NB):
        t0 = tb * 512
        tw = min(512, t_total - t0)
        if pre_block is not None:
            pre_block(ctx, tb, t0, tw)
        for ci in range(8):
            ctx.wait("scalar", ln.pe, ln.pe.v)
            sev = ctx.act(ln.sq[:, 0:tw], x_at(ci, t0, tw), AF.Square,
                          ctr=ln.se)
            ctx.wait("tensor", ln.se, sev)
            if ci == 0:
                ctx.wait("tensor", ln.se, prev_rowmath)
            ctx.mm(ln.ps_stat[0:1, 0:tw], ln.ones128f[:, 0:1],
                   x_at(ci, t0, tw), start=(ci == 0), stop=(ci == 7))
            ctx.mm(ln.ps_stat[32:33, 0:tw], ln.ones128f[:, 0:1],
                   ln.sq[:, 0:tw], start=(ci == 0), stop=(ci == 7), ctr=ln.pe)
        ctx.wait("scalar", ln.pe, ln.pe.v)
        ctx.act(r[0:1, 0 * 512:0 * 512 + tw], ln.ps_stat[0:1, 0:tw], AF.Copy,
                scale=1.0 / 1024)
        sev = ctx.act(r[0:1, 1 * 512:1 * 512 + tw], ln.ps_stat[32:33, 0:tw], AF.Copy,
                      scale=1.0 / 1024, ctr=ln.se)
        prev_rowmath = sev
        ctx.wait("vector", ln.se, sev)
        ctx.tt(r[0:1, 4 * 512:4 * 512 + tw], r[0:1, 0 * 512:0 * 512 + tw], r[0:1, 0 * 512:0 * 512 + tw],
               ALU.mult)
        vev = ctx.tt(r[0:1, 1 * 512:1 * 512 + tw], r[0:1, 1 * 512:1 * 512 + tw],
                     r[0:1, 4 * 512:4 * 512 + tw], ALU.subtract, ctr=ln.ve)
        ctx.wait("scalar", ln.ve, vev)
        sev = ctx.act(r[0:1, 2 * 512:2 * 512 + tw], r[0:1, 1 * 512:1 * 512 + tw], AF.Sqrt,
                      bias=ln.eps1[0:1, :], ctr=ln.se)
        ctx.wait("vector", ln.se, sev)
        ctx.recip(r[0:1, 2 * 512:2 * 512 + tw], r[0:1, 2 * 512:2 * 512 + tw])
        if keep_s:
            ctx.copy(r[0:1, 5 * 512:5 * 512 + tw], r[0:1, 2 * 512:2 * 512 + tw])
        if rms_eps is not None:
            ctx.tt(r[0:1, 4 * 512:4 * 512 + tw], r[0:1, 2 * 512:2 * 512 + tw], r[0:1, 2 * 512:2 * 512 + tw],
                   ALU.mult)
            vev = ctx.tt(r[0:1, 4 * 512:4 * 512 + tw], r[0:1, 4 * 512:4 * 512 + tw],
                         r[0:1, 1 * 512:1 * 512 + tw], ALU.mult, ctr=ln.ve)
            ctx.wait("scalar", ln.ve, vev)
            sev = ctx.act(r[0:1, 4 * 512:4 * 512 + tw], r[0:1, 4 * 512:4 * 512 + tw], AF.Sqrt,
                          bias=ln.eps2[0:1, :], ctr=ln.se)
            ctx.wait("vector", ln.se, sev)
            ctx.recip(r[0:1, 4 * 512:4 * 512 + tw], r[0:1, 4 * 512:4 * 512 + tw])
            ctx.tt(r[0:1, 2 * 512:2 * 512 + tw], r[0:1, 2 * 512:2 * 512 + tw],
                   r[0:1, 4 * 512:4 * 512 + tw], ALU.mult)
        vev = ctx.ts(r[0:1, 3 * 512:3 * 512 + tw], r[0:1, 0 * 512:0 * 512 + tw],
                     1.0, ALU.mult, ctr=ln.ve)
        ctx.wait("tensor", ln.ve, vev)
        ctx.mm(ln.ps_b1[:, 0:tw], ln.ones128f[0:1, :], r[0:1, 3 * 512:3 * 512 + tw])
        pev = ctx.mm(ln.ps_b2[:, 0:tw], ln.ones128f[0:1, :],
                     r[0:1, 2 * 512:2 * 512 + tw], ctr=ln.pe)
        ctx.wait("vector", ln.pe, pev)
        ctx.copy(ln.sb_s[:, 0:tw], ln.ps_b2[:, 0:tw])
        for ci in range(8):
            wc = wcol[ci] if wcol is not None else ln.ones_col
            ctx.tt(ln.tmp[:, 0:tw], x_at(ci, t0, tw),
                   ln.ps_b1[:, 0:tw], ALU.subtract)
            ctx.stt(xn_at(ci, t0, tw), ln.tmp[:, 0:tw], wc[:],
                    ln.sb_s[:, 0:tw], ALU.mult, ALU.mult, ctr=ln.ve)
        if post_block is not None:
            post_block(ctx, tb, t0, tw)
    return ln.ve



import os
MOE_NE = int(os.environ.get("MOE_NE", "9"))
MOE_PHASE = os.environ.get("MOE_PHASE", "full")
MOE_PB = os.environ.get("MOE_PB", "all")

def build_moe():
    k = Kern("moe")
    xT = k.inp("xT", [1024, TK], F32)
    upS = k.inp("upS", [16384, 1024], BF16)
    gateS = k.inp("gateS", [16384, 1024], BF16)
    downS = k.inp("downS", [8192, 2048], BF16)
    routerTf = k.inp("routerTf", [1024, 8], F32)
    wupsS = k.inp("wupsS", [2048, 1024], BF16)
    wgatesS = k.inp("wgatesS", [2048, 1024], BF16)
    wdownsS = k.inp("wdownsS", [1024, 2048], BF16)
    identf = k.inp("identf", [128, 128], F32)
    oxT = k.out("oxT", [1024, TK], F32)

    pb = [k.ps(f"pb{i}", [128, 512], F32) for i in range(7)]
    psm = k.ps("psm", [128, 128], BF16)
    pbA, pbB, pbC, pbD, pbE, pbF, pbG = pb

    x_blk = [k.sb(f"xb{i}", [128, 512], F32) for i in range(8)]
    xn_ch = [k.sb(f"xn{i}", [128, TK], BF16) for i in range(8)]
    ln = LNBlock(k, TK, ps3=(pbA, pbB, pbC))
    ones128b = k.sb("ones128b", [128, 128], BF16)
    rtf = [k.sb(f"rtf{i}", [128, 8], F32) for i in range(8)]
    idf = k.sb("idf", [128, 128], F32)
    idb = k.sb("idb", [128, 128], BF16)
    su = [k.sb(f"su{j}", [128, 1024], BF16) for j in range(2)]
    sg = [k.sb(f"sg{j}", [128, 1024], BF16) for j in range(2)]
    sd = [k.sb(f"sd{j}", [128, 2048], BF16) for j in range(2)]
    h_sb = [k.sb(f"h{j}", [128, TK], BF16) for j in range(16)]
    u_tmp = k.sb("u_tmp", [128, 512], BF16)
    sg_tmp = k.sb("sg_tmp", [128, 512], BF16)
    y_sb = [k.sb(f"y{j}", [128, TK], BF16) for j in range(8)]
    logits_f = k.sb("logits_f", [128, 128], F32)
    Rrow = k.sb("Rrow", [1, 8], F32)
    Rb = k.sb("Rb", [128, 8], F32)
    lt1 = k.sb("lt1", [128, 8], F32)
    mcols = k.sb("mcols", [128, 16], F32)
    scols = k.sb("scols", [128, 16], F32)
    mask_tok = k.sb("mask_tok", [128, 128], BF16)
    maxc = k.sb("maxc", [128, 16], F32)
    maskT = k.sb("maskT", [8, TK], BF16)
    maskb = k.sb("maskb", [128, TK], BF16)
    mrow = k.sb("mrow", [1, TK], BF16)

    ld = k.ctr("ld")
    mld = k.ctr("mld")
    xld = k.ctr("xld")
    wlds = [k.ctr("wld0"), k.ctr("wld1")]
    pe = k.ctr("pe")
    ve = k.ctr("ve")
    se = k.ctr("se")
    st = k.ctr("st")

    def wslab(e, kind, i):
        if kind in ("u", "g"):
            if e < 8:
                src = upS if kind == "u" else gateS
                r0 = (e * 16 + i) * 128
            else:
                src = wupsS if kind == "u" else wgatesS
                r0 = i * 128
            return src[r0:r0 + 128, :]
        if e < 8:
            return downS[(e * 8 + i) * 128:(e * 8 + i + 1) * 128, :]
        return wdownsS[i * 128:(i + 1) * 128, :]

    items = []
    for e in range(MOE_NE):
        for ic in range(16):
            items.append(("ug", e, ic))
        for dc in range(8):
            items.append(("d", e, dc))

    def script(ctx):
        for i in range(8):
            ctx.dma("sync", rtf[i][:], routerTf[i * 128:(i + 1) * 128, :], ld)
        ldv = ctx.dma("sync", idf[:], identf, ld)
        ctx.memset(ones128b[:], 1.0)
        for j in range(8):
            ctx.memset(y_sb[j][:], 0.0)
        for eng in ("vector", "scalar", "tensor"):
            ctx.wait(eng, ld, ldv)
        ctx.copy(idb[:], idf[:], ctr=ve)
        iniv = ln.init(ctx, ve)
        ctx.wait("tensor", ve, iniv)
        if MOE_PHASE == "none":
            finish_residual(ctx)
            return

        for ci in range(8):
            pev = ctx.mm(pbD[0:1, 0:8], ln.ones128f[:, 0:1], rtf[ci][:],
                         start=(ci == 0), stop=(ci == 7), ctr=pe)
        ctx.wait("vector", pe, pev)
        vev = ctx.copy(Rrow[0:1, 0:8], pbD[0:1, 0:8], ctr=ve)
        ctx.wait("tensor", ve, vev)
        pev = ctx.mm(pbD[:, 0:8], ln.ones128f[0:1, :], Rrow[0:1, 0:8], ctr=pe)
        ctx.wait("vector", pe, pev)
        vev = ctx.copy(Rb[:], pbD[:, 0:8], ctr=ve)
        ctx.wait("tensor", ve, vev)

        state = {"norm_v": 0, "logit_pe": 0}

        def pre_block(ctx, tb, t0, tw):
            if tb > 0:
                ctx.wait("sync", ln.ve, state["norm_v"])
                ctx.wait("sync", pe, state["logit_pe"])
            for ci in range(8):
                xv = ctx.dma("sync", x_blk[ci][:, 0:tw],
                             xT[ci * 128:(ci + 1) * 128, t0:t0 + tw], xld)
            for eng in ("vector", "scalar", "tensor"):
                ctx.wait(eng, xld, xv)

        def post_block(ctx, tb, t0, tw):
            state["norm_v"] = ln.ve.v
            for q in range(4):
                tt_ = tb * 4 + q
                c0 = q * 128
                ctx.wait("tensor", ln.ve, ln.ve.v)
                pev = ctx.trans(pbD[0:128, 0:1],
                                ln.row[0:1, c0:c0 + 128], idf[0:1, 0:1],
                                ctr=pe)
                ctx.wait("vector", pe, pev)
                vev = ctx.copy(mcols[:, tt_:tt_ + 1], pbD[0:128, 0:1], ctr=ve)
                ctx.wait("tensor", ve, vev)
                pev = ctx.trans(pbD[0:128, 0:1],
                                ln.row[0:1, 2 * 512 + c0:2 * 512 + c0 + 128],
                                idf[0:1, 0:1],
                                ctr=pe)
                ctx.wait("vector", pe, pev)
                vev = ctx.copy(scols[:, tt_:tt_ + 1], pbD[0:128, 0:1], ctr=ve)
                ctx.wait("tensor", ve, vev)
                if MOE_PB == "trans":
                    continue
                for ci in range(8):
                    pev = ctx.mm(pbD[:, 0:8],
                                 x_blk[ci][:, c0:c0 + 128], rtf[ci][:],
                                 start=(ci == 0), stop=(ci == 7), ctr=pe)
                ctx.wait("vector", pe, pev)
                ctx.ts(lt1[:], Rb[:], mcols[:, tt_:tt_ + 1], ALU.mult)
                ctx.tt(lt1[:], pbD[:, 0:8], lt1[:], ALU.subtract)
                vev = ctx.ts(logits_f[:, tt_ * 8:(tt_ + 1) * 8], lt1[:],
                             scols[:, tt_:tt_ + 1], ALU.mult, ctr=ve)
                if MOE_PB == "logits":
                    ctx.wait("tensor", ve, vev)
                    continue
                ctx.red(maxc[:, tt_:tt_ + 1],
                        logits_f[:, tt_ * 8:(tt_ + 1) * 8], ALU.max)
                vev = ctx.ts(mask_tok[:, tt_ * 8:(tt_ + 1) * 8],
                             logits_f[:, tt_ * 8:(tt_ + 1) * 8],
                             maxc[:, tt_:tt_ + 1], ALU.is_equal, ctr=ve)
                ctx.wait("tensor", ve, vev)
                if MOE_PB == "mask":
                    continue
                pev = ctx.trans(psm[0:8, 0:128],
                                mask_tok[:, tt_ * 8:(tt_ + 1) * 8], idb[:],
                                ctr=pe)
                ctx.wait("vector", pe, pev)
                vev = ctx.copy(maskT[:, tt_ * 128:(tt_ + 1) * 128],
                               psm[0:8, 0:128], ctr=ve)
                ctx.wait("tensor", ve, vev)
            state["logit_pe"] = pe.v

        lnc = emit_ln(
            ctx, ln,
            x_at=lambda ci, t0, tw: x_blk[ci][:, 0:tw],
            xn_at=lambda ci, t0, tw: xn_ch[ci][:, t0:t0 + tw],
            pre_block=pre_block,
            post_block=(None if MOE_PHASE == "ln" else post_block))
        if MOE_PHASE in ("ln", "router"):
            finish_residual(ctx)
            return

        nug = [0]
        nd = [0]
        nit = [0]
        wv_item = {}

        def load_item(it):
            kind, e, i = it
            wl = wlds[nit[0] % 2]
            nit[0] += 1
            if kind == "ug":
                j = nug[0] % 2
                nug[0] += 1
                ctx.dma("sync", su[j][:], wslab(e, "u", i), wl)
                wv_item[it] = (wl, ctx.dma("sync", sg[j][:], wslab(e, "g", i),
                                           wl))
                return j
            else:
                j = nd[0] % 2
                nd[0] += 1
                wv_item[it] = (wl, ctx.dma("sync", sd[j][:], wslab(e, "d", i),
                                           wl))
                return j

        bufidx = {}
        bufidx[items[0]] = load_item(items[0])

        it_pos = 0
        for e in range(MOE_NE):
            ctx.wait("tensor", ve, ve.v)
            if e < 8:
                ctx.wait("gpsimd", ve, ve.v)
                mv = ctx.dma("gpsimd", mrow[0:1, :], maskT[e:e + 1, :], mld)
                ctx.wait("tensor", mld, mv)
                bank4 = [pbC, pbD, pbE, pbF]
                for tb in range(4):
                    pev = ctx.mm(bank4[tb][:, 0:512], ones128b[0:1, :],
                                 mrow[0:1, tb * 512:(tb + 1) * 512],
                                 ctr=pe)
                ctx.wait("vector", pe, pev)
                for tb in range(4):
                    vev = ctx.copy(maskb[:, tb * 512:(tb + 1) * 512],
                                   bank4[tb][:, 0:512], ctr=ve)
                ctx.wait("tensor", ve, vev)

            for ic in range(16):
                it = ("ug", e, ic)
                if it_pos + 1 < len(items):
                    ctx.wait("sync", pe, pe.v)
                    bufidx[items[it_pos + 1]] = load_item(items[it_pos + 1])
                it_pos += 1
                j = bufidx[it]
                ctx.wait("tensor", wv_item[it][0], wv_item[it][1])
                for tbq in range(4):
                    pu = pbA if tbq % 2 == 0 else pbB
                    pg2 = pbC if tbq % 2 == 0 else pbD
                    tsl = slice(tbq * 512, (tbq + 1) * 512)
                    ctx.wait("tensor", ve, ve.v)
                    for ci in range(8):
                        ctx.mm(pu[:, 0:512],
                               su[j][:, ci * 128:(ci + 1) * 128],
                               xn_ch[ci][:, tsl],
                               start=(ci == 0), stop=(ci == 7))
                    for ci in range(8):
                        pev = ctx.mm(pg2[:, 0:512],
                                     sg[j][:, ci * 128:(ci + 1) * 128],
                                     xn_ch[ci][:, tsl],
                                     start=(ci == 0), stop=(ci == 7), ctr=pe)
                    ctx.wait("scalar", pe, pev)
                    ctx.wait("scalar", ve, ve.v)
                    sev = ctx.act(sg_tmp[:], pg2[:, 0:512], AF.Silu, ctr=se)
                    ctx.wait("vector", pe, pev)
                    if e < 8:
                        ctx.tt(u_tmp[:], pu[:, 0:512], maskb[:, tsl], ALU.mult)
                    else:
                        ctx.copy(u_tmp[:], pu[:, 0:512])
                    ctx.wait("vector", se, sev)
                    vev = ctx.tt(h_sb[ic][:, tsl], u_tmp[:], sg_tmp[:],
                                 ALU.mult, ctr=ve)
            ctx.wait("tensor", ve, ve.v)
            for dc in range(8):
                it = ("d", e, dc)
                if it_pos + 1 < len(items):
                    ctx.wait("sync", pe, pe.v)
                    bufidx[items[it_pos + 1]] = load_item(items[it_pos + 1])
                it_pos += 1
                j = bufidx[it]
                ctx.wait("tensor", wv_item[it][0], wv_item[it][1])
                for tbq in range(4):
                    pd = pbE if tbq % 2 == 0 else pbF
                    tsl = slice(tbq * 512, (tbq + 1) * 512)
                    ctx.wait("tensor", ve, ve.v)
                    for ic in range(16):
                        pev = ctx.mm(pd[:, 0:512],
                                     sd[j][:, ic * 128:(ic + 1) * 128],
                                     h_sb[ic][:, tsl],
                                     start=(ic == 0), stop=(ic == 15),
                                     ctr=(pe if ic == 15 else None))
                    ctx.wait("vector", pe, pev)
                    vev = ctx.tt(y_sb[dc][:, tsl], y_sb[dc][:, tsl],
                                 pd[:, 0:512], ALU.add, ctr=ve)
            ctx.wait("tensor", ve, ve.v)

        finish_residual(ctx)

    def finish_residual(ctx):
        ctx.wait("sync", pe, pe.v)
        ctx.wait("sync", ve, ve.v)
        hrl = [h_sb[j][:].bitcast(F32) for j in range(16)]
        for j in range(8):
            for h2 in range(2):
                lv = ctx.dma("sync", hrl[2 * j + h2],
                             xT[j * 128:(j + 1) * 128,
                                h2 * 1024:(h2 + 1) * 1024], ld)
        ctx.wait("vector", ld, lv)
        for j in range(8):
            for h2 in range(2):
                buf = hrl[2 * j + h2]
                vev = ctx.tt(buf, buf,
                             y_sb[j][:, h2 * 1024:(h2 + 1) * 1024], ALU.add,
                             ctr=ve)
                ctx.wait("gpsimd", ve, vev)
                ctx.dma("gpsimd",
                        oxT[j * 128:(j + 1) * 128, h2 * 1024:(h2 + 1) * 1024],
                        buf, st)
        ctx.wait("gpsimd", st, st.v)

    return k.finish(script)


def moe_slabs(W):
    upS = c_(W["upT"].reshape(8, 8, 128, 16, 128)
             .transpose(0, 3, 2, 1, 4).reshape(16384, 1024))
    gateS = c_(W["gateT"].reshape(8, 8, 128, 16, 128)
               .transpose(0, 3, 2, 1, 4).reshape(16384, 1024))
    downS = c_(W["downT"].reshape(8, 16, 128, 8, 128)
               .transpose(0, 3, 2, 1, 4).reshape(8192, 2048))
    wupsS = c_(W["wupsT"].reshape(8, 128, 16, 128)
               .transpose(2, 1, 0, 3).reshape(2048, 1024))
    wgatesS = c_(W["wgatesT"].reshape(8, 128, 16, 128)
                 .transpose(2, 1, 0, 3).reshape(2048, 1024))
    wdownsS = c_(W["wdownsT"].reshape(16, 128, 8, 128)
                 .transpose(2, 1, 0, 3).reshape(1024, 2048))
    return dict(upS=upS, gateS=gateS, downS=downS, wupsS=wupsS,
                wgatesS=wgatesS, wdownsS=wdownsS)


def run_moe(xT_full, W, raw):
    nc = built(f"moe{MOE_NE}{MOE_PHASE}{MOE_PB}", build_moe)
    identf = np.eye(128, dtype=np.float32)
    routerTf = c_(raw["L1_ffn_router"].T.astype(np.float32))
    slabs = moe_slabs(W)
    in_maps = []
    for c in range(NCORE):
        b, h = c // 2, c % 2
        in_maps.append(dict(
            xT=c_(xT_full[b][:, h * TK:(h + 1) * TK]),
            routerTf=routerTf, identf=identf, **slabs))
    res = run8(nc, in_maps)
    out = np.empty_like(xT_full)
    for c in range(NCORE):
        b, h = c // 2, c % 2
        out[b][:, h * TK:(h + 1) * TK] = res.results[c]["oxT"]
    return out



def build_attn_a():
    k = Kern("attn_a")
    xTb = k.inp("xTb", [1024, T], F32)
    xq = k.inp("xq", [1024, TK], F32)
    Wc_down = k.inp("Wc_down", [1024, 512], BF16)
    Wq_up = k.inp("Wq_up", [512, 1024], BF16)
    Wkv_up = k.inp("Wkv_up", [512, 128], BF16)
    Wc_proj = k.inp("Wc_proj", [1024, 1024], BF16)
    identf = k.inp("identf", [128, 128], F32)
    cmask = k.inp("cmask", [4 * 128, 512], BF16)
    pfull = k.inp("pfull", [128, 65], BF16)
    oxT = k.out("oxT", [1024, TK], F32)

    pb = [k.ps(f"pb{i}", [128, 512], F32) for i in range(7)]
    psb = k.ps("psb", [128, 128], BF16)
    pbA, pbB, pbC, pbD, pbE, pbF, pbG = pb

    x_blk = [k.sb(f"xb{i}", [128, 512], F32) for i in range(8)]
    xn_blk = [k.sb(f"xn{i}", [128, 1024], BF16) for i in range(8)]
    ln = LNBlock(k, T, ps3=(pbA, pbB, pbC))
    latT = [k.sb(f"lat{i}", [128, T], BF16) for i in range(4)]
    kvT = k.sb("kvT", [128, T], BF16)
    qh = [k.sb(f"qh{i}", [64, TK], BF16) for i in range(16)]
    vtok = [k.sb(f"vt{i}", [128, 65], BF16) for i in range(32)]
    ex = [k.sb(f"exb{i}", [128, 512], BF16) for i in range(2)]
    cm = [k.sb(f"cm{i}", [128, 512], BF16) for i in range(4)]
    pf = k.sb("pf", [128, 65], BF16)
    def yts(ci, tb):
        idx = ci * 4 + tb
        if idx < 16:
            return x_blk[idx // 2][:].bitcast(BF16)[
                :, (idx % 2) * 512:(idx % 2) * 512 + 512]
        j = idx - 16
        return xn_blk[j // 2][:, (j % 2) * 512:(j % 2) * 512 + 512]
    rrow = k.sb("rrow", [1, 512], F32)
    rb = k.sb("rb", [64, 512], F32)
    xres = [k.sb(f"xr{i}", [128, 512], F32) for i in range(2)]
    oxs = [k.sb(f"oxsb{i}", [128, 512], F32) for i in range(2)]
    idf = k.sb("idf", [128, 128], F32)
    idb2 = k.sb("idb2", [128, 128], BF16)
    onesb = k.sb("onesb", [1, 128], BF16)
    Wc_down_sb = [k.sb(f"wcd{i}", [128, 512], BF16) for i in range(8)]
    wkv = [k.sb(f"wkv{i}", [128, 128], BF16) for i in range(4)]
    wqu = [k.sb(f"wqu{i}", [128, 1024], BF16) for i in range(4)]

    ld = k.ctr("ld")
    xld = k.ctr("xld")
    rld = [k.ctr("rld0"), k.ctr("rld1")]
    pe = k.ctr("pe")
    ve = k.ctr("ve")
    se = k.ctr("se")
    st = k.ctr("st")

    def script(ctx):
        ctx.dma("sync", idf[:], identf, ld)
        for i in range(4):
            ctx.dma("sync", cm[i][:], cmask[i * 128:(i + 1) * 128, :], ld)
        ctx.dma("sync", pf[:], pfull, ld)
        for i in range(8):
            ctx.dma("sync", Wc_down_sb[i][:],
                    Wc_down[i * 128:(i + 1) * 128, :], ld)
        for i in range(4):
            ctx.dma("sync", wkv[i][:], Wkv_up[i * 128:(i + 1) * 128, :], ld)
        for i in range(4):
            ldv = ctx.dma("sync", wqu[i][:], Wq_up[i * 128:(i + 1) * 128, :],
                          ld)
        ctx.wait("vector", ld, ldv)
        ctx.memset(onesb[:], 1.0)
        ctx.copy(idb2[:], idf[:])
        iniv = ln.init(ctx, ve)
        for eng in ("tensor", "scalar"):
            ctx.wait(eng, ve, iniv)
            ctx.wait(eng, ld, ldv)

        state = {"norm_v": 0}

        def pre_block(ctx, tb, t0, tw):
            if tb > 0:
                ctx.wait("sync", ln.ve, state["norm_v"])
            for ci in range(8):
                xv = ctx.dma("sync", x_blk[ci][:, 0:tw],
                             xTb[ci * 128:(ci + 1) * 128, t0:t0 + tw], xld)
            for eng in ("vector", "scalar", "tensor"):
                ctx.wait(eng, xld, xv)

        def post_block(ctx, tb, t0, tw):
            state["norm_v"] = ln.ve.v
            ctx.wait("tensor", ln.ve, ln.ve.v)
            lp = [pbD, pbE, pbF, pbG]
            par = (t0 // 512) % 2
            for lc in range(4):
                for ci in range(8):
                    pev = ctx.mm(lp[lc][:, 0:tw],
                                 Wc_down_sb[ci][:, lc * 128:(lc + 1) * 128],
                                 xn_blk[ci][:, par * 512:par * 512 + tw],
                                 start=(ci == 0), stop=(ci == 7), ctr=pe)
            ctx.wait("vector", pe, pev)
            for lc in range(4):
                vev = ctx.copy(latT[lc][:, t0:t0 + tw], lp[lc][:, 0:tw],
                               ctr=ve)
            ctx.wait("tensor", ve, vev)

        emit_ln(ctx, ln,
                x_at=lambda ci, t0, tw: x_blk[ci][:, 0:tw],
                xn_at=lambda ci, t0, tw: xn_blk[ci][
                    :, ((t0 // 512) % 2) * 512:((t0 // 512) % 2) * 512 + tw],
                pre_block=pre_block, post_block=post_block)

        for tb in range(8):
            ctx.wait("tensor", ve, ve.v)
            for ci in range(4):
                pev = ctx.mm(pbA[:, 0:512], wkv[ci][:],
                             latT[ci][:, tb * 512:(tb + 1) * 512],
                             start=(ci == 0), stop=(ci == 3), ctr=pe)
            ctx.wait("vector", pe, pev)
            vev = ctx.copy(kvT[:, tb * 512:(tb + 1) * 512], pbA[:, 0:512],
                           ctr=ve)
        ctx.wait("tensor", ve, ve.v)
        for si in range(32):
            pev = ctx.trans(psb[0:128, 0:64],
                            kvT[64:128, si * 128:(si + 1) * 128],
                            idb2[64:128, 64:128], ctr=pe)
            ctx.wait("vector", pe, pev)
            ctx.copy(vtok[si][:, 0:64], psb[0:128, 0:64])
            vev = ctx.memset(vtok[si][:, 64:65], 1.0, ctr=ve)
            if si < 16:
                vev = ctx.tt(vtok[si][:], vtok[si][:], pf[:], ALU.mult,
                             ctr=ve)
            ctx.wait("tensor", ve, vev)
        for h in range(16):
            for qb in range(4):
                ctx.wait("tensor", ve, ve.v)
                for ci in range(4):
                    pev = ctx.mm(pbB[0:64, 0:512],
                                 wqu[ci][:, h * 64:(h + 1) * 64],
                                 latT[ci][:, 2048 + qb * 512:
                                           2048 + (qb + 1) * 512],
                                 start=(ci == 0), stop=(ci == 3), ctr=pe)
                ctx.wait("vector", pe, pev)
                vev = ctx.copy(qh[h][:, qb * 512:(qb + 1) * 512],
                               pbB[0:64, 0:512], ctr=ve)
        ctx.wait("tensor", ve, ve.v)

        qdone_pe = pe.v
        for h in range(16):
            for qb in range(4):
                nown = (qb + 1) * 4
                nsi = 16 + nown
                for sj in range(nsi):
                    si = sj
                    own_j = sj - 16
                    bsel = sj % 2
                    pssc = pbC if bsel == 0 else pbD
                    ctx.wait("tensor", ve, ve.v)
                    ctx.wait("tensor", se, se.v)
                    pev = ctx.mm(pssc[:, 0:512],
                                 kvT[0:64, si * 128:(si + 1) * 128],
                                 qh[h][:, qb * 512:(qb + 1) * 512],
                                 start=True, stop=True, ctr=pe)
                    ctx.wait("scalar", pe, pev)
                    sev = ctx.act(ex[bsel][:, 0:512], pssc[:, 0:512], AF.Exp,
                                  scale=0.125, ctr=se)
                    diag = own_j >= 0 and own_j >= qb * 4
                    lastv = sev
                    if diag:
                        kk = own_j - qb * 4
                        ctx.wait("vector", se, sev)
                        lastv = ctx.tt(ex[bsel][:, 0:512], ex[bsel][:, 0:512],
                                       cm[kk][:], ALU.mult, ctr=ve)
                        ctx.wait("tensor", ve, lastv)
                    else:
                        ctx.wait("tensor", se, sev)
                    pev = ctx.mm(pbE[0:65, 0:512], vtok[si][:, 0:65],
                                 ex[bsel][:, 0:512],
                                 start=(sj == 0), stop=(sj == nsi - 1),
                                 ctr=pe)
                ctx.wait("vector", pe, pev)
                vev = ctx.recip(rrow[0:1, 0:512], pbE[64:65, 0:512], ctr=ve)
                ctx.wait("tensor", ve, vev)
                pev = ctx.mm(pbF[0:64, 0:512], ln.ones128f[0:1, 0:64],
                             rrow[0:1, 0:512], ctr=pe)
                ctx.wait("vector", pe, pev)
                ctx.copy(rb[:, 0:512], pbF[0:64, 0:512])
                ysl = yts(h // 2, qb)[(h % 2) * 64:(h % 2) * 64 + 64, :]
                vev = ctx.tt(ysl, pbE[0:64, 0:512], rb[:, 0:512], ALU.mult,
                             ctr=ve)
                ctx.wait("tensor", ve, vev)

        wcp = [latT[i // 4][:, (i % 4) * 1024:(i % 4 + 1) * 1024]
               for i in range(8)]
        ctx.wait("sync", pe, qdone_pe)
        for i in range(8):
            wv = ctx.dma("sync", wcp[i], Wc_proj[i * 128:(i + 1) * 128, :],
                         ld)
        ctx.wait("tensor", ld, wv)
        st_hist = []
        for n, (ec, tb) in enumerate([(a, b) for a in range(8)
                                      for b in range(4)]):
            par = n % 2
            if n >= 2:
                ctx.wait("sync", st, st_hist[n - 2])
            rv = ctx.dma("sync", xres[par][:],
                         xq[ec * 128:(ec + 1) * 128,
                            tb * 512:(tb + 1) * 512], rld[par])
            ctx.wait("tensor", ve, ve.v)
            for ci in range(8):
                pev = ctx.mm(pbG[:, 0:512],
                             wcp[ci][:, ec * 128:(ec + 1) * 128],
                             yts(ci, tb),
                             start=(ci == 0), stop=(ci == 7), ctr=pe)
            ctx.wait("vector", pe, pev)
            ctx.wait("vector", rld[par], rv)
            if n >= 2:
                ctx.wait("vector", st, st_hist[n - 2])
            vev = ctx.tt(oxs[par][:], pbG[:, 0:512], xres[par][:], ALU.add,
                         ctr=ve)
            ctx.wait("gpsimd", ve, vev)
            sv = ctx.dma("gpsimd",
                         oxT[ec * 128:(ec + 1) * 128,
                             tb * 512:(tb + 1) * 512], oxs[par][:], st)
            st_hist.append(sv)
        ctx.wait("gpsimd", st, st.v)

    return k.finish(script)


def attn_a_cmask():
    m = np.zeros((4, 128, 512), dtype=BF)
    s = np.arange(128)[:, None]
    q = np.arange(512)[None, :]
    for kk in range(4):
        m[kk] = (s + kk * 128 <= q)
    return m.reshape(512, 512)


def run_attn_a(xT_full, W):
    nc = built("attn_a", build_attn_a)
    identf = np.eye(128, dtype=np.float32)
    cmask = attn_a_cmask()
    in_maps = []
    for c in range(NCORE):
        b, h = c // 2, c % 2
        own = xT_full[b][:, h * TK:(h + 1) * TK]
        other = xT_full[b][:, (1 - h) * TK:(2 - h) * TK]
        rot = np.concatenate([other, own], axis=1)
        pf = np.full((128, 65), 1.0 if h == 1 else 0.0, dtype=BF)
        in_maps.append(dict(
            xTb=c_(rot), xq=c_(own),
            Wc_down=W["a_Wc_down"], Wq_up=W["a_Wq_up"],
            Wkv_up=W["a_Wkv_up"], Wc_proj=W["a_Wc_proj"],
            identf=identf, cmask=cmask, pfull=pf))
    res = run8(nc, in_maps)
    out = np.empty_like(xT_full)
    for c in range(NCORE):
        b, h = c // 2, c % 2
        out[b][:, h * TK:(h + 1) * TK] = res.results[c]["oxT"]
    return out



def build_attn_b0():
    k = Kern("attn_b0")
    xTb = k.inp("xTb", [1024, T], F32)
    xq = k.inp("xq", [1024, TK], F32)
    bU = k.inp("bU", [1024, 32], BF16)
    identf = k.inp("identf", [128, 128], F32)
    minus1 = k.inp("minus1", [1, T], BF16)
    oxT = k.out("oxT", [1024, TK], F32)

    pb = [k.ps(f"pb{i}", [128, 512], F32) for i in range(7)]
    psb = k.ps("psb", [128, 128], BF16)
    pbA, pbB, pbC, pbD, pbE, pbF, pbG = pb

    x_blk = [k.sb(f"xb{i}", [128, 512], F32) for i in range(8)]
    xn_blk = [k.sb(f"xn{i}", [128, 1024], BF16) for i in range(8)]
    ln = LNBlock(k, T, ps3=(pbA, pbB, pbC))
    xtok = [k.sb(f"xt{i}", [128, 1032], BF16) for i in range(32)]
    Lt = k.sb("Lt", [34, T], BF16)
    Rt = k.sb("Rt", [34, T], BF16)
    sqy = k.sb("sqy", [32, 512], F32)
    nrow = k.sb("nrow", [1, 512], F32)
    nrow_b = k.sb("nrow_b", [1, T], BF16)
    ex = [k.sb(f"exb{i}", [128, 512], BF16) for i in range(32)]
    rrow = k.sb("rrow", [1, 512], F32)
    rb = k.sb("rb", [128, 512], F32)
    xres = [k.sb(f"xr{i}", [128, 512], F32) for i in range(2)]
    oxs = [k.sb(f"oxsb{i}", [128, 512], F32) for i in range(2)]
    idf = k.sb("idf", [128, 128], F32)
    idb2 = k.sb("idb2", [128, 128], BF16)
    bU_sb = [k.sb(f"bu{i}", [128, 32], BF16) for i in range(8)]

    ld = k.ctr("ld")
    xld = k.ctr("xld")
    rld = [k.ctr("rld0"), k.ctr("rld1")]
    dmv = k.ctr("dmv")
    pe = k.ctr("pe")
    ve = k.ctr("ve")
    se = k.ctr("se")
    st = k.ctr("st")

    def script(ctx):
        ctx.dma("sync", idf[:], identf, ld)
        for i in range(8):
            ldv = ctx.dma("sync", bU_sb[i][:], bU[i * 128:(i + 1) * 128, :],
                          ld)
        ctx.wait("vector", ld, ldv)
        ctx.copy(idb2[:], idf[:])
        iniv = ln.init(ctx, ve)
        for eng in ("tensor", "scalar"):
            ctx.wait(eng, ve, iniv)
            ctx.wait(eng, ld, ldv)
        ctx.dma("gpsimd", Lt[33:34, :], minus1, dmv)
        dv = ctx.dma("gpsimd", Rt[32:33, :], minus1, dmv)
        ctx.wait("tensor", dmv, dv)

        state = {"norm_v": 0}

        def pre_block(ctx, tb, t0, tw):
            if tb > 0:
                ctx.wait("sync", ln.ve, state["norm_v"])
            for ci in range(8):
                xv = ctx.dma("sync", x_blk[ci][:, 0:tw],
                             xTb[ci * 128:(ci + 1) * 128, t0:t0 + tw], xld)
            for eng in ("vector", "scalar", "tensor"):
                ctx.wait(eng, xld, xv)

        def post_block(ctx, tb, t0, tw):
            state["norm_v"] = ln.ve.v
            ctx.wait("tensor", ln.ve, ln.ve.v)
            par = (t0 // 512) % 2
            xs = slice(par * 512, par * 512 + tw)
            for ci in range(8):
                pev = ctx.mm(pbD[0:32, 0:tw], bU_sb[ci][:],
                             xn_blk[ci][:, xs],
                             start=(ci == 0), stop=(ci == 7), ctr=pe)
            ctx.wait("scalar", pe, pev)
            ctx.act(Lt[0:32, t0:t0 + tw], pbD[0:32, 0:tw], AF.Copy,
                    scale=float(np.sqrt(2.0)))
            ctx.act(Rt[0:32, t0:t0 + tw], pbD[0:32, 0:tw], AF.Copy,
                    scale=float(np.sqrt(2.0)))
            sev = ctx.act(sqy[:, 0:tw], pbD[0:32, 0:tw], AF.Square, ctr=se)
            ctx.wait("tensor", se, sev)
            pev = ctx.mm(pbE[0:1, 0:tw], ln.ones128f[0:32, 0:1],
                         sqy[0:32, 0:tw], ctr=pe)
            ctx.wait("vector", pe, pev)
            vev = ctx.copy(nrow_b[0:1, t0:t0 + tw], pbE[0:1, 0:tw], ctr=ve)
            ctx.wait("gpsimd", ve, vev)
            dv = ctx.dma("gpsimd", Lt[32:33, t0:t0 + tw],
                         nrow_b[0:1, t0:t0 + tw], dmv)
            dv = ctx.dma("gpsimd", Rt[33:34, t0:t0 + tw],
                         nrow_b[0:1, t0:t0 + tw], dmv)
            for sl in range(tw // 128):
                si = (t0 + sl * 128) // 128
                for dc in range(8):
                    pev = ctx.trans(psb[:, 0:128],
                                    xn_blk[dc][:, par * 512 + sl * 128:
                                               par * 512 + sl * 128 + 128],
                                    idb2[:], ctr=pe)
                    ctx.wait("vector", pe, pev)
                    vev = ctx.copy(xtok[si][:, dc * 128:(dc + 1) * 128],
                                   psb[:, 0:128], ctr=ve)
                    ctx.wait("tensor", ve, vev)
                vev = ctx.memset(xtok[si][:, 1024:1025], 1.0, ctr=ve)
            ctx.wait("tensor", ve, ve.v)

        emit_ln(ctx, ln,
                x_at=lambda ci, t0, tw: x_blk[ci][:, 0:tw],
                xn_at=lambda ci, t0, tw: xn_blk[ci][
                    :, ((t0 // 512) % 2) * 512:((t0 // 512) % 2) * 512 + tw],
                pre_block=pre_block, post_block=post_block)
        ctx.wait("tensor", dmv, dmv.v)

        st_hist = []
        n_out = 0
        for qb in range(4):
            q0 = 2048 + qb * 512
            for si in range(32):
                psc = pbC if si % 2 == 0 else pbD
                ctx.wait("tensor", se, se.v)
                pev = ctx.mm(psc[:, 0:512], Lt[:, si * 128:(si + 1) * 128],
                             Rt[:, q0:q0 + 512], start=True, stop=True,
                             ctr=pe)
                ctx.wait("scalar", pe, pev)
                sev = ctx.act(ex[si][:, 0:512], psc[:, 0:512], AF.Exp,
                              ctr=se)
            ctx.wait("tensor", se, se.v)
            pev = None
            for si in range(32):
                pev = ctx.mm(pbF[0:1, 0:512], xtok[si][:, 1024:1025],
                             ex[si][:, 0:512], start=(si == 0),
                             stop=(si == 31), ctr=pe)
            ctx.wait("vector", pe, pev)
            vev = ctx.recip(rrow[0:1, 0:512], pbF[0:1, 0:512], ctr=ve)
            ctx.wait("tensor", ve, vev)
            pev = ctx.mm(pbG[:, 0:512], ln.ones128f[0:1, :],
                         rrow[0:1, 0:512], ctr=pe)
            ctx.wait("vector", pe, pev)
            vev = ctx.copy(rb[:, 0:512], pbG[:, 0:512], ctr=ve)
            ctx.wait("tensor", ve, vev)
            for dc in range(8):
                pd = pbE if dc % 2 == 0 else pbF
                for si in range(32):
                    pev = ctx.mm(pd[:, 0:512],
                                 xtok[si][:, dc * 128:(dc + 1) * 128],
                                 ex[si][:, 0:512], start=(si == 0),
                                 stop=(si == 31), ctr=(pe if si == 31 else None))
                par = n_out % 2
                if n_out >= 2:
                    ctx.wait("sync", st, st_hist[n_out - 2])
                rv = ctx.dma("sync", xres[par][:],
                             xq[dc * 128:(dc + 1) * 128,
                                qb * 512:(qb + 1) * 512], rld[par])
                ctx.wait("vector", pe, pev)
                ctx.wait("vector", rld[par], rv)
                if n_out >= 2:
                    ctx.wait("vector", st, st_hist[n_out - 2])
                ctx.tt(oxs[par][:], pd[:, 0:512], rb[:, 0:512], ALU.mult)
                vev = ctx.tt(oxs[par][:], oxs[par][:], xres[par][:], ALU.add,
                             ctr=ve)
                ctx.wait("gpsimd", ve, vev)
                sv = ctx.dma("gpsimd",
                             oxT[dc * 128:(dc + 1) * 128,
                                 qb * 512:(qb + 1) * 512], oxs[par][:], st)
                st_hist.append(sv)
                n_out += 1
                ctx.wait("tensor", ve, ve.v)
        ctx.wait("gpsimd", st, st.v)

    return k.finish(script)


def run_attn_b0(xT_full, W):
    nc = built("attn_b0", build_attn_b0)
    identf = np.eye(128, dtype=np.float32)
    minus1 = np.full((1, T), -1.0, dtype=BF)
    in_maps = []
    for c in range(NCORE):
        b, h = c // 2, c % 2
        own = xT_full[b][:, h * TK:(h + 1) * TK]
        other = xT_full[b][:, (1 - h) * TK:(2 - h) * TK]
        rot = np.concatenate([other, own], axis=1)
        in_maps.append(dict(xTb=c_(rot), xq=c_(own), bU=W["b_U"],
                            identf=identf, minus1=minus1))
    res = run8(nc, in_maps)
    out = np.empty_like(xT_full)
    for c in range(NCORE):
        b, h = c // 2, c % 2
        out[b][:, h * TK:(h + 1) * TK] = res.results[c]["oxT"]
    return out



def build_gffn():
    k = Kern("gffn")
    xTb = k.inp("xTb", [1024, T], F32)
    xq = k.inp("xq", [1024, TK], F32)
    fWq = k.inp("fWq", [1024, 1024], BF16)
    fWk = k.inp("fWk", [1024, 1024], BF16)
    fWv = k.inp("fWv", [1024, 1024], BF16)
    fWo = k.inp("fWo", [1024, 1024], BF16)
    Ws1 = k.inp("Ws1", [1024, 256], F32)
    bs1 = k.inp("bs1", [2 * 128, 1], F32)
    Ws2 = k.inp("Ws2", [256, 1], F32)
    bs2 = k.inp("bs2", [1, 1], F32)
    identf = k.inp("identf", [128, 128], F32)
    REP = k.inp("REP", [256, TK], BF16)
    oxT = k.out("oxT", [1024, TK], F32)

    pb = [k.ps(f"pb{i}", [128, 512], F32) for i in range(7)]
    psb = k.ps("psb", [128, 128], BF16)
    pbA, pbB, pbC, pbD, pbE, pbF, pbG = pb

    x_blk = [k.sb(f"xb{i}", [128, 512], F32) for i in range(8)]
    xn_blk = [k.sb(f"xn{i}", [128, 1024], BF16) for i in range(8)]
    ln = LNBlock(k, T, ps3=(pbA, pbB, pbC))
    meansT = [k.sb(f"mn{i}", [128, 256], F32) for i in range(8)]
    meansB = [k.sb(f"mb{i}", [128, 256], BF16) for i in range(8)]
    QT = [k.sb(f"qt{i}", [128, 256], BF16) for i in range(8)]
    KT = [k.sb(f"kt{i}", [128, 256], BF16) for i in range(8)]
    Vg = [k.sb(f"vg{i}", [128, 1025], BF16) for i in range(2)]
    ctxT = [k.sb(f"cx{i}", [128, 256], BF16) for i in range(8)]
    og = [k.sb(f"og{i}", [128, 1024], BF16) for i in range(2)]
    hT = [k.sb(f"ht{i}", [128, 256], F32) for i in range(2)]
    srow = k.sb("srow", [1, 256], F32)
    scol = [k.sb(f"sc{i}", [128, 1], F32) for i in range(2)]
    sbc = k.sb("sbc", [128, 256], F32)
    cmp2 = [k.sb(f"cmp{i}", [128, 256], F32) for i in range(2)]
    selrow = k.sb("selrow", [1, 256], F32)
    selcol = [k.sb(f"sel{i}", [128, 1], F32) for i in range(2)]
    exg = [k.sb(f"exg{i}", [128, 256], BF16) for i in range(2)]
    rrow = k.sb("rrow", [1, 256], F32)
    rbg = k.sb("rbg", [64, 256], F32)
    xres = [k.sb(f"xr{i}", [128, 512], F32) for i in range(2)]
    oxs = [k.sb(f"oxsb{i}", [128, 512], F32) for i in range(2)]
    idf = k.sb("idf", [128, 128], F32)
    rep_sb = [k.sb(f"rp{i}", [128, TK], BF16) for i in range(2)]
    ws1_sb = [k.sb(f"w1s{i}", [128, 256], F32) for i in range(8)]
    ws2_sb = [k.sb(f"w2s{i}", [128, 1], F32) for i in range(2)]
    bs1_sb = [k.sb(f"b1s{i}", [128, 1], F32) for i in range(2)]
    bs2_sb = k.sb("b2s", [1, 1], F32)
    wq_sb = [k.sb(f"wqs{i}", [128, 1024], BF16) for i in range(8)]
    wk_sb = [k.sb(f"wks{i}", [128, 1024], BF16) for i in range(8)]
    wv_sb = [k.sb(f"wvs{i}", [128, 1024], BF16) for i in range(8)]
    wo_sb = [k.sb(f"wos{i}", [128, 1024], BF16) for i in range(8)]

    ld = k.ctr("ld")
    xld = k.ctr("xld")
    rld = [k.ctr("rld0"), k.ctr("rld1")]
    pe = k.ctr("pe")
    ve = k.ctr("ve")
    se = k.ctr("se")
    st = k.ctr("st")

    def script(ctx):
        ctx.dma("sync", idf[:], identf, ld)
        for i in range(8):
            ctx.dma("sync", ws1_sb[i][:], Ws1[i * 128:(i + 1) * 128, :], ld)
        for i in range(2):
            ctx.dma("sync", ws2_sb[i][:], Ws2[i * 128:(i + 1) * 128, :], ld)
            ctx.dma("sync", bs1_sb[i][:], bs1[i * 128:(i + 1) * 128, :], ld)
            ctx.dma("sync", rep_sb[i][:], REP[i * 128:(i + 1) * 128, :], ld)
        ctx.dma("sync", bs2_sb[:], bs2, ld)
        for i in range(8):
            ctx.dma("sync", wq_sb[i][:], fWq[i * 128:(i + 1) * 128, :], ld)
            ctx.dma("sync", wk_sb[i][:], fWk[i * 128:(i + 1) * 128, :], ld)
            ctx.dma("sync", wv_sb[i][:], fWv[i * 128:(i + 1) * 128, :], ld)
            ldv = ctx.dma("sync", wo_sb[i][:], fWo[i * 128:(i + 1) * 128, :],
                          ld)
        ctx.wait("vector", ld, ldv)
        iniv = ln.init(ctx, ve)
        for eng in ("tensor", "scalar"):
            ctx.wait(eng, ve, iniv)
            ctx.wait(eng, ld, ldv)

        state = {"norm_v": 0}

        def pre_block(ctx, tb, t0, tw):
            if tb > 0:
                ctx.wait("sync", ln.ve, state["norm_v"])
            for ci in range(8):
                xv = ctx.dma("sync", x_blk[ci][:, 0:tw],
                             xTb[ci * 128:(ci + 1) * 128, t0:t0 + tw], xld)
            for eng in ("vector", "scalar", "tensor"):
                ctx.wait(eng, xld, xv)

        def post_block(ctx, tb, t0, tw):
            state["norm_v"] = ln.ve.v
            par = (t0 // 512) % 2
            g0 = t0 // 16
            for ci in range(8):
                ctx.red(meansT[ci][:, g0:g0 + 32].rearrange("p g -> p g 1")
                        if False else meansT[ci][:, g0:g0 + 32],
                        xn_blk[ci][:, par * 512:par * 512 + tw]
                        .rearrange("p (g w) -> p g w", w=16), ALU.add)
                vev = ctx.ts(meansT[ci][:, g0:g0 + 32],
                             meansT[ci][:, g0:g0 + 32], 1.0 / 16, ALU.mult,
                             ctr=ve)
                ctx.copy(meansB[ci][:, g0:g0 + 32], meansT[ci][:, g0:g0 + 32],
                         ctr=ve)

        emit_ln(ctx, ln,
                x_at=lambda ci, t0, tw: x_blk[ci][:, 0:tw],
                xn_at=lambda ci, t0, tw: xn_blk[ci][
                    :, ((t0 // 512) % 2) * 512:((t0 // 512) % 2) * 512 + tw],
                pre_block=pre_block, post_block=post_block)

        ctx.wait("tensor", ve, ve.v)
        for ft in range(2):
            for ci in range(8):
                pev = ctx.mm(pbD[:, 0:256], ws1_sb[ci][:, ft * 128:(ft + 1) * 128],
                             meansT[ci][:], start=(ci == 0), stop=(ci == 7),
                             ctr=pe)
            ctx.wait("scalar", pe, pev)
            sev = ctx.act(hT[ft][:], pbD[:, 0:256], AF.Relu,
                          bias=bs1_sb[ft][:], ctr=se)
            ctx.wait("tensor", se, sev)
        for ft in range(2):
            pev = ctx.mm(pbE[0:1, 0:256], ws2_sb[ft][:], hT[ft][:],
                         start=(ft == 0), stop=(ft == 1), ctr=pe)
        ctx.wait("vector", pe, pev)
        vev = ctx.ts(srow[0:1, :], pbE[0:1, 0:256], bs2_sb[0:1, :], ALU.add,
                     ctr=ve)
        ctx.wait("tensor", ve, vev)
        for gt in range(2):
            pev = ctx.trans(pbF[0:128, 0:1], srow[0:1, gt * 128:(gt + 1) * 128],
                            idf[0:1, 0:1], ctr=pe)
            ctx.wait("vector", pe, pev)
            vev = ctx.copy(scol[gt][:], pbF[0:128, 0:1], ctr=ve)
            ctx.wait("tensor", ve, vev)
        pev = ctx.mm(pbG[:, 0:256], ln.ones128f[0:1, :], srow[0:1, :], ctr=pe)
        ctx.wait("vector", pe, pev)
        vev = ctx.copy(sbc[:], pbG[:, 0:256], ctr=ve)
        for gt in range(2):
            vev = ctx.ts(cmp2[gt][:], sbc[:], scol[gt][:], ALU.is_lt, ctr=ve)
            ctx.wait("tensor", ve, vev)
            pev = ctx.mm(pbE[0:1, 0:256], ln.ones128f[:, 0:1], cmp2[gt][:],
                         start=(gt == 0), stop=(gt == 1), ctr=pe)
        ctx.wait("vector", pe, pev)
        vev = ctx.ts(selrow[0:1, :], pbE[0:1, 0:256], 64.0, ALU.is_lt,
                     ctr=ve)
        ctx.wait("tensor", ve, vev)
        for gt in range(2):
            pev = ctx.trans(pbF[0:128, 0:1],
                            selrow[0:1, gt * 128:(gt + 1) * 128],
                            idf[0:1, 0:1], ctr=pe)
            ctx.wait("vector", pe, pev)
            vev = ctx.copy(selcol[gt][:], pbF[0:128, 0:1], ctr=ve)
            ctx.wait("tensor", ve, vev)

        for ci_out in range(8):
            for ci in range(8):
                ctx.mm(pbC[:, 0:256], wq_sb[ci][:, ci_out * 128:(ci_out + 1) * 128],
                       meansB[ci][:], start=(ci == 0), stop=(ci == 7))
            for ci in range(8):
                pev = ctx.mm(pbD[:, 0:256],
                             wk_sb[ci][:, ci_out * 128:(ci_out + 1) * 128],
                             meansB[ci][:], start=(ci == 0), stop=(ci == 7),
                             ctr=pe)
            ctx.wait("vector", pe, pev)
            ctx.copy(QT[ci_out][:], pbC[:, 0:256])
            vev = ctx.copy(KT[ci_out][:], pbD[:, 0:256], ctr=ve)
            ctx.wait("tensor", ve, vev)
        for gt in range(2):
            for eb in range(2):
                ctx.wait("tensor", ve, ve.v)
                for ci in range(8):
                    pev = ctx.mm(pbG[:, 0:512],
                                 meansB[ci][:, gt * 128:(gt + 1) * 128],
                                 wv_sb[ci][:, eb * 512:(eb + 1) * 512],
                                 start=(ci == 0), stop=(ci == 7), ctr=pe)
                ctx.wait("vector", pe, pev)
                vev = ctx.copy(Vg[gt][:, eb * 512:(eb + 1) * 512],
                               pbG[:, 0:512], ctr=ve)
            vev = ctx.memset(Vg[gt][:, 1024:1025], 1.0, ctr=ve)
            ctx.wait("tensor", ve, vev)

        for h in range(16):
            for gt in range(2):
                psc = pbC if gt == 0 else pbD
                hc, hr = h // 2, (h % 2) * 64
                ctx.wait("tensor", se, se.v)
                ctx.wait("tensor", ve, ve.v)
                pev = ctx.mm(psc[:, 0:256],
                             KT[hc][hr:hr + 64, gt * 128:(gt + 1) * 128],
                             QT[hc][hr:hr + 64, :], start=True, stop=True,
                             ctr=pe)
                ctx.wait("scalar", pe, pev)
                ctx.wait("scalar", ve, ve.v)
                sev = ctx.act(exg[gt][:], psc[:, 0:256], AF.Exp, scale=0.125,
                              ctr=se)
                ctx.wait("vector", se, sev)
                vev = ctx.ts(exg[gt][:], exg[gt][:], selcol[gt][:], ALU.mult,
                             ctr=ve)
                ctx.wait("tensor", ve, vev)
            for gt in range(2):
                pev = ctx.mm(pbE[0:65, 0:256],
                             Vg[gt][:, h * 64:h * 64 + 64]
                             if False else Vg[gt][:, list(range(h * 64, h * 64 + 64)) + [1024]]
                             if False else Vg[gt][:, h * 64:h * 64 + 64],
                             exg[gt][:], start=(gt == 0), stop=False,
                             ctr=None) if False else None
            ctx.wait("tensor", ve, ve.v)
            for gt in range(2):
                ctx.mm(pbE[0:64, 0:256], Vg[gt][:, h * 64:h * 64 + 64],
                       exg[gt][:], start=(gt == 0), stop=(gt == 1))
                pev = ctx.mm(pbF[0:1, 0:256], Vg[gt][:, 1024:1025],
                             exg[gt][:], start=(gt == 0), stop=(gt == 1),
                             ctr=pe)
            ctx.wait("vector", pe, pev)
            vev = ctx.recip(rrow[0:1, :], pbF[0:1, 0:256], ctr=ve)
            ctx.wait("tensor", ve, vev)
            pev = ctx.mm(pbG[0:64, 0:256], ln.ones128f[0:1, 0:64],
                         rrow[0:1, :], ctr=pe)
            ctx.wait("vector", pe, pev)
            ctx.copy(rbg[:, :], pbG[0:64, 0:256])
            vev = ctx.tt(ctxT[h // 2][(h % 2) * 64:(h % 2) * 64 + 64, :],
                         pbE[0:64, 0:256], rbg[:, :], ALU.mult, ctr=ve)
            ctx.wait("tensor", ve, vev)

        for gt in range(2):
            for db in range(2):
                for ci in range(8):
                    pev = ctx.mm(pbC[:, 0:512],
                                 ctxT[ci][:, gt * 128:(gt + 1) * 128],
                                 wo_sb[ci][:, db * 512:(db + 1) * 512],
                                 start=(ci == 0), stop=(ci == 7), ctr=pe)
                ctx.wait("vector", pe, pev)
                vev = ctx.ts(og[gt][:, db * 512:(db + 1) * 512],
                             pbC[:, 0:512], selcol[gt][:], ALU.mult, ctr=ve)
                ctx.wait("tensor", ve, vev)

        st_hist = []
        n_out = 0
        for dc in range(8):
            for tb in range(4):
                par = n_out % 2
                if n_out >= 2:
                    ctx.wait("sync", st, st_hist[n_out - 2])
                rv = ctx.dma("sync", xres[par][:],
                             xq[dc * 128:(dc + 1) * 128,
                                tb * 512:(tb + 1) * 512], rld[par])
                for gt in range(2):
                    pev = ctx.mm(pbE[:, 0:512],
                                 og[gt][:, dc * 128:(dc + 1) * 128],
                                 rep_sb[gt][:, tb * 512:(tb + 1) * 512],
                                 start=(gt == 0), stop=(gt == 1), ctr=pe)
                ctx.wait("vector", pe, pev)
                ctx.wait("vector", rld[par], rv)
                if n_out >= 2:
                    ctx.wait("vector", st, st_hist[n_out - 2])
                vev = ctx.tt(oxs[par][:], pbE[:, 0:512], xres[par][:],
                             ALU.add, ctr=ve)
                ctx.wait("gpsimd", ve, vev)
                sv = ctx.dma("gpsimd",
                             oxT[dc * 128:(dc + 1) * 128,
                                 tb * 512:(tb + 1) * 512], oxs[par][:], st)
                st_hist.append(sv)
                n_out += 1
                ctx.wait("tensor", ve, ve.v)
        ctx.wait("gpsimd", st, st.v)

    return k.finish(script)


def run_gffn(xT_full, W, raw):
    nc = built("gffn", build_gffn)
    identf = np.eye(128, dtype=np.float32)
    g = 128 + np.arange(TK) // 16
    REP = (np.arange(256)[:, None] == g[None, :]).astype(BF)
    bs1 = c_(raw["L0_ffn_bs1"].astype(np.float32).reshape(256, 1))
    bs2 = c_(raw["L0_ffn_bs2"].astype(np.float32).reshape(1, 1))
    Ws1 = c_(raw["L0_ffn_Ws1"].astype(np.float32))
    Ws2 = c_(raw["L0_ffn_Ws2"].astype(np.float32))
    in_maps = []
    for c in range(NCORE):
        b, h = c // 2, c % 2
        own = xT_full[b][:, h * TK:(h + 1) * TK]
        other = xT_full[b][:, (1 - h) * TK:(2 - h) * TK]
        rot = np.concatenate([other, own], axis=1)
        in_maps.append(dict(xTb=c_(rot), xq=c_(own),
                            fWq=W["f_Wq"], fWk=W["f_Wk"], fWv=W["f_Wv"],
                            fWo=W["f_Wo"], Ws1=Ws1, bs1=bs1, Ws2=Ws2,
                            bs2=bs2, identf=identf, REP=REP))
    res = run8(nc, in_maps)
    out = np.empty_like(xT_full)
    for c in range(NCORE):
        b, h = c // 2, c % 2
        out[b][:, h * TK:(h + 1) * TK] = res.results[c]["oxT"]
    return out



G_EPS = float(np.finfo(np.float32).eps)
G_LINIT = 0.8 - 0.6 * float(np.exp(-0.3 * 0))
G_SCALE = 1.0 / float(np.sqrt(512))


def build_dattn():
    k = Kern("dattn")
    xq = k.inp("xq", [1024, TK], F32)
    WqT = k.inp("WqT", [1024, 1024], BF16)
    WkT = k.inp("WkT", [1024, 1024], BF16)
    WvT = k.inp("WvT", [1024, 1024], BF16)
    W1T = k.inp("W1T", [1024, 1024], BF16)
    W2T = k.inp("W2T", [1024, 1024], BF16)
    lam = k.inp("lam", [1, 4], F32)
    g1c = k.inp("g1c", [1024, 1], F32)
    g2r = k.inp("g2r", [1, 64], F32)
    g3full = k.inp("g3full", [128, 64], F32)
    identf = k.inp("identf", [128, 128], F32)
    maskq = k.inp("maskq", [128, 512], BF16)
    blkS = k.inp("blkS", [128, 8], F32)
    blkST = k.inp("blkST", [8, 128], F32)
    oxT = k.out("oxT", [1024, TK], F32)

    pb = [k.ps(f"pb{i}", [128, 512], F32) for i in range(7)]
    psb = k.ps("psb", [128, 256], BF16)
    pbA, pbB, pbC, pbD, pbE, pbF, pbG = pb

    x_blk = [k.sb(f"xb{i}", [128, 512], F32) for i in range(8)]
    xn_blk = [k.sb(f"xn{i}", [128, 1024], BF16) for i in range(8)]
    ln = LNBlock(k, TK, ps3=(pbA, pbB, pbC))
    wA = [k.sb(f"wA{i}", [128, 1024], BF16) for i in range(8)]
    wB = [k.sb(f"wB{i}", [128, 1024], BF16) for i in range(8)]
    wC = [k.sb(f"wC{i}", [128, 1024], BF16) for i in range(8)]
    Qb = k.sb("Qb", [128, 4096], BF16)
    Kb = k.sb("Kb", [128, 4096], BF16)
    Vb = k.sb("Vb", [128, 4096], BF16)
    QV = k.sb("QV", [128, 8192], BF16)
    KTb = k.sb("KTb", [128, 8192], BF16)
    Qs = QV[0:64, :]
    Vs = QV[64:128, :]
    Ks = KTb[0:64, :]
    Tbig = KTb[64:128, :]
    A3 = [k.sb(f"A3{i}", [128, 1024], BF16) for i in range(8)]
    A3b = [k.sb(f"A3b{i}", [128, 1024], BF16) for i in range(8)]
    sm1 = k.sb("sm1", [128, 512], BF16)
    sm2 = k.sb("sm2", [128, 512], BF16)
    vhf = k.sb("vhf", [128, 256], BF16)
    att_f = k.sb("att_f", [128, 256], F32)
    att2 = k.sb("att2", [128, 256], F32)
    att2b = k.sb("att2b", [128, 256], BF16)
    sq2 = k.sb("sq2", [128, 256], F32)
    rms4 = k.sb("rms4", [128, 4], F32)
    rstdx = k.sb("rstdx", [128, 4], F32)
    ssq8 = k.sb("ssq8", [8, 4], F32)
    lcol = k.sb("lcol", [128, 1], F32)
    g2b = k.sb("g2b", [128, 64], F32)
    g3t = k.sb("g3t", [128, 64], F32)
    epsc = k.sb("epsc", [128, 1], F32)
    ltmp = k.sb("ltmp", [1, 4], F32)
    xres = [k.sb(f"xr{i}", [128, 512], F32) for i in range(2)]
    oxs = [k.sb(f"oxsb{i}", [128, 512], F32) for i in range(2)]
    idf = k.sb("idf", [128, 128], F32)
    idb2 = k.sb("idb2", [128, 128], BF16)
    mq = k.sb("mq", [128, 512], BF16)
    blkS_sb = k.sb("blkS_sb", [128, 8], F32)
    blkST_sb = k.sb("blkST_sb", [8, 128], F32)
    g1sb = [k.sb(f"g1s{i}", [128, 1], F32) for i in range(8)]
    g2sb = k.sb("g2sb", [1, 64], F32)

    def hts(ci, tb):
        sl = slice(tb * 512, (tb + 1) * 512)
        if ci < 2:
            return Qb[:, ci * 2048:(ci + 1) * 2048][:, sl]
        if ci < 4:
            return Kb[:, (ci - 2) * 2048:(ci - 1) * 2048][:, sl]
        if ci < 6:
            return Vb[:, (ci - 4) * 2048:(ci - 3) * 2048][:, sl]
        return wC[(ci - 6) * 4 + tb][:, 0:512]

    def a3(ci):
        return (A3b[ci][:], A3[ci][:])

    ld = k.ctr("ld")
    xld = k.ctr("xld")
    sld = k.ctr("sld")
    rld = [k.ctr("rld0"), k.ctr("rld1")]
    pe = k.ctr("pe")
    ve = k.ctr("ve")
    se = k.ctr("se")
    st = k.ctr("st")

    def script(ctx):
        ctx.dma("sync", idf[:], identf, ld)
        ctx.dma("sync", mq[:], maskq, ld)
        ctx.dma("sync", blkS_sb[:], blkS, ld)
        ctx.dma("sync", blkST_sb[:], blkST, ld)
        ctx.dma("sync", ltmp[:], lam, ld)
        ctx.dma("sync", g2sb[:], g2r, ld)
        for i in range(8):
            ctx.dma("sync", g1sb[i][:], g1c[i * 128:(i + 1) * 128, :], ld)
        for i in range(8):
            ctx.dma("sync", wA[i][:], WqT[i * 128:(i + 1) * 128, :], ld)
            ctx.dma("sync", wB[i][:], WkT[i * 128:(i + 1) * 128, :], ld)
            ldv = ctx.dma("sync", wC[i][:], WvT[i * 128:(i + 1) * 128, :],
                          ld)
        ldv = ctx.dma("sync", g3t[:], g3full, ld)
        ctx.wait("vector", ld, ldv)
        ctx.copy(idb2[:], idf[:])
        ctx.memset(epsc[:], G_EPS)
        iniv = ln.init(ctx, ve, rms_eps=G_EPS)
        for eng in ("tensor", "scalar"):
            ctx.wait(eng, ve, iniv)
            ctx.wait(eng, ld, ldv)
        ctx.tt(ltmp[0:1, 0:1], ltmp[0:1, 0:1], ltmp[0:1, 1:2], ALU.mult)
        vev = ctx.tt(ltmp[0:1, 2:3], ltmp[0:1, 2:3], ltmp[0:1, 3:4], ALU.mult,
                     ctr=ve)
        ctx.wait("scalar", ve, vev)
        ctx.act(ltmp[0:1, 0:1], ltmp[0:1, 0:1], AF.Exp)
        sev = ctx.act(ltmp[0:1, 2:3], ltmp[0:1, 2:3], AF.Exp, ctr=se)
        ctx.wait("vector", se, sev)
        ctx.tt(ltmp[0:1, 0:1], ltmp[0:1, 0:1], ltmp[0:1, 2:3], ALU.subtract)
        vev = ctx.ts(ltmp[0:1, 0:1], ltmp[0:1, 0:1], G_LINIT, ALU.add, ctr=ve)
        ctx.wait("tensor", ve, vev)
        pev = ctx.mm(pbD[0:128, 0:1], ln.ones128f[0:1, :], ltmp[0:1, 0:1],
                     ctr=pe)
        ctx.wait("vector", pe, pev)
        vev = ctx.copy(lcol[:], pbD[0:128, 0:1], ctr=ve)
        ctx.wait("tensor", ve, vev)
        pev = ctx.mm(pbD[:, 0:64], ln.ones128f[0:1, :], g2sb[0:1, :], ctr=pe)
        ctx.wait("vector", pe, pev)
        vev = ctx.ts(g2b[:], pbD[:, 0:64], 1.0 - G_LINIT, ALU.mult, ctr=ve)
        ctx.wait("tensor", ve, vev)

        state = {"norm_v": 0, "blk_pe": 0}

        def pre_block(ctx, tb, t0, tw):
            if tb > 0:
                ctx.wait("sync", ln.ve, state["norm_v"])
                ctx.wait("sync", pe, state["blk_pe"])
            for ci in range(8):
                xv = ctx.dma("sync", x_blk[ci][:, 0:tw],
                             xq[ci * 128:(ci + 1) * 128, t0:t0 + tw], xld)
            for eng in ("vector", "scalar", "tensor"):
                ctx.wait(eng, xld, xv)

        def post_block(ctx, tb, t0, tw):
            state["norm_v"] = ln.ve.v
            ctx.wait("tensor", ln.ve, ln.ve.v)
            par = (t0 // 512) % 2
            xs = slice(par * 512, par * 512 + tw)
            for dst, wsb in ((Qb, wA), (Kb, wB), (Vb, wC)):
                for co in range(8):
                    ctx.wait("tensor", ve, ve.v)
                    for ci in range(8):
                        pev = ctx.mm(pbD[:, 0:512],
                                     wsb[ci][:, co * 128:(co + 1) * 128],
                                     xn_blk[ci][:, xs],
                                     start=(ci == 0), stop=(ci == 7), ctr=pe)
                    ctx.wait("vector", pe, pev)
                    vev = ctx.copy(dst[:, co * 512:(co + 1) * 512],
                                   pbD[:, 0:512], ctr=ve)
            ctx.wait("scalar", ve, ve.v)
            ctx.wait("vector", se, se.v)
            for dst in (Qb, Kb):
                for co in range(8):
                    sl = slice(co * 512, (co + 1) * 512)
                    vev = ctx.ts(att_f[:, 0:256].bitcast(BF16)[:, 0:512]
                                 if False else sq2[:, 0:256].bitcast(BF16)[:, 0:512],
                                 dst[:, sl], 0.0, ALU.min, ctr=ve)
                    ctx.wait("scalar", ve, vev)
                    sev = ctx.act(att2[:, 0:256].bitcast(BF16)[:, 0:512],
                                  sq2[:, 0:256].bitcast(BF16)[:, 0:512],
                                  AF.Exp, ctr=se)
                    ctx.wait("vector", se, sev)
                    ctx.stt(dst[:, sl], dst[:, sl], 0.0,
                            att2[:, 0:256].bitcast(BF16)[:, 0:512],
                            ALU.max, ALU.add)
                    vev = ctx.ts(dst[:, sl], dst[:, sl], 1.0, ALU.subtract,
                                 ctr=ve)
            ctx.wait("gpsimd", ve, ve.v)
            for h in range(16):
                par2, hh = h % 2, h // 2
                for dstT, src, r0 in (
                        (QV[0:32, :], Qb, 0), (QV[32:64, :], Qb, 32),
                        (KTb[0:32, :], Kb, 0), (KTb[32:64, :], Kb, 32)):
                    sv = ctx.dma(
                        "gpsimd",
                        dstT.rearrange("p (g c) -> p g c", c=128)[
                            :, :, h * 8:(h + 1) * 8],
                        src[par2 * 64 + r0:par2 * 64 + r0 + 32,
                            hh * 512:(hh + 1) * 512].rearrange(
                            "p (g s) -> p g s", s=8),
                        sld)
                sv = ctx.dma(
                    "gpsimd",
                    Vs.rearrange("p (g c) -> p g c", c=128)[
                        :, :, h * 8:(h + 1) * 8],
                    Vb[par2 * 64:par2 * 64 + 64,
                       hh * 512:(hh + 1) * 512].rearrange(
                        "p (g s) -> p g s", s=8),
                    sld)
            ctx.wait("tensor", sld, sv)

            for q4 in range(16):
                ctx.wait("tensor", ve, ve.v)
                ctx.wait("tensor", se, se.v)
                for j in range(4):
                    g = q4 * 4 + j
                    gap = lambda src, r0: src[r0:r0 + 32,
                                              g * 128:(g + 1) * 128]
                    ctx.mm(pbD[:, j * 128:(j + 1) * 128], gap(KTb, 0),
                           gap(QV, 0), start=True, stop=True)
                    ctx.mm(pbE[:, j * 128:(j + 1) * 128], gap(KTb, 32),
                           gap(QV, 32), start=True, stop=True)
                    vap = Vs[:, g * 128:(g + 1) * 128]
                    pev = ctx.trans(psb[:, j * 64:(j + 1) * 64], vap,
                                    idb2[64:128, 64:128], ctr=pe)
                ctx.wait("vector", pe, pev)
                ctx.tt(sm1[:], pbD[:, 0:512], mq[:], ALU.mult)
                ctx.tt(sm2[:], pbE[:, 0:512], mq[:], ALU.mult)
                vev = ctx.copy(vhf[:], psb[:, 0:256], ctr=ve)
                ctx.wait("tensor", ve, vev)
                for j in range(4):
                    ctx.mm(pbF[:, j * 64:(j + 1) * 64],
                           sm1[:, j * 128:(j + 1) * 128],
                           vhf[:, j * 64:(j + 1) * 64], start=True, stop=True)
                    pev = ctx.mm(pbG[:, j * 64:(j + 1) * 64],
                                 sm2[:, j * 128:(j + 1) * 128],
                                 vhf[:, j * 64:(j + 1) * 64], start=True,
                                 stop=True, ctr=pe)
                ctx.wait("vector", pe, pev)
                ctx.ts(att_f[:], pbG[:, 0:256], lcol[:], ALU.mult)
                ctx.tt(att_f[:], pbF[:, 0:256], att_f[:], ALU.subtract)
                vev = ctx.ts(att_f[:], att_f[:], G_SCALE, ALU.mult, ctr=ve)
                ctx.tt(sq2[:], att_f[:], att_f[:], ALU.mult)
                ctx.red(rms4[:], sq2[:].rearrange("p (j d) -> p j d", d=64),
                        ALU.add)
                vev = ctx.ts(rms4[:], rms4[:], 1.0 / 64, ALU.mult, ctr=ve)
                ctx.wait("scalar", ve, vev)
                sev = ctx.act(rms4[:], rms4[:], AF.Sqrt, bias=epsc[:], ctr=se)
                ctx.wait("vector", se, sev)
                vev = ctx.recip(rms4[:], rms4[:], ctr=ve)
                for j in range(4):
                    ctx.ts(att2[:, j * 64:(j + 1) * 64],
                           att_f[:, j * 64:(j + 1) * 64],
                           rms4[:, j:j + 1], ALU.mult)
                    vev = ctx.tt(att2[:, j * 64:(j + 1) * 64],
                                 att2[:, j * 64:(j + 1) * 64], g2b[:],
                                 ALU.mult, ctr=ve)
                vev = ctx.tt(sq2[:], att2[:], att2[:], ALU.mult, ctr=ve)
                ctx.wait("tensor", ve, vev)
                pev = ctx.mm(pbD[0:8, 0:256], blkS_sb[:], sq2[:], ctr=pe)
                ctx.wait("vector", pe, pev)
                ctx.red(ssq8[:], pbD[0:8, 0:256].rearrange(
                    "p (j d) -> p j d", d=64), ALU.add)
                vev = ctx.ts(ssq8[:], ssq8[:], 1.0 / 1024, ALU.mult, ctr=ve)
                ctx.wait("scalar", ve, vev)
                sev = ctx.act(ssq8[:], ssq8[:], AF.Sqrt, bias=epsc[0:8, :],
                              ctr=se)
                ctx.wait("vector", se, sev)
                vev = ctx.recip(ssq8[:], ssq8[:], ctr=ve)
                ctx.wait("tensor", ve, vev)
                pev = ctx.mm(pbD[0:128, 0:4], blkST_sb[:], ssq8[:], ctr=pe)
                ctx.wait("vector", pe, pev)
                vev = ctx.copy(rstdx[:], pbD[0:128, 0:4], ctr=ve)
                for j in range(4):
                    ctx.ts(sq2[:, j * 64:(j + 1) * 64],
                           att2[:, j * 64:(j + 1) * 64],
                           rstdx[:, j:j + 1], ALU.mult)
                ctx.tt(sq2[:], sq2[:], g3t[:].rearrange(
                    "p d -> p () d").broadcast_to([128, 4, 64])
                    if False else sq2[:], ALU.bypass) if False else None
                for j in range(4):
                    ctx.tt(sq2[:, j * 64:(j + 1) * 64],
                           sq2[:, j * 64:(j + 1) * 64], g3t[:], ALU.mult)
                ctx.tt(att2[:], sq2[:], att2[:], ALU.add)
                vev = ctx.copy(att2b[:], att2[:], ctr=ve)
                ctx.wait("tensor", ve, vev)
                for j in range(4):
                    g = q4 * 4 + j
                    pev = ctx.trans(psb[0:64, 0:128],
                                    att2b[:, j * 64:(j + 1) * 64],
                                    idb2[:], ctr=pe)
                    ctx.wait("vector", pe, pev)
                    vev = ctx.copy(Tbig[:, g * 128:(g + 1) * 128],
                                   psb[0:64, 0:128], ctr=ve)
                    ctx.wait("tensor", ve, vev)
            ctx.wait("gpsimd", ve, ve.v)
            for h in range(16):
                lo, hi = a3(h // 2)
                dst = (lo if t0 < 1024 else hi)
                doff = (t0 % 1024)
                sv = ctx.dma(
                    "gpsimd",
                    dst[(h % 2) * 64:(h % 2) * 64 + 64, doff:doff + 512]
                    .rearrange("p (g s) -> p g s", s=8),
                    Tbig.rearrange("p (g c) -> p g c", c=128)[
                        :, :, h * 8:(h + 1) * 8],
                    sld)
            ctx.wait("tensor", sld, sv)
            state["blk_pe"] = pe.v

        emit_ln(ctx, ln,
                x_at=lambda ci, t0, tw: x_blk[ci][:, 0:tw],
                xn_at=lambda ci, t0, tw: xn_blk[ci][
                    :, ((t0 // 512) % 2) * 512:((t0 // 512) % 2) * 512 + tw],
                wcol=g1sb,
                rms_eps=G_EPS,
                pre_block=pre_block, post_block=post_block)

        ctx.wait("sync", pe, pe.v)
        for i in range(8):
            ctx.dma("sync", wA[i][:], W1T[i * 128:(i + 1) * 128, :], ld)
            ldv = ctx.dma("sync", wB[i][:], W2T[i * 128:(i + 1) * 128, :],
                          ld)
        ctx.wait("tensor", ld, ldv)
        for co in range(8):
            for tb in range(4):
                ctx.wait("tensor", se, se.v)
                for ci in range(8):
                    lo, hi = a3(ci)
                    src = lo if tb < 2 else hi
                    pev = ctx.mm(pbD[:, 0:512],
                                 wA[ci][:, co * 128:(co + 1) * 128],
                                 src[:, (tb % 2) * 512:(tb % 2) * 512 + 512],
                                 start=(ci == 0), stop=(ci == 7), ctr=pe)
                ctx.wait("scalar", pe, pev)
                sev = ctx.act(hts(co, tb), pbD[:, 0:512], AF.Gelu, ctr=se)
        ctx.wait("tensor", se, se.v)
        st_hist = []
        n_out = 0
        for co in range(8):
            for tb in range(4):
                par = n_out % 2
                if n_out >= 2:
                    ctx.wait("sync", st, st_hist[n_out - 2])
                rv = ctx.dma("sync", xres[par][:],
                             xq[co * 128:(co + 1) * 128,
                                tb * 512:(tb + 1) * 512], rld[par])
                ctx.wait("tensor", ve, ve.v)
                for ci in range(8):
                    pev = ctx.mm(pbE[:, 0:512],
                                 wB[ci][:, co * 128:(co + 1) * 128],
                                 hts(ci, tb),
                                 start=(ci == 0), stop=(ci == 7), ctr=pe)
                ctx.wait("vector", pe, pev)
                ctx.wait("vector", rld[par], rv)
                if n_out >= 2:
                    ctx.wait("vector", st, st_hist[n_out - 2])
                lo, hi = a3(co)
                src = lo if tb < 2 else hi
                ctx.tt(oxs[par][:], pbE[:, 0:512],
                       src[:, (tb % 2) * 512:(tb % 2) * 512 + 512], ALU.add)
                vev = ctx.tt(oxs[par][:], oxs[par][:], xres[par][:], ALU.add,
                             ctr=ve)
                ctx.wait("gpsimd", ve, vev)
                sv = ctx.dma("gpsimd",
                             oxT[co * 128:(co + 1) * 128,
                                 tb * 512:(tb + 1) * 512], oxs[par][:], st)
                st_hist.append(sv)
                n_out += 1
        ctx.wait("gpsimd", st, st.v)

    return k.finish(script)


def run_dattn(xT_full, W, raw):
    nc = built("dattn", build_dattn)
    identf = np.eye(128, dtype=np.float32)
    s = np.arange(128) % 8
    maskq = np.tile((s[:, None] == s[None, :]), (1, 4)).astype(BF)
    blkS = (np.arange(128)[:, None] % 8 == np.arange(8)[None, :]) \
        .astype(np.float32)
    blkST = c_(blkS.T)
    lam = c_(raw["L1_attn_b_lambda_params"].astype(np.float32).reshape(1, 4))
    g1c = c_(raw["L1_attn_b_g1"].astype(np.float32).reshape(1024, 1))
    g2r = c_(raw["L1_attn_b_g2"].astype(np.float32).reshape(1, 64))
    g3f = raw["L1_attn_b_g3"].astype(np.float32).reshape(16, 64)
    g3full = c_(np.repeat(g3f, 8, axis=0))
    in_maps = []
    for c in range(NCORE):
        b, h = c // 2, c % 2
        own = xT_full[b][:, h * TK:(h + 1) * TK]
        in_maps.append(dict(xq=c_(own), WqT=W["g_WqT"], WkT=W["g_WkT"],
                            WvT=W["g_WvT"], W1T=W["g_W1T"], W2T=W["g_W2T"],
                            lam=lam, g1c=g1c, g2r=g2r, g3full=g3full,
                            identf=identf, maskq=c_(maskq), blkS=blkS,
                            blkST=blkST))
    res = run8(nc, in_maps)
    out = np.empty_like(xT_full)
    for c in range(NCORE):
        b, h = c // 2, c % 2
        out[b][:, h * TK:(h + 1) * TK] = res.results[c]["oxT"]
    return out



def build_e1():
    k = Kern("e1")
    XH = 2051
    xh = k.inp("xh", [1024, XH], F32)
    winS = k.inp("winS", [65 * 128, 1024], BF16)
    nwc = k.inp("nwc", [1024, 1], F32)
    convw = k.inp("convw", [2048, 4], F32)
    convb = k.inp("convb", [2048, 1], F32)
    alog = k.inp("alog", [16, 1], F32)
    dtb = k.inp("dtb", [16, 1], F32)
    o_va = k.out("o_va", [2048, TK], BF16)
    o_sg = k.out("o_sg", [2048, TK], BF16)
    o_bc = k.out("o_bc", [4096, TK], BF16)
    o_ld = k.out("o_ld", [16, TK], F32)
    o_xa = k.out("o_xa", [1024, TK], F32)

    pb = [k.ps(f"pb{i}", [128, 512], F32) for i in range(7)]
    pbA, pbB, pbC, pbD, pbE, pbF, pbG = pb
    x_blk = [k.sb(f"xb{i}", [128, 512], F32) for i in range(8)]
    xnT = [k.sb(f"xn{i}", [128, XH], BF16) for i in range(8)]
    ln = LNBlock(k, XH, ps3=(pbA, pbB, pbC), keep_s=True)
    xat = [k.sb(f"xat{i}", [128, 512], F32) for i in range(2)]
    slab = [k.sb(f"sl{i}", [128, 1024], BF16) for i in range(2)]
    valT = [k.sb(f"vt{i}", [128, XH], BF16) for i in range(16)]
    nw_sb = [k.sb(f"nw{i}", [128, 1], F32) for i in range(8)]
    cw_sb = [k.sb(f"cw{i}", [128, 4], F32) for i in range(16)]
    cb_sb = [k.sb(f"cb{i}", [128, 1], F32) for i in range(16)]
    al_sb = k.sb("al_sb", [16, 1], F32)
    dtb_sb = k.sb("dtb_sb", [16, 1], F32)
    nal = k.sb("nal", [16, 1], F32)
    dtt = k.sb("dtt", [16, XH], F32)
    ldt = k.sb("ldt", [16, TK], F32)
    acc = k.sb("acc", [128, 512], F32)
    otile = [k.sb(f"ot{i}", [128, 512], BF16) for i in range(2)]
    t16 = k.sb("t16", [16, 512], F32)

    ld = k.ctr("ld")
    xld = k.ctr("xld")
    wld = [k.ctr("wld0"), k.ctr("wld1")]
    pe = k.ctr("pe")
    ve = k.ctr("ve")
    se = k.ctr("se")
    st = k.ctr("st")

    NBLK = cdiv(XH, 512)

    def script(ctx):
        for i in range(8):
            ctx.dma("sync", nw_sb[i][:], nwc[i * 128:(i + 1) * 128, :], ld)
        for i in range(16):
            ctx.dma("sync", cw_sb[i][:], convw[i * 128:(i + 1) * 128, :], ld)
            ctx.dma("sync", cb_sb[i][:], convb[i * 128:(i + 1) * 128, :], ld)
        ctx.dma("sync", al_sb[:], alog, ld)
        ldv = ctx.dma("sync", dtb_sb[:], dtb, ld)
        ctx.wait("vector", ld, ldv)
        iniv = ln.init(ctx, ve, rms_eps=1e-6)
        for eng in ("tensor", "scalar"):
            ctx.wait(eng, ve, iniv)
            ctx.wait(eng, ld, ldv)
        sev = ctx.act(nal[:], al_sb[:], AF.Exp, ctr=se)
        ctx.wait("vector", se, sev)
        vev = ctx.ts(nal[:], nal[:], -1.0, ALU.mult, ctr=ve)

        state = {"norm_v": 0}
        xa_store = {0: None, 1: None}

        def pre_block(ctx, tb, t0, tw):
            if tb > 0:
                ctx.wait("sync", ln.ve, state["norm_v"])
            for ci in range(8):
                xv = ctx.dma("sync", x_blk[ci][:, 0:tw],
                             xh[ci * 128:(ci + 1) * 128, t0:t0 + tw], xld)
            for eng in ("vector", "scalar", "tensor"):
                ctx.wait(eng, xld, xv)

        def post_block(ctx, tb, t0, tw):
            r5 = ln.row[0:1, 5 * 512:5 * 512 + tw]
            ctx.wait("tensor", ln.ve, ln.ve.v)
            pev = ctx.mm(ln.ps_b2[:, 0:tw], ln.ones128f[0:1, :], r5,
                         ctr=ln.pe)
            ctx.wait("vector", ln.pe, pev)
            lo = max(t0, 3)
            hi = min(t0 + tw, XH)
            for ci in range(8):
                par = ci % 2
                if xa_store[par] is not None:
                    ctx.wait("vector", st, xa_store[par])
                ctx.tt(acc[:, 0:tw], x_blk[ci][:, 0:tw], ln.ps_b1[:, 0:tw],
                       ALU.subtract)
                ctx.tt(acc[:, 0:tw], acc[:, 0:tw], ln.ps_b2[:, 0:tw],
                       ALU.mult)
                vev = ctx.tt(xat[par][:, 0:tw], acc[:, 0:tw],
                             x_blk[ci][:, 0:tw], ALU.add, ctr=ln.ve)
                ctx.wait("gpsimd", ln.ve, vev)
                sv = ctx.dma("gpsimd",
                             o_xa[ci * 128:(ci + 1) * 128, lo - 3:hi - 3],
                             xat[par][:, lo - t0:hi - t0], st)
                xa_store[par] = sv
            state["norm_v"] = ln.ve.v

        emit_ln(ctx, ln,
                x_at=lambda ci, t0, tw: x_blk[ci][:, 0:tw],
                xn_at=lambda ci, t0, tw: xnT[ci][:, t0:t0 + tw],
                wcol=nw_sb, rms_eps=1e-6,
                pre_block=pre_block, post_block=post_block, keep_s=True)
        ln_done = ln.ve.v

        sv0 = ctx.dma("sync", slab[0][:], winS[0:128, :], wld[0])
        wv = {0: (wld[0], sv0)}
        for eng in ("tensor",):
            ctx.wait(eng, ln.ve, ln_done)
        ot_store = {0: None, 1: None}
        for pc in range(65):
            if pc + 1 < 65:
                ctx.wait("sync", pe, pe.v)
                wl = wld[(pc + 1) % 2]
                svn = ctx.dma("sync", slab[(pc + 1) % 2][:],
                              winS[(pc + 1) * 128:(pc + 2) * 128, :], wl)
                wv[pc + 1] = (wl, svn)
            ctx.wait("tensor", wv[pc][0], wv[pc][1])
            pw = 128 if pc < 64 else 16
            for tb in range(NBLK):
                t0 = tb * 512
                tw = min(512, XH - t0)
                pp = pbD if tb % 2 == 0 else pbE
                ctx.wait("tensor", ve, ve.v)
                ctx.wait("tensor", se, se.v)
                for ci in range(8):
                    pev = ctx.mm(pp[0:pw, 0:tw],
                                 slab[pc % 2][:, ci * 128:ci * 128 + pw],
                                 xnT[ci][:, t0:t0 + tw],
                                 start=(ci == 0), stop=(ci == 7), ctr=pe)
                if pc < 16:
                    ctx.wait("vector", pe, pev)
                    vev = ctx.copy(valT[pc][:, t0:t0 + tw], pp[:, 0:tw],
                                   ctr=ve)
                elif pc < 32:
                    ctx.wait("scalar", pe, pev)
                    if ot_store[tb % 2] is not None:
                        ctx.wait("scalar", st, ot_store[tb % 2])
                    sev = ctx.act(otile[tb % 2][:, 0:tw], pp[:, 0:tw],
                                  AF.Silu, ctr=se)
                    ctx.wait("gpsimd", se, sev)
                    lo = max(t0, 3)
                    hi = min(t0 + tw, XH)
                    if hi > lo:
                        sv = ctx.dma("gpsimd",
                                     o_sg[(pc - 16) * 128:(pc - 15) * 128,
                                          lo - 3:hi - 3],
                                     otile[tb % 2][:, lo - t0:hi - t0], st)
                        ot_store[tb % 2] = sv
                elif pc < 64:
                    ctx.wait("vector", pe, pev)
                    if ot_store[tb % 2] is not None:
                        ctx.wait("vector", st, ot_store[tb % 2])
                    vev = ctx.copy(otile[tb % 2][:, 0:tw], pp[:, 0:tw],
                                   ctr=ve)
                    ctx.wait("gpsimd", ve, vev)
                    lo = max(t0, 3)
                    hi = min(t0 + tw, XH)
                    if hi > lo:
                        sv = ctx.dma("gpsimd",
                                     o_bc[(pc - 32) * 128:(pc - 31) * 128,
                                          lo - 3:hi - 3],
                                     otile[tb % 2][:, lo - t0:hi - t0], st)
                        ot_store[tb % 2] = sv
                else:
                    ctx.wait("vector", pe, pev)
                    vev = ctx.copy(dtt[:, t0:t0 + tw], pp[0:16, 0:tw],
                                   ctr=ve)

        ctx.wait("vector", pe, pe.v)
        for tb in range(4):
            sl = slice(3 + tb * 512, 3 + (tb + 1) * 512)
            ol = slice(tb * 512, (tb + 1) * 512)
            vev = ctx.ts(dtt[:, sl], dtt[:, sl], dtb_sb[:], ALU.add, ctr=ve)
            ctx.ts(t16[:], dtt[:, sl], -1.0, ALU.mult)
            vev = ctx.tt(t16[:], t16[:], dtt[:, sl], ALU.min, ctr=ve)
            ctx.wait("scalar", ve, vev)
            ctx.act(t16[:], t16[:], AF.Exp)
            sev = ctx.act(t16[:], t16[:], AF.Ln, bias=ln.ones_col[0:16, :],
                          ctr=se)
            ctx.wait("vector", se, sev)
            ctx.ts(ldt[:, ol], dtt[:, sl], 0.0, ALU.max)
            ctx.tt(ldt[:, ol], ldt[:, ol], t16[:], ALU.add)
            vev = ctx.ts(ldt[:, ol], ldt[:, ol], nal[:], ALU.mult, ctr=ve)
            ctx.wait("gpsimd", ve, vev)
            sv = ctx.dma("gpsimd", o_ld[:, ol], ldt[:, ol], st)

        for cc in range(16):
            for tb in range(4):
                sl0 = 3 + tb * 512
                par = (cc * 4 + tb) % 2
                ctx.wait("vector", se, se.v)
                if ot_store[par] is not None:
                    ctx.wait("scalar", st, ot_store[par])
                ctx.ts(acc[:], valT[cc][:, sl0 - 3:sl0 + 509], cw_sb[cc][:, 0:1],
                       ALU.mult)
                for kk in range(1, 4):
                    ctx.stt(acc[:], valT[cc][:, sl0 - 3 + kk:sl0 + 509 + kk],
                            cw_sb[cc][:, kk:kk + 1], acc[:], ALU.mult,
                            ALU.add)
                vev = ctx.op("vector", lambda: ctx.nc.vector.tensor_copy(
                    acc[:], acc[:]), ctr=ve) if False else None
                vev = ctx.ts(acc[:], acc[:], 1.0, ALU.mult, ctr=ve)
                ctx.wait("scalar", ve, vev)
                sev = ctx.act(otile[par][:], acc[:], AF.Silu,
                              bias=cb_sb[cc][:], ctr=se)
                ctx.wait("gpsimd", se, sev)
                sv = ctx.dma("gpsimd",
                             o_va[cc * 128:(cc + 1) * 128,
                                  tb * 512:(tb + 1) * 512], otile[par][:],
                             st)
                ot_store[par] = sv
        ctx.wait("gpsimd", st, st.v)

    return k.finish(script)


def build_e2():
    k = Kern("e2")
    vA = k.inp("vA", [1024, T], BF16)
    BT = k.inp("BT", [1024, T], BF16)
    CT = k.inp("CT", [1024, T], BF16)
    ldr = k.inp("ldr", [8, T], F32)
    dsk = k.inp("dsk", [8, 128], F32)
    identf = k.inp("identf", [128, 128], F32)
    trilT = k.inp("trilT", [256, 256], BF16)
    o_yT = k.out("o_yT", [1024, T], BF16)

    pb = [k.ps(f"pb{i}", [128, 512], F32) for i in range(7)]
    psb = k.ps("psb", [128, 256], BF16)
    pbA, pbB, pbC, pbD, pbE, pbF, pbG = pb

    vsl = [k.sb(f"vsl{i}", [128, T], BF16) for i in range(8)]
    bsl = [k.sb(f"bsl{i}", [128, T], BF16) for i in range(8)]
    cstr = [k.sb(f"cst{i}", [128, 256], BF16) for i in range(2)]
    ldst = [k.sb(f"ldst{i}", [1, 256], F32) for i in range(2)]
    dsk_sb = k.sb("dsk_sb", [8, 128], F32)
    dvc = [k.sb(f"dvc{i}", [128, 1], F32) for i in range(8)]
    tril_sb = [k.sb(f"tr{i}", [128, 256], BF16) for i in range(2)]
    idf = k.sb("idf", [128, 128], F32)
    idb2 = k.sb("idb2", [128, 128], BF16)
    onesf = k.sb("onesf", [1, 128], F32)
    hst = [k.sb(f"hst{i}", [128, 128], F32) for i in range(8)]
    hbf = [k.sb(f"hbf{i}", [128, 128], BF16) for i in range(8)]
    cum = k.sb("cum", [1, 256], F32)
    ecr = k.sb("ecr", [1, 256], F32)
    zrow = k.sb("zrow", [1, 256], F32)
    cumc = [k.sb(f"cumc{i}", [128, 1], F32) for i in range(2)]
    ddt = [k.sb(f"ddt{i}", [128, 256], F32) for i in range(2)]
    lmat = [k.sb(f"lm{i}", [128, 256], BF16) for i in range(2)]
    ecol = [k.sb(f"ec{i}", [128, 1], F32) for i in range(2)]
    elast = k.sb("elast", [128, 1], F32)
    vtok = [k.sb(f"vtk{i}", [128, 128], BF16) for i in range(2)]
    btok = [k.sb(f"btk{i}", [128, 128], BF16) for i in range(2)]
    bw = [k.sb(f"bw{i}", [128, 128], BF16) for i in range(2)]
    ctsc = k.sb("ctsc", [128, 256], BF16)
    ysb = k.sb("ysb", [128, 256], F32)
    ytmp = k.sb("ytmp", [128, 256], F32)
    ybf = [k.sb(f"ybf{i}", [128, 256], BF16) for i in range(2)]
    rtmp = k.sb("rtmp", [1, 128], F32)

    ld = k.ctr("ld")
    dmv = k.ctr("dmv")
    pe = k.ctr("pe")
    ve = k.ctr("ve")
    se = k.ctr("se")
    st = k.ctr("st")
    cld = [k.ctr("cld0"), k.ctr("cld1")]

    def script(ctx):
        for i in range(8):
            ctx.dma("sync", vsl[i][:], vA[i * 128:(i + 1) * 128, :], ld)
            ctx.dma("sync", bsl[i][:], BT[i * 128:(i + 1) * 128, :], ld)
        ctx.dma("sync", dsk_sb[:], dsk, ld)
        ctx.dma("sync", idf[:], identf, ld)
        for i in range(2):
            ldv = ctx.dma("sync", tril_sb[i][:],
                          trilT[i * 128:(i + 1) * 128, :], ld)
        ctx.wait("vector", ld, ldv)
        ctx.memset(zrow[:], 0.0)
        ctx.memset(onesf[:], 1.0)
        vev = ctx.copy(idb2[:], idf[:], ctr=ve)
        for eng in ("tensor", "scalar", "gpsimd"):
            ctx.wait(eng, ve, vev)
            ctx.wait(eng, ld, ldv)
        for u in range(8):
            dv = ctx.dma("gpsimd", rtmp[0:1, :], dsk_sb[u:u + 1, :], dmv)
            ctx.wait("tensor", dmv, dv)
            ctx.wait("tensor", ve, ve.v)
            pev = ctx.trans(pbD[0:128, 0:1], rtmp[0:1, :], idf[0:1, 0:1],
                            ctr=pe)
            ctx.wait("vector", pe, pev)
            vev = ctx.copy(dvc[u][:], pbD[0:128, 0:1], ctr=ve)
            ctx.wait("gpsimd", ve, vev)
        for u in range(8):
            ctx.memset(hst[u][:], 0.0)
            vev = ctx.memset(hbf[u][:], 0.0, ctr=ve)
        ctx.wait("tensor", ve, ve.v)

        items = [(c, u) for c in range(16) for u in range(8)]
        cw = {}

        def loadC(n):
            c_, u_ = items[n]
            wl = cld[n % 2]
            ctx.dma("sync", ldst[n % 2][:],
                    ldr[u_:u_ + 1, c_ * 256:(c_ + 1) * 256], wl)
            cw[n] = (wl, ctx.dma("sync", cstr[n % 2][:],
                                 CT[u_ * 128:(u_ + 1) * 128,
                                    c_ * 256:(c_ + 1) * 256], wl))
        loadC(0)
        yb_store = {0: None, 1: None}
        for c in range(16):
            c0 = c * 256
            for u in range(8):
                nit = c * 8 + u
                if nit + 1 < len(items):
                    ctx.wait("sync", pe, pe.v)
                    loadC(nit + 1)
                csl_u = cstr[nit % 2]
                ctx.wait("tensor", cw[nit][0], cw[nit][1])
                ctx.wait("vector", cw[nit][0], cw[nit][1])
                ctx.wait("vector", pe, pe.v)
                vev = ctx.scan(cum[:], ldst[nit % 2][0:1, :], zrow[:],
                               0.0, ALU.add, ALU.add, ctr=ve)
                ctx.wait("tensor", ve, vev)
                for j in range(2):
                    pev = ctx.trans(pbD[0:128, 0:1],
                                    cum[0:1, j * 128:(j + 1) * 128],
                                    idf[0:1, 0:1], ctr=pe)
                    ctx.wait("vector", pe, pev)
                    vev = ctx.copy(cumc[j][:], pbD[0:128, 0:1], ctr=ve)
                    ctx.wait("tensor", ve, vev)
                pev = ctx.mm(pbE[:, 0:256], onesf[0:1, :], cum[0:1, :],
                             ctr=pe)
                ctx.wait("vector", pe, pev)
                for j in range(2):
                    ctx.ts(ddt[j][:], pbE[:, 0:256], cumc[j][:],
                           ALU.subtract)
                    ctx.ts(ddt[j][:], ddt[j][:], 0.0, ALU.min)
                vev = ctx.op("vector", lambda: ctx.nc.vector.tensor_copy(
                    ecol[0][:], ddt[0][:, 255:256]), ctr=ve)
                ctx.wait("scalar", ve, ve.v)
                for j in range(2):
                    sev = ctx.act(ddt[j][:], ddt[j][:], AF.Exp, ctr=se)
                ctx.wait("vector", se, sev)
                for j in range(2):
                    ctx.copy(ecol[j][:], ddt[j][:, 255:256])
                    vev = ctx.tt(lmat[j][:], ddt[j][:], tril_sb[j][:],
                                 ALU.mult, ctr=ve)
                ctx.wait("tensor", ve, vev)
                for j in range(2):
                    pev = ctx.trans(psb[:, 0:128],
                                    vsl[u][:, c0 + j * 128:c0 + (j + 1) * 128],
                                    idb2[:], ctr=pe)
                    ctx.wait("vector", pe, pev)
                    vev = ctx.copy(vtok[j][:], psb[:, 0:128], ctr=ve)
                    ctx.wait("tensor", ve, vev)
                    pev = ctx.trans(psb[:, 0:128],
                                    bsl[u][:, c0 + j * 128:c0 + (j + 1) * 128],
                                    idb2[:], ctr=pe)
                    ctx.wait("vector", pe, pev)
                    vev = ctx.copy(btok[j][:], psb[:, 0:128], ctr=ve)
                    ctx.wait("tensor", ve, vev)
                for j in range(2):
                    pev = ctx.mm(pbF[:, 0:256],
                                 bsl[u][:, c0 + j * 128:c0 + (j + 1) * 128],
                                 csl_u[:, 0:256], ctr=pe)
                    ctx.wait("vector", pe, pev)
                    vev = ctx.tt(lmat[j][:], lmat[j][:], pbF[:, 0:256],
                                 ALU.mult, ctr=ve)
                    ctx.wait("tensor", ve, vev)
                sev = ctx.act(ecr[:], cum[0:1, :], AF.Exp, ctr=se)
                ctx.wait("tensor", se, sev)
                pev = ctx.mm(pbE[:, 0:256], onesf[0:1, :], ecr[0:1, :],
                             ctr=pe)
                ctx.wait("vector", pe, pev)
                vev = ctx.tt(ctsc[:], csl_u[:, 0:256], pbE[:, 0:256],
                             ALU.mult, ctr=ve)
                ctx.wait("tensor", ve, vev)
                pev = ctx.mm(pbD[0:128, 0:1], onesf[0:1, :],
                             ecr[0:1, 255:256], ctr=pe)
                ctx.wait("vector", pe, pev)
                vev = ctx.copy(elast[:], pbD[0:128, 0:1], ctr=ve)
                ctx.wait("tensor", ve, ve.v)
                for j in range(2):
                    ctx.mm(pbF[:, 0:256], vtok[j][:], lmat[j][:],
                           start=(j == 0), stop=(j == 1))
                ctx.mm(pbG[:, 0:256], hbf[u][:], ctsc[:])
                for j in range(2):
                    vev = ctx.ts(bw[j][:], btok[j][:], ecol[j][:], ALU.mult,
                                 ctr=ve)
                ctx.wait("tensor", ve, vev)
                for j in range(2):
                    pev = ctx.mm(pbE[:, 0:128], bw[j][:], vtok[j][:],
                                 start=(j == 0), stop=(j == 1), ctr=pe)
                ctx.wait("vector", pe, pe.v)
                ctx.copy(ysb[:], pbG[:, 0:256])
                ctx.tt(ysb[:], ysb[:], pbF[:, 0:256], ALU.add)
                ctx.ts(ytmp[:], vsl[u][:, c0:c0 + 256], dvc[u][:], ALU.mult)
                ctx.tt(ysb[:], ysb[:], ytmp[:], ALU.add)
                par = (c * 8 + u) % 2
                if yb_store[par] is not None:
                    ctx.wait("vector", st, yb_store[par])
                vev = ctx.copy(ybf[par][:], ysb[:], ctr=ve)
                ctx.wait("gpsimd", ve, vev)
                sv = ctx.dma("gpsimd", o_yT[u * 128:(u + 1) * 128,
                                            c0:c0 + 256], ybf[par][:], st)
                yb_store[par] = sv
                ctx.ts(hst[u][:], hst[u][:], elast[:], ALU.mult)
                ctx.tt(hst[u][:], hst[u][:], pbE[:, 0:128], ALU.add)
                vev = ctx.copy(hbf[u][:], hst[u][:], ctr=ve)
                ctx.wait("tensor", ve, ve.v)
        ctx.wait("gpsimd", st, st.v)

    return k.finish(script)


def build_e3():
    k = Kern("e3")
    YT = k.inp("YT", [2048, TK], BF16)
    SG = k.inp("SG", [2048, TK], BF16)
    woT = k.inp("woT", [2048, 1024], BF16)
    xq = k.inp("xq", [1024, TK], F32)
    identf = k.inp("identf", [128, 128], F32)
    o_outT = k.out("o_outT", [1024, TK], BF16)
    o_otok = k.out("o_otok", [TK, 1024], BF16)
    o_x45 = k.out("o_x45", [1024, TK], F32)

    pb = [k.ps(f"pb{i}", [128, 512], F32) for i in range(4)]
    psb = [k.ps(f"psb{i}", [128, 128], BF16) for i in range(2)]
    zT = [k.sb(f"z{i}", [128, TK], BF16) for i in range(16)]
    sg_sb = [k.sb(f"sg{i}", [128, TK], BF16) for i in range(16)]
    wo_sb = [k.sb(f"wo{i}", [128, 1024], BF16) for i in range(16)]
    osb = [k.sb(f"os{i}", [128, TK], BF16) for i in range(8)]
    otk = k.sb("otk", [128, 1024], BF16)
    xres = [k.sb(f"xr{i}", [128, 512], F32) for i in range(2)]
    x45s = [k.sb(f"x45_{i}", [128, 512], F32) for i in range(2)]
    idb2 = k.sb("idb2", [128, 128], BF16)
    idf = k.sb("idf", [128, 128], F32)

    ld = k.ctr("ld")
    rld = [k.ctr("rld0"), k.ctr("rld1")]
    pe = k.ctr("pe")
    ve = k.ctr("ve")
    st = k.ctr("st")

    def script(ctx):
        ctx.dma("sync", idf[:], identf, ld)
        for i in range(16):
            ctx.dma("sync", zT[i][:], YT[i * 128:(i + 1) * 128, :], ld)
            ctx.dma("sync", sg_sb[i][:], SG[i * 128:(i + 1) * 128, :], ld)
            ldv = ctx.dma("sync", wo_sb[i][:], woT[i * 128:(i + 1) * 128, :],
                          ld)
        ctx.wait("vector", ld, ldv)
        ctx.copy(idb2[:], idf[:])
        for i in range(16):
            vev = ctx.tt(zT[i][:], zT[i][:], sg_sb[i][:], ALU.mult, ctr=ve)
        ctx.wait("tensor", ve, vev)
        st_hist = []
        n_out = 0
        for co in range(8):
            for tb in range(4):
                pp = pb[n_out % 2]
                par = n_out % 2
                if n_out >= 2:
                    ctx.wait("sync", st, st_hist[n_out - 2][0])
                rv = ctx.dma("sync", xres[par][:],
                             xq[co * 128:(co + 1) * 128,
                                tb * 512:(tb + 1) * 512], rld[par])
                ctx.wait("tensor", ve, ve.v)
                for ic in range(16):
                    pev = ctx.mm(pp[:, 0:512],
                                 wo_sb[ic][:, co * 128:(co + 1) * 128],
                                 zT[ic][:, tb * 512:(tb + 1) * 512],
                                 start=(ic == 0), stop=(ic == 15), ctr=pe)
                ctx.wait("vector", pe, pev)
                ctx.wait("vector", rld[par], rv)
                if n_out >= 2:
                    ctx.wait("vector", st, st_hist[n_out - 2][1])
                ctx.copy(osb[co][:, tb * 512:(tb + 1) * 512], pp[:, 0:512])
                vev = ctx.tt(x45s[par][:], pp[:, 0:512], xres[par][:],
                             ALU.add, ctr=ve)
                ctx.wait("gpsimd", ve, vev)
                sv1 = ctx.dma("gpsimd",
                              o_outT[co * 128:(co + 1) * 128,
                                     tb * 512:(tb + 1) * 512],
                              osb[co][:, tb * 512:(tb + 1) * 512], st)
                sv2 = ctx.dma("gpsimd",
                              o_x45[co * 128:(co + 1) * 128,
                                    tb * 512:(tb + 1) * 512], x45s[par][:],
                              st)
                st_hist.append((sv2, sv2))
                n_out += 1
        otk_store = None
        for tt_ in range(16):
            if otk_store is not None:
                ctx.wait("vector", st, otk_store)
            for co in range(8):
                pq = psb[co % 2]
                ctx.wait("tensor", ve, ve.v)
                pev = ctx.trans(pq[:, 0:128],
                                osb[co][:, tt_ * 128:(tt_ + 1) * 128],
                                idb2[:], ctr=pe)
                ctx.wait("vector", pe, pev)
                vev = ctx.copy(otk[:, co * 128:(co + 1) * 128], pq[:, 0:128],
                               ctr=ve)
            ctx.wait("gpsimd", ve, vev)
            sv = ctx.dma("gpsimd", o_otok[tt_ * 128:(tt_ + 1) * 128, :],
                         otk[:], st)
            otk_store = sv
        ctx.wait("gpsimd", st, st.v)

    return k.finish(script)


def build_e4():
    k = Kern("e4")
    outTz = k.inp("outTz", [1024, T + 1], BF16)
    otkz = k.inp("otkz", [T + 1, 1024], BF16)
    wwTh = k.inp("wwTh", [1024, 512], BF16)
    decay = k.inp("decay", [1, 1], F32)
    DTc = k.inp("DTc", [256, 256], F32)
    causT = k.inp("causT", [256, 256], BF16)
    prow = k.inp("prow", [1, 256], F32)
    qrow = k.inp("qrow", [1, 256], F32)
    identf = k.inp("identf", [128, 128], F32)
    o_reads = k.out("o_reads", [512, T], BF16)

    pb = [k.ps(f"pb{i}", [128, 512], F32) for i in range(7)]
    pbA, pbB, pbC, pbD, pbE, pbF, pbG = pb

    oz = [k.sb(f"oz{i}", [128, T + 1], BF16) for i in range(8)]
    ot = [k.sb(f"ot{i}", [128, 1024], BF16) for i in range(33)]
    ww = [k.sb(f"ww{i}", [128, 512], BF16) for i in range(8)]
    WT = [k.sb(f"WT{i}", [128, 512], F32) for i in range(8)]
    WB = [k.sb(f"WB{i}", [128, 512], BF16) for i in range(8)]
    dts = [k.sb(f"dts{i}", [128, 256], F32) for i in range(2)]
    cts = [k.sb(f"cts{i}", [128, 256], BF16) for i in range(2)]
    mmT = [k.sb(f"mmT{i}", [128, 256], BF16) for i in range(2)]
    smv = [k.sb(f"smv{i}", [128, 256], BF16) for i in range(2)]
    vtok = [k.sb(f"vq{i}", [128, 512], BF16) for i in range(2)]
    vg = [k.sb(f"vgq{i}", [128, 512], BF16) for i in range(2)]
    gpt = k.sb("gpt", [128, 256], F32)
    gwc = [k.sb(f"gwc{i}", [128, 1], F32) for i in range(2)]
    g256 = k.sb("g256", [128, 1], F32)
    lgs = k.sb("lgs", [1, 4], F32)
    growb = k.sb("growb", [1, 256], F32)
    lcol = k.sb("lcol", [128, 1], F32)
    rsb = k.sb("rsb", [128, 256], F32)
    rbf = [k.sb(f"rbf{i}", [128, 256], BF16) for i in range(2)]
    onesf = k.sb("onesf", [1, 128], F32)
    idf = k.sb("idf", [128, 128], F32)

    ld = k.ctr("ld")
    pe = k.ctr("pe")
    ve = k.ctr("ve")
    se = k.ctr("se")
    st = k.ctr("st")

    def script(ctx):
        for i in range(8):
            ctx.dma("sync", oz[i][:], outTz[i * 128:(i + 1) * 128, :], ld)
        for i in range(33):
            r0 = i * 128
            nr = min(128, T + 1 - r0)
            ctx.dma("sync", ot[i][0:nr, :], otkz[r0:r0 + nr, :], ld)
        for i in range(8):
            ctx.dma("sync", ww[i][:], wwTh[i * 128:(i + 1) * 128, :], ld)
        ctx.dma("sync", lgs[0:1, 0:1], decay, ld)
        ctx.dma("sync", dts[0][:], DTc[0:128, :], ld)
        ctx.dma("sync", dts[1][:], DTc[128:256, :], ld)
        ctx.dma("sync", cts[0][:], causT[0:128, :], ld)
        ctx.dma("sync", cts[1][:], causT[128:256, :], ld)
        ctx.dma("sync", growb[:], qrow, ld)
        ldv = ctx.dma("sync", idf[:], identf, ld)
        ctx.wait("scalar", ld, ldv)
        ctx.wait("vector", ld, ldv)
        ctx.memset(onesf[:], 1.0)
        for i in range(8):
            vev = ctx.memset(WT[i][:], 0.0, ctr=ve)
            vev = ctx.memset(WB[i][:], 0.0, ctr=ve)
        sev = ctx.act(lgs[0:1, 0:1], lgs[0:1, 0:1], AF.Sigmoid, ctr=se)
        sev = ctx.act(lgs[0:1, 1:2], lgs[0:1, 0:1], AF.Ln, ctr=se)
        ctx.wait("vector", se, sev)
        vev = ctx.ts(lgs[0:1, 2:3], lgs[0:1, 1:2], 256.0, ALU.mult, ctr=ve)
        ctx.wait("scalar", ve, vev)
        sev = ctx.act(lgs[0:1, 2:3], lgs[0:1, 2:3], AF.Exp, ctr=se)
        ctx.wait("tensor", se, sev)
        ctx.wait("tensor", ve, ve.v)
        pev = ctx.mm(pbD[0:128, 0:1], onesf[0:1, :], lgs[0:1, 1:2], ctr=pe)
        ctx.wait("vector", pe, pev)
        vev = ctx.copy(lcol[:], pbD[0:128, 0:1], ctr=ve)
        ctx.wait("tensor", ve, vev)
        pev = ctx.mm(pbD[0:128, 0:1], onesf[0:1, :], lgs[0:1, 2:3], ctr=pe)
        ctx.wait("vector", pe, pev)
        vev = ctx.copy(g256[:], pbD[0:128, 0:1], ctr=ve)
        vev = ctx.ts(growb[:], growb[:], lgs[0:1, 1:2], ALU.mult, ctr=ve)
        ctx.wait("scalar", ve, vev)
        sev = ctx.act(growb[:], growb[:], AF.Exp, ctr=se)
        ctx.wait("tensor", se, sev)
        for j in range(2):
            pev = ctx.trans(pbE[0:128, 0:1], growb[0:1, j * 128:(j + 1) * 128],
                            idf[0:1, 0:1], ctr=pe)
            ctx.wait("vector", pe, pev)
            vev = ctx.copy(gwc[j][:], pbE[0:128, 0:1], ctr=ve)
            ctx.wait("tensor", ve, vev)
        ctx.wait("sync", pe, pe.v)
        dv = ctx.dma("sync", growb[:], prow, ld)
        ctx.wait("vector", ld, dv)
        vev = ctx.ts(growb[:], growb[:], lgs[0:1, 1:2], ALU.mult, ctr=ve)
        ctx.wait("scalar", ve, vev)
        sev = ctx.act(growb[:], growb[:], AF.Exp, ctr=se)
        ctx.wait("tensor", se, sev)
        pev = ctx.mm(pbD[:, 0:256], onesf[0:1, :], growb[0:1, :], ctr=pe)
        ctx.wait("vector", pe, pev)
        vev = ctx.copy(gpt[:], pbD[:, 0:256], ctr=ve)
        for j in range(2):
            vev = ctx.ts(dts[j][:], dts[j][:], lcol[:], ALU.mult, ctr=ve)
        ctx.wait("scalar", ve, vev)
        for j in range(2):
            sev = ctx.act(dts[j][:], dts[j][:], AF.Exp, ctr=se)
        ctx.wait("vector", se, sev)
        for j in range(2):
            vev = ctx.tt(mmT[j][:], dts[j][:], cts[j][:], ALU.mult, ctr=ve)
        ctx.wait("tensor", ve, ve.v)

        rb_store = {0: None, 1: None}
        for c in range(16):
            c0 = c * 256
            for tt_ in range(2):
                pp = pbA if tt_ == 0 else pbB
                for ci in range(8):
                    pev = ctx.mm(pp[:, 0:512],
                                 oz[ci][:, 1 + c0 + tt_ * 128:
                                        1 + c0 + tt_ * 128 + 128],
                                 ww[ci][:], start=(ci == 0), stop=(ci == 7),
                                 ctr=pe)
            ctx.wait("vector", pe, pev)
            for tt_ in range(2):
                pp = pbA if tt_ == 0 else pbB
                ctx.copy(vtok[tt_][:], pp[:, 0:512])
                vev = ctx.ts(vg[tt_][:], pp[:, 0:512], gwc[tt_][:], ALU.mult,
                             ctr=ve)
            ctx.wait("tensor", ve, vev)
            for j in range(2):
                pp = pbC if j == 0 else pbD
                for ci in range(8):
                    pev = ctx.mm(pp[:, 0:256],
                                 oz[ci][:, c0 + j * 128:c0 + j * 128 + 128],
                                 oz[ci][:, 1 + c0:1 + c0 + 256],
                                 start=(ci == 0), stop=(ci == 7), ctr=pe)
            ctx.wait("vector", pe, pev)
            for j in range(2):
                pp = pbC if j == 0 else pbD
                vev = ctx.tt(smv[j][:], pp[:, 0:256], mmT[j][:], ALU.mult,
                             ctr=ve)
            ctx.wait("tensor", ve, vev)
            for ds in range(4):
                for j in range(2):
                    ctx.mm(pbE[:, 0:256], vtok[j][:, ds * 128:(ds + 1) * 128],
                           smv[j][:], start=(j == 0), stop=(j == 1))
                for ci in range(8):
                    pev = ctx.mm(pbF[:, 0:256],
                                 WB[ci][:, ds * 128:(ds + 1) * 128],
                                 oz[ci][:, 1 + c0:1 + c0 + 256],
                                 start=(ci == 0), stop=(ci == 7), ctr=pe)
                ctx.wait("vector", pe, pev)
                par = (c * 4 + ds) % 2
                ctx.tt(rsb[:], pbF[:, 0:256], gpt[:], ALU.mult)
                vev = ctx.tt(rsb[:], rsb[:], pbE[:, 0:256], ALU.add, ctr=ve)
                if rb_store[par] is not None:
                    ctx.wait("vector", st, rb_store[par])
                vev = ctx.copy(rbf[par][:], rsb[:], ctr=ve)
                ctx.wait("gpsimd", ve, vev)
                sv = ctx.dma("gpsimd",
                             o_reads[ds * 128:(ds + 1) * 128, c0:c0 + 256],
                             rbf[par][:], st)
                rb_store[par] = sv
                ctx.wait("tensor", ve, ve.v)
            for ci in range(8):
                pp = pbE if ci % 2 == 0 else pbF
                ctx.wait("tensor", ve, ve.v)
                for tt_ in range(2):
                    pev = ctx.mm(pp[:, 0:512],
                                 ot[2 * c + tt_][:, ci * 128:(ci + 1) * 128],
                                 vg[tt_][:], start=(tt_ == 0),
                                 stop=(tt_ == 1), ctr=pe)
                ctx.wait("vector", pe, pev)
                ctx.stt(WT[ci][:], WT[ci][:], g256[:], pp[:, 0:512],
                        ALU.mult, ALU.add)
                vev = ctx.copy(WB[ci][:], WT[ci][:], ctr=ve)
        ctx.wait("gpsimd", st, st.v)

    return k.finish(script)


def build_e5():
    k = Kern("e5")
    rT = k.inp("rT", [1024, TK], BF16)
    wrT = k.inp("wrT", [1024, 1024], BF16)
    x45 = k.inp("x45", [1024, TK], F32)
    oxT = k.out("oxT", [1024, TK], F32)

    pb = [k.ps(f"pb{i}", [128, 512], F32) for i in range(2)]
    r_sb = [k.sb(f"r{i}", [128, TK], BF16) for i in range(8)]
    wr_sb = [k.sb(f"wr{i}", [128, 1024], BF16) for i in range(8)]
    xres = [k.sb(f"xr{i}", [128, 512], F32) for i in range(2)]
    oxs = [k.sb(f"oxsb{i}", [128, 512], F32) for i in range(2)]

    ld = k.ctr("ld")
    rld = [k.ctr("rld0"), k.ctr("rld1")]
    pe = k.ctr("pe")
    ve = k.ctr("ve")
    st = k.ctr("st")

    def script(ctx):
        for i in range(8):
            ctx.dma("sync", r_sb[i][:], rT[i * 128:(i + 1) * 128, :], ld)
            ldv = ctx.dma("sync", wr_sb[i][:], wrT[i * 128:(i + 1) * 128, :],
                          ld)
        ctx.wait("tensor", ld, ldv)
        st_hist = []
        n_out = 0
        for co in range(8):
            for tb in range(4):
                pp = pb[n_out % 2]
                par = n_out % 2
                if n_out >= 2:
                    ctx.wait("sync", st, st_hist[n_out - 2])
                rv = ctx.dma("sync", xres[par][:],
                             x45[co * 128:(co + 1) * 128,
                                 tb * 512:(tb + 1) * 512], rld[par])
                ctx.wait("tensor", ve, ve.v)
                for ci in range(8):
                    pev = ctx.mm(pp[:, 0:512],
                                 wr_sb[ci][:, co * 128:(co + 1) * 128],
                                 r_sb[ci][:, tb * 512:(tb + 1) * 512],
                                 start=(ci == 0), stop=(ci == 7), ctr=pe)
                ctx.wait("vector", pe, pev)
                ctx.wait("vector", rld[par], rv)
                if n_out >= 2:
                    ctx.wait("vector", st, st_hist[n_out - 2])
                vev = ctx.stt(oxs[par][:], pp[:, 0:512], 0.5, xres[par][:],
                              ALU.mult, ALU.add, ctr=ve)
                ctx.wait("gpsimd", ve, vev)
                sv = ctx.dma("gpsimd",
                             oxT[co * 128:(co + 1) * 128,
                                 tb * 512:(tb + 1) * 512], oxs[par][:], st)
                st_hist.append(sv)
                n_out += 1
        ctx.wait("gpsimd", st, st.v)

    return k.finish(script)


def run_estage(xT_full, W, raw):
    identf = np.eye(128, dtype=np.float32)
    slabs = []
    for pc in range(65):
        blk = W["w_inT"][:, pc * 128:min((pc + 1) * 128, 8208)]
        pw = blk.shape[1]
        sl = blk.reshape(8, 128, pw).transpose(1, 0, 2).reshape(128, 8 * pw)
        if pw < 128:
            sl = np.concatenate(
                [sl.reshape(128, 8, pw),
                 np.zeros((128, 8, 128 - pw), dtype=sl.dtype)],
                axis=2).reshape(128, 1024)
        slabs.append(sl)
    winS = c_(np.concatenate(slabs, axis=0))
    nwc = c_(raw["L1_attn_a_norm_w"].astype(np.float32).reshape(1024, 1))
    convw = c_(raw["L1_attn_a_conv_w"].astype(np.float32))
    convb = c_(raw["L1_attn_a_conv_b"].astype(np.float32).reshape(2048, 1))
    alog = c_(raw["L1_attn_a_A_log"].astype(np.float32).reshape(16, 1))
    dtb = c_(raw["L1_attn_a_dt_bias"].astype(np.float32).reshape(16, 1))
    in_maps = []
    for c in range(NCORE):
        b, h = c // 2, c % 2
        t0 = h * TK
        xh = np.zeros((1024, 2051), dtype=np.float32)
        xh[:, 3:] = xT_full[b][:, t0:t0 + TK]
        if t0 >= 3:
            xh[:, 0:3] = xT_full[b][:, t0 - 3:t0]
        in_maps.append(dict(xh=c_(xh), winS=winS, nwc=nwc, convw=convw,
                            convb=convb, alog=alog, dtb=dtb))
    res1 = runk("e1", build_e1, in_maps)
    va = np.empty((B, 2048, T), dtype=BF)
    sg = np.empty((B, 2048, T), dtype=BF)
    bc = np.empty((B, 4096, T), dtype=BF)
    ldv = np.empty((B, 16, T), dtype=np.float32)
    xa = np.empty((B, 1024, T), dtype=np.float32)
    for c in range(NCORE):
        b, h = c // 2, c % 2
        sl = slice(h * TK, (h + 1) * TK)
        va[b][:, sl] = res1[c]["o_va"]
        sg[b][:, sl] = res1[c]["o_sg"]
        bc[b][:, sl] = res1[c]["o_bc"]
        ldv[b][:, sl] = res1[c]["o_ld"]
        xa[b][:, sl] = res1[c]["o_xa"]

    tril = (np.arange(256)[None, :] >= np.arange(256)[:, None]).astype(BF)
    in_maps = []
    for c in range(NCORE):
        b, hh = c // 2, c % 2
        hsl = slice(hh * 8 * 128, (hh + 1) * 8 * 128)
        in_maps.append(dict(
            vA=c_(va[b][hsl]), BT=c_(bc[b][hsl]),
            CT=c_(bc[b][2048 + hh * 8 * 128: 2048 + (hh + 1) * 8 * 128]),
            ldr=c_(ldv[b][hh * 8:(hh + 1) * 8]),
            dsk=c_(raw["L1_attn_a_D_skip"].astype(np.float32)
                   .reshape(16, 128)[hh * 8:(hh + 1) * 8]),
            identf=identf, trilT=c_(tril)))
    res2 = runk("e2", build_e2, in_maps)
    yT = np.empty((B, 2048, T), dtype=BF)
    for c in range(NCORE):
        b, hh = c // 2, c % 2
        yT[b][hh * 8 * 128:(hh + 1) * 8 * 128] = res2[c]["o_yT"]

    in_maps = []
    for c in range(NCORE):
        b, h = c // 2, c % 2
        sl = slice(h * TK, (h + 1) * TK)
        in_maps.append(dict(YT=c_(yT[b][:, sl]), SG=c_(sg[b][:, sl]),
                            woT=W["w_outT"], xq=c_(xa[b][:, sl]),
                            identf=identf))
    res3 = runk("e3", build_e3, in_maps)
    outT = np.empty((B, 1024, T), dtype=BF)
    otok = np.empty((B, T, 1024), dtype=BF)
    x45 = np.empty((B, 1024, T), dtype=np.float32)
    for c in range(NCORE):
        b, h = c // 2, c % 2
        sl = slice(h * TK, (h + 1) * TK)
        outT[b][:, sl] = res3[c]["o_outT"]
        otok[b][sl] = res3[c]["o_otok"]
        x45[b][:, sl] = res3[c]["o_x45"]

    i_ = np.arange(256)
    DTc = c_(np.maximum(i_[None, :] - 1 - i_[:, None], 0).astype(np.float32))
    causT = c_((i_[None, :] > i_[:, None]).astype(BF))
    prow = c_(i_.astype(np.float32).reshape(1, 256))
    qrow = c_((255.0 - i_).astype(np.float32).reshape(1, 256))
    decay = c_(np.asarray(raw["L1_attn_a_decay"], np.float32).reshape(1, 1))
    in_maps = []
    for c in range(NCORE):
        b, dh = c // 2, c % 2
        outTz = np.zeros((1024, T + 1), dtype=BF)
        outTz[:, 1:] = outT[b]
        otkz = np.zeros((T + 1, 1024), dtype=BF)
        otkz[1:] = otok[b]
        in_maps.append(dict(
            outTz=c_(outTz), otkz=c_(otkz),
            wwTh=c_(W["w_writeT"][:, dh * 512:(dh + 1) * 512]),
            decay=decay, DTc=DTc, causT=causT, prow=prow, qrow=qrow,
            identf=identf))
    res4 = runk("e4", build_e4, in_maps)
    readsT = np.empty((B, 1024, T), dtype=BF)
    for c in range(NCORE):
        b, dh = c // 2, c % 2
        readsT[b][dh * 512:(dh + 1) * 512] = res4[c]["o_reads"]

    in_maps = []
    for c in range(NCORE):
        b, h = c // 2, c % 2
        sl = slice(h * TK, (h + 1) * TK)
        in_maps.append(dict(rT=c_(readsT[b][:, sl]), wrT=W["w_readT"],
                            x45=c_(x45[b][:, sl])))
    res5 = runk("e5", build_e5, in_maps)
    out = np.empty_like(xT_full)
    for c in range(NCORE):
        b, h = c // 2, c % 2
        out[b][:, h * TK:(h + 1) * TK] = res5[c]["oxT"]
    return out



def kernel(**inputs):
    inputs = {k: np.asarray(v) for k, v in inputs.items()}
    W = run_w0(inputs)
    x = np.asarray(inputs["L0_attn_a_x"], np.float32)
    xT = toT(x)
    xT = run_attn_a(xT, W)
    xT = run_gffn(xT, W, inputs)
    xT = run_attn_b0(xT, W)
    xT = run_gffn(xT, W, inputs)
    xT = run_estage(xT, W, inputs)
    xT = run_moe(xT, W, inputs)
    xT = run_dattn(xT, W, inputs)
    xT = run_moe(xT, W, inputs)
    return np.ascontiguousarray(xT.transpose(0, 2, 1))
